# version 17 (final)
# baseline (speedup 1.0000x reference)
from contextlib import ExitStack

import numpy as np
import ml_dtypes
import concourse.bass as bass
import concourse.mybir as mybir
from concourse.bass_utils import run_bass_kernel_spmd

F32 = mybir.dt.float32
BF16 = mybir.dt.bfloat16
ALU = mybir.AluOpType
ACT = mybir.ActivationFunctionType
AXX = mybir.AxisListType
BF = ml_dtypes.bfloat16

D = 1024
J = 8
TFULL = 8192
TOWN = 4096
NTF = 64
NTO = 32
NST = 8
HP = 8
INNER = 4096
NI = 32
NTG = 16
TGW = 256
NS_ITERS = 18

C23 = float(np.float32(1.5 * 2 ** 23))
C52 = float(np.float32(1.5 * 2 ** 52))
EPS_LN = 1e-5


class Prog:
    ENGS = ("sync", "vector", "scalar", "tensor", "gpsimd")
    POOL = {"sa": 8, "sg": 8, "sq": 4, "sw": 8, "sx": 4}

    def __init__(self):
        self.q = {e: [] for e in self.ENGS}
        self.c = {"v": 0, "s": 0, "t": 0, "g": 0}
        for qn, n in self.POOL.items():
            for i in range(n):
                self.c[f"{qn}{i}"] = 0
        self.rr = {qn: 0 for qn in self.POOL}
        self.sem = {}
        self.esem = {"vector": "v", "scalar": "s", "tensor": "t", "gpsimd": "g"}
        self.lastw = {}

    def emit(self, eng, fn):
        self.q[eng].append(fn)

    def wait(self, eng, sem, val):
        if val is None or val <= 0:
            return
        key = (eng, sem)
        if self.lastw.get(key, -1) >= val:
            return
        self.lastw[key] = val
        s = self.sem[sem]
        self.emit(eng, lambda e: e.wait_ge(s, val))

    def dwait(self, eng, tok):
        if tok is None or tok == 0:
            return
        self.qbarrier(eng, tok[0][:2])

    def dwait_tok(self, eng, tok):
        if tok is None or tok == 0:
            return
        self.wait(eng, tok[0], tok[1])

    def dma(self, eng, queue, out, in_, **kw):
        i = self.rr[queue]
        self.rr[queue] = (i + 1) % self.POOL[queue]
        sn = f"{queue}{i}"
        self.wait(eng, sn, self.c[sn])
        s = self.sem[sn]
        self.emit(eng, lambda e: e.dma_start(out=out, in_=in_, **kw).then_inc(s, 16))
        self.c[sn] += 16
        return (sn, self.c[sn])

    def dmat(self, out, in_):
        i = self.rr["sq"]
        self.rr["sq"] = (i + 1) % self.POOL["sq"]
        sn = f"sq{i}"
        self.wait("sync", sn, self.c[sn])
        s = self.sem[sn]
        self.emit("sync", lambda e: e.dma_start_transpose(out, in_).then_inc(s, 16))
        self.c[sn] += 16
        return (sn, self.c[sn])

    def qbarrier(self, eng, queue):
        for i in range(self.POOL[queue]):
            self.wait(eng, f"{queue}{i}", self.c[f"{queue}{i}"])

    def op(self, eng, fn, inc=True, selfwait=True):
        sk = self.esem[eng]
        if selfwait and eng != "tensor":
            self.wait(eng, sk, self.c[sk])
        if inc:
            s = self.sem[sk]
            self.emit(eng, lambda e: fn(e).then_inc(s, 1))
            self.c[sk] += 1
            return self.c[sk]
        self.emit(eng, lambda e: fn(e))
        return None

    def vgap(self):
        pass

    def run(self, eng, e):
        for fn in self.q[eng]:
            fn(e)


def build_nc():
    nc = bass.Bass("TRN2", target_bir_lowering=False, debug=False)

    x_h = nc.dram_tensor("x", [TFULL, D], F32, kind="ExternalInput").ap()
    wq_h = nc.dram_tensor("wq", [D, D], F32, kind="ExternalInput").ap()
    wk_h = nc.dram_tensor("wk", [D, D], F32, kind="ExternalInput").ap()
    wv_h = nc.dram_tensor("wv", [D, D], F32, kind="ExternalInput").ap()
    wo_h = nc.dram_tensor("wo", [D, D], F32, kind="ExternalInput").ap()
    bo_h = nc.dram_tensor("bo", [1, D], F32, kind="ExternalInput").ap()
    alpha_h = nc.dram_tensor("alpha", [1, 1], F32, kind="ExternalInput").ap()
    temp_h = nc.dram_tensor("temp", [1, 1], F32, kind="ExternalInput").ap()
    gamma_h = nc.dram_tensor("gamma", [1, D], F32, kind="ExternalInput").ap()
    beta_h = nc.dram_tensor("beta", [1, D], F32, kind="ExternalInput").ap()
    wg_h = nc.dram_tensor("wg", [INNER, D], F32, kind="ExternalInput").ap()
    wu_h = nc.dram_tensor("wu", [INNER, D], F32, kind="ExternalInput").ap()
    wd_h = nc.dram_tensor("wd", [D, INNER], F32, kind="ExternalInput").ap()
    cvec_h = nc.dram_tensor("cvec", [1, D], F32, kind="ExternalInput").ap()
    wmod_h = nc.dram_tensor("wmod", [D, 6 * D], F32, kind="ExternalInput").ap()
    eye_h = nc.dram_tensor("eye", [128, 128], BF16, kind="ExternalInput").ap()
    eyf_h = nc.dram_tensor("eyf", [128, 64], F32, kind="ExternalInput").ap()
    hsel_h = nc.dram_tensor("hsel", [128, 2], BF16, kind="ExternalInput").ap()
    g2sel_h = nc.dram_tensor("g2sel", [2, 128], BF16, kind="ExternalInput").ap()

    out_h = nc.dram_tensor("out", [TOWN, D], F32, kind="ExternalOutput").ap()

    x2_hbm = nc.dram_tensor("x2s", [TOWN, D], F32).ap()
    h2q_hbm = nc.dram_tensor("h2qs", [NST, 128, NI, 128], BF16).ap()
    yq_hbm = nc.dram_tensor("yqs", [NTG, 128, NI, TGW], BF16).ap()
    wdt_hbm = nc.dram_tensor("wdts", [NI, 128, D], BF16).ap()
    wgq_hbm = nc.dram_tensor("wgq", [2, NI, 128, D], BF16).ap()

    P = Prog()

    with ExitStack() as stack:
        def sb(name, shape, dt):
            return stack.enter_context(nc.sbuf_tensor(name, shape, dt))

        HF = sb("HF", [128, J, TOWN], BF16)
        WAT = sb("WAT", [128, J, 4, J, 128], BF16)
        YTG = sb("YTG", [128, NI, TGW], BF16)
        U = sb("U", [128, 24576], BF16)
        SCR = sb("SCR", [128, D], F32)
        STAT = sb("STAT", [128, 16], F32)
        EYE = sb("EYE", [128, 128], BF16)
        EYF = sb("EYF", [128, 64], F32)
        ONE1 = sb("ONE1", [1, 128], BF16)
        N1R = sb("N1R", [1, 512], BF16)
        ONEC = sb("ONEC", [128, 1], BF16)
        CPX = sb("CPX", [128, 1], F32)
        DDT = sb("DDT", [128, 4], F32)
        CA1 = sb("CA1", [128, J], F32)
        SAF = sb("SAF", [128, J], F32)
        BOF = sb("BOF", [128, J], F32)
        GBF = sb("GBF", [128, J], F32)
        CB1BC = sb("CB1BC", [128, D], F32)
        SBBC = sb("SBBC", [128, D], F32)
        ALP = sb("ALP", [128, 1], F32)
        ITP = sb("ITP", [128, 1], F32)
        CSF = sb("CSF", [128, J], BF16)
        MAPB = sb("MAPB", [128, HP, 64], BF16)
        BST = sb("BST", [128, 8], BF16)
        NSS = sb("NSS", [128, 32], F32)
        HSEL = sb("HSEL", [128, 2], BF16)
        G2SEL = sb("G2SEL", [2, 128], BF16)
        SYR = sb("SYR", [128, NTG * 2], F32)
        SYCOL = sb("SYCOL", [128, 8], F32)
        SGP = sb("SGP", [128, 4], F32)
        SGR = sb("SGR", [128, 4], F32)
        G1F0 = sb("G1F0", [128, 16], F32)
        PS = stack.enter_context(nc.psum_tensor("PS", [128, 8, 512], F32))
        for k in list(P.c.keys()):
            P.sem[k] = stack.enter_context(nc.semaphore(k))
        block = stack.enter_context(nc.Block())

        Uf = U.bitcast(F32)
        XNB = U[:, 0:2048].rearrange("p (b c) -> p b c", b=2)
        HTMP = U[:, 2048:4096].rearrange("p (b j c) -> p b j c", b=2, j=J)
        KTM = U[:, 4096:5120]
        VTM = U[:, 5120:6144]
        XT = Uf[:, 3072:5120].rearrange("p (b c) -> p b c", b=2)
        XT3 = [Uf[:, 3072:4096], Uf[:, 4096:5120], Uf[:, 11264:12288]]
        KKS = Uf[:, 0:1024].rearrange("p (h c) -> p h c", h=HP)
        ABLK = U[:, 2048:3072].rearrange("p (h c) -> p h c", h=HP)
        XNS = U[:, 3072:3584].rearrange("p (h c) -> p h c", h=HP)
        ZNS = U[:, 3584:4096].rearrange("p (h c) -> p h c", h=HP)
        KTVB = U[:, 4096:4608].rearrange("p (h c) -> p h c", h=HP)
        SMB = U[:, 4608:5120].rearrange("p (h c) -> p h c", h=HP)
        X3T = Uf[:, 0:2048].rearrange("p (b c) -> p b c", b=2)
        ST1 = Uf[0:1, 5120:6656]
        H2T1 = U[:, 13312:14336]
        X2T = Uf[:, 7168:9216].rearrange("p (b c) -> p b c", b=2)
        H2FM = U[:, 18432:22528].rearrange("p (i c) -> p i c", i=NI)
        SQB = U[:, 4096:5120].rearrange("p (b c) -> p b c", b=2)
        H2QT = U[:, 0:2048].rearrange("p (t k c) -> p t k c", t=2, k=J)
        H2QT2 = U[:, 2048:4096].rearrange("p (t k c) -> p t k c", t=2, k=J)
        WFNAT = Uf[:, 2048:4096].rearrange("p (b c) -> p b c", b=2)
        WFQ = U[:, 8192:9216]
        GS = U[:, 9216:9216 + TGW]
        UPS = U[:, 9472:9472 + TGW]
        MXT = U[:, 9728:9728 + TGW]
        MNT = U[:, 10752:10752 + TGW]
        SYBS = U[:, 9984:9984 + TGW]
        ZQ = U[:, 10240:10752].rearrange("p (b c) -> p b c", b=2)
        GSB = U[:, 23568:23824]
        UPSB = U[:, 23824:24080]
        YQB = U[:, 10496:11008].rearrange("p (b c) -> p b c", b=2)
        SYKT = U[:, 11008:11016]
        WDQ = U[:, 11280:15376]
        WDNAT = Uf[:, 7688:9736]
        WDDEST = U[:, 19472:23568].rearrange("p (i c) -> p i c", i=NI)
        WDTS = U[:, 4096:6144].rearrange("p (b c) -> p b c", b=2)
        WDTS8 = U[:, 4096:12288].rearrange("p (b c) -> p b c", b=8)
        YQI = U[:, 6144:6656].rearrange("p (b c) -> p b c", b=2)
        ABSS = Uf[:, 9216:9312]
        PREPROW = U[:, 8192:9216][0:1, 0:1024]
        EYFB = U[:, 5120:5184]
        ROWS = STAT[0:2, 0:8]
        ROWSB = BST[0:2, 0:8]
        ST1B = U[:, 5120:6144][0:1, 0:1024]
        SYROW = U[:, 11016:11272][0:1, 0:256]
        G1R = U[:, 22528:23552][0:1, 0:1024]
        G0R = U[:, 23552:24576][0:1, 0:1024]

        QF = WAT[:, :, 1, 0:4, :].rearrange("p k r c -> p k (r c)")
        OF = WAT[:, :, 1, 4:8, :].rearrange("p k r c -> p k (r c)")
        ZBB = WAT[:, :, 2, 0:4, :].rearrange("p k r c -> p k (r c)")
        AOTM = WAT[:, :, 2, 4:8, :]
        AOC = U[:, 6144:10240]
        EPSA = DDT[:, 0:1]
        HFf = HF.rearrange("p j c -> p (j c)").bitcast(F32)
        MODR = HFf[0:1, 0:11 * D]
        CROW = HFf[0:1, 11 * D:12 * D]
        YTGflat = YTG.rearrange("p a b -> p (a b)")
        WNATv = YTGflat.rearrange("p (r f) -> p r f", r=J)
        WMODSv = YTGflat[:, 0:4096].rearrange("p (k c) -> p k c", k=J)

        def bank(i, w=512):
            return PS[:, i, 0:w]

        def bankb(i):
            return PS[:, i, :].bitcast(BF16)

        def st1(k):
            return ST1[0:1, (k % 3) * 512:(k % 3) * 512 + 512]

        def st1b(k):
            return ST1B[0:1, k * 512:(k + 1) * 512]

        ld_eye = P.dma("gpsimd", "sg", EYE[:], eye_h)
        ld_eyf = P.dma("gpsimd", "sg", EYF[:], eyf_h)
        ld_c = P.dma("gpsimd", "sg", CROW, cvec_h)
        P.dma("gpsimd", "sg", MODR[0:1, 6 * D:7 * D], bo_h)
        P.dma("gpsimd", "sg", MODR[0:1, 7 * D:7 * D + 1], alpha_h)
        P.dma("gpsimd", "sg", MODR[0:1, 7 * D + 4:7 * D + 5], temp_h)
        P.dma("gpsimd", "sg", MODR[0:1, 8 * D:9 * D], gamma_h)
        ld_small = P.dma("gpsimd", "sg", MODR[0:1, 9 * D:10 * D], beta_h)
        P.dma("gpsimd", "sg", HSEL[:], hsel_h)
        ld_sel = P.dma("gpsimd", "sg", G2SEL[:], g2sel_h)

        P.op("vector", lambda e: e.memset(DDT[:, 0:1], 2.0 ** -40))
        P.vgap()
        P.op("vector", lambda e: e.tensor_single_scalar(DDT[:, 1:2], DDT[:, 0:1], 1.0, ALU.add))
        P.vgap()
        P.op("vector", lambda e: e.tensor_single_scalar(DDT[:, 2:3], DDT[:, 1:2], 1.0, ALU.subtract))
        P.vgap()
        P.op("vector", lambda e: e.tensor_single_scalar(DDT[:, 3:4], DDT[:, 2:3], 2.0 ** 40, ALU.mult))
        P.vgap()
        P.op("vector", lambda e: e.tensor_scalar(
            CPX[:], DDT[:, 3:4], C52 - C23, C23, ALU.mult, ALU.add))
        P.vgap()
        P.op("vector", lambda e: e.memset(DDT[:, 0:1], EPS_LN))
        P.op("vector", lambda e: e.memset(ONE1[:], 1.0))
        P.op("vector", lambda e: e.memset(N1R[:], -1.0))
        P.op("vector", lambda e: e.memset(ONEC[:], 1.0))

        sq_wat = 0
        for wi, wsrc in enumerate((wq_h, wk_h, wv_h, wo_h)):
            if wi > 0:
                P.dwait("gpsimd", sq_wat)
            g_w = P.dma("gpsimd", "sg", WNATv[:, :, :],
                        wsrc.rearrange("(r p) f -> p r f", p=128))
            P.dwait("sync", g_w)
            for r in range(J):
                sq_wat = P.dmat(WAT[:, :, wi, r, :], WNATv[:, r, :])

        P.dwait("scalar", ld_c)
        s_csil = P.op("scalar", lambda e: e.activation(CROW, CROW, ACT.Silu))
        P.wait("tensor", "s", s_csil)
        P.dwait("tensor", ld_eyf)
        for j in range(J):
            P.op("tensor", lambda e, j=j: e.matmul(
                bank(7)[:, j:j + 1], CROW[0:1, j * 128:(j + 1) * 128],
                EYF[0:1, 0:1], start=(j == 0), stop=(j == J - 1)), inc=(j == J - 1))
        t_ctr = P.c["t"]
        P.wait("vector", "t", t_ctr)
        v_csf = P.op("vector", lambda e: e.tensor_copy(CSF[:], bank(7)[:, 0:J]))

        v_modev = v_csf
        P.dwait("gpsimd", sq_wat)
        for n in range(12):
            g_wm = P.dma("gpsimd", "sg", WMODSv,
                         wmod_h.rearrange("(k p) m -> p k m", p=128)[:, :, n * 512:(n + 1) * 512])
            P.dwait("tensor", g_wm)
            P.wait("tensor", "v", v_modev)
            for k in range(J):
                P.op("tensor", lambda e, k=k: e.matmul(
                    bank(6)[0:1, :], CSF[:, k:k + 1], WMODSv[:, k, :],
                    start=(k == 0), stop=(k == J - 1)), inc=(k == J - 1))
            t_mm = P.c["t"]
            P.wait("vector", "t", t_mm)
            v_modev = P.op("vector", lambda e, n=n: e.tensor_copy(
                MODR[0:1, n * 512:(n + 1) * 512], bank(6)[0:1, :]))
            P.wait("gpsimd", "t", t_mm)

        P.wait("scalar", "v", v_modev)
        for base in (2, 5):
            row = MODR[0:1, base * D:(base + 1) * D]
            sgn = MODR[0:1, 10 * D:11 * D]
            P.wait("scalar", "v", P.c["v"])
            P.op("scalar", lambda e, row=row, sgn=sgn: e.activation(sgn, row, ACT.Sign))
            s_sg = P.op("scalar", lambda e, row=row: e.activation(PREPROW, row, ACT.Abs))
            P.wait("vector", "s", s_sg)
            P.op("vector", lambda e: e.tensor_single_scalar(PREPROW, PREPROW, 0.25, ALU.max))
            P.op("vector", lambda e, row=row, sgn=sgn: e.tensor_mul(row, PREPROW, sgn))
        P.op("vector", lambda e: e.tensor_mul(G1R[:], MODR[0:1, 2 * D:3 * D], MODR[0:1, 8 * D:9 * D]))
        v_rows = P.op("vector", lambda e: e.tensor_mul(G0R[:], MODR[0:1, 2 * D:3 * D], MODR[0:1, 9 * D:10 * D]))

        last_fme = v_rows
        for (ri, dstt, addv) in ((1, CA1, 1.0), (0, SAF, 0.0), (6, BOF, 0.0), (5, GBF, 0.0)):
            P.wait("tensor", "v", last_fme)
            for j in range(J):
                P.op("tensor", lambda e, ri=ri, j=j: e.matmul(
                    bank(7)[:, j:j + 1],
                    MODR[0:1, ri * D + j * 128: ri * D + (j + 1) * 128],
                    EYF[0:1, 0:1], start=(j == 0), stop=(j == J - 1)), inc=(j == J - 1))
            t_tr = P.c["t"]
            P.wait("vector", "t", t_tr)
            last_fme = P.op("vector", lambda e, dstt=dstt, addv=addv: e.tensor_scalar(
                dstt[:], bank(7)[:, 0:J], addv, None, ALU.add))

        P.dwait("tensor", ld_eye)
        P.wait("tensor", "v", last_fme)
        for j in range(J):
            P.op("tensor", lambda e, j=j: e.matmul(
                bank(7)[:, j:j + 1], G1R[0:1, j * 128:(j + 1) * 128], EYE[0:1, 0:1],
                start=(j == 0), stop=False), inc=False)
            P.op("tensor", lambda e, j=j: e.matmul(
                bank(7)[:, 8 + j:9 + j], G0R[0:1, j * 128:(j + 1) * 128], EYE[0:1, 0:1],
                start=False, stop=(j == J - 1)), inc=(j == J - 1))
        t_g1f = P.c["t"]
        P.wait("vector", "t", t_g1f)
        P.op("vector", lambda e: e.tensor_copy(G1F0[:], bank(7)[:, 0:16]))
        G1F = G1F0[:, 0:8]
        G0F = G1F0[:, 8:16]
        v_cbx = last_fme
        for dst, ri, addv in ((CB1BC, 4, 1.0), (SBBC, 3, 0.0)):
            for half in range(2):
                P.op("vector", lambda e, ri=ri, addv=addv, half=half: e.tensor_scalar(
                    PREPROW[0:1, 0:512], MODR[0:1, ri * D + half * 512:ri * D + (half + 1) * 512],
                    addv, None, ALU.add))
                v_cbx = P.c["v"]
                P.wait("tensor", "v", v_cbx)
                P.op("tensor", lambda e: e.matmul(
                    bank(5), ONE1[0:1, 0:128], PREPROW[0:1, 0:512], start=True, stop=True))
                t_x = P.c["t"]
                P.wait("vector", "t", t_x)
                v_cbx = P.op("vector", lambda e, dst=dst, half=half: e.tensor_copy(
                    dst[:, half * 512:(half + 1) * 512], bank(5)))

        P.dwait("vector", ld_small)
        P.op("vector", lambda e: e.reciprocal(MODR[0:1, 7 * D + 8:7 * D + 9],
                                              MODR[0:1, 7 * D + 4:7 * D + 5]))
        P.vgap()
        P.op("vector", lambda e: e.tensor_copy(PREPROW[0:1, 0:1], MODR[0:1, 7 * D:7 * D + 1]))
        P.op("vector", lambda e: e.tensor_copy(PREPROW[0:1, 1:2], MODR[0:1, 7 * D + 8:7 * D + 9]))
        v_al = P.c["v"]
        P.wait("tensor", "v", v_al)
        P.op("tensor", lambda e: e.matmul(bank(5)[:, 0:2], ONE1[0:1, 0:128],
                                          PREPROW[0:1, 0:2], start=True, stop=True))
        t_al = P.c["t"]
        P.wait("vector", "t", t_al)
        P.op("vector", lambda e: e.tensor_copy(ALP[:], bank(5)[:, 0:1]))
        v_itp = P.op("vector", lambda e: e.tensor_copy(ITP[:], bank(5)[:, 1:2]))

        P1B = [Uf[:, k * 1024:(k + 1) * 1024] for k in range(8)]

        def p1_src(u):
            if u < 64:
                return ((wg_h, wu_h)[u % 2])[(u // 2) * 128:(u // 2 + 1) * 128, :]
            mm = u - 64
            r, qq = mm // 4, mm % 4
            return wd_h[r * 128:(r + 1) * 128, qq * 1024:(qq + 1) * 1024]

        def p1_col(u):
            if u < 64:
                return (u % 2) * 32 + u // 2
            return 64 + (u - 64)

        P1RING = ["sg", "sw"]
        p1_tok = {}
        s_abs_hist = {}
        P.wait("gpsimd", "t", P.c["t"])
        P.wait("gpsimd", "v", P.c["v"])
        for u0 in range(8):
            p1_tok[u0] = P.dma("gpsimd", P1RING[u0 % 2], P1B[u0][:], p1_src(u0))
        for u in range(96):
            P.dwait_tok("vector", p1_tok[u])
            na = p1_col(u)
            s_abs = P.op("vector", lambda e, u=u, na=na: e.reduce_sum(
                ABSS[:, na:na + 1], P1B[u % 8][:], AXX.X, apply_absolute_value=True))
            s_abs_hist[u] = s_abs
            if u + 8 < 96:
                P.wait("gpsimd", "v", s_abs)
                P.dwait_tok("gpsimd", p1_tok[u])
                p1_tok[u + 8] = P.dma("gpsimd", P1RING[(u + 8) % 2], P1B[u % 8][:],
                                      p1_src(u + 8))
        s_abs = s_abs_hist[95]
        for mi, (a, b) in enumerate(((0, 32), (32, 64), (64, 96))):
            P.op("vector", lambda e, a=a, b=b, mi=mi: e.reduce_sum(
                STAT[:, mi:mi + 1], ABSS[:, a:b], AXX.X))
        P.vgap()
        v_ab = P.op("vector", lambda e: e.tensor_copy(BST[:, 0:3], STAT[:, 0:3]))
        P.wait("tensor", "v", v_ab)
        P.op("tensor", lambda e: e.matmul(bank(5)[0:1, 0:3], ONEC[:], BST[:, 0:3],
                                          start=True, stop=True))
        t_ab = P.c["t"]
        P.wait("vector", "t", t_ab)
        P.op("vector", lambda e: e.tensor_scalar(
            st1(0)[0:1, 0:3], bank(5)[0:1, 0:3], 1.0 / (INNER * D), 1e-5,
            ALU.mult, ALU.max))
        P.vgap()
        P.op("vector", lambda e: e.reciprocal(st1(1)[0:1, 0:3], st1(0)[0:1, 0:3]))
        P.vgap()
        P.op("vector", lambda e: e.tensor_copy(ROWSB[0:1, 0:3], st1(1)[0:1, 0:3]))
        v_sc = P.op("vector", lambda e: e.tensor_copy(ROWSB[0:1, 3:6], st1(0)[0:1, 0:3]))
        P.wait("tensor", "v", v_sc)
        P.op("tensor", lambda e: e.matmul(bank(5)[:, 0:6], ONE1[0:1, 0:128],
                                          ROWSB[0:1, 0:6], start=True, stop=True))
        t_sc = P.c["t"]
        P.wait("vector", "t", t_sc)
        P.op("vector", lambda e: e.tensor_copy(SGP[:, 0:3], bank(5)[:, 0:3]))
        v_sgp = P.op("vector", lambda e: e.tensor_copy(SGR[:, 0:3], bank(5)[:, 3:6]))

        def kk_bank(hp):
            b = 6 + hp // 4
            off = (hp % 4) * 128
            return PS[:, b, off:off + 128]

        def stat_a(pbuf, a, b):
            return STAT[:, pbuf * 8 + a:pbuf * 8 + b]

        sq_hist = {}
        v_kv_prev = None
        t_kk_hist = {}
        v_xn_hist = {}

        v_prep_end = P.c["v"]
        s_prep_end = P.c["s"]
        t_prep_end = P.c["t"]

        sq_hist = {}
        v_mod_hist = {}
        ev_v = {}
        ev_s = {}
        t_kk_hist = {}
        v_xn_hist = {}

        def a_prep(ft):
            pbuf = ft % 2
            xb3 = XT3[ft % 3]
            own = ft < NTO
            if ft >= 2:
                P.wait("scalar", "v", v_xn_hist.get(ft - 2, 0))
            P.dwait_tok("scalar", ax_tok[ft])
            P.op("scalar", lambda e, xb3=xb3, pbuf=pbuf: e.activation(
                SCR[:], xb3[:], ACT.Square, accum_out=stat_a(pbuf, 0, 1)))
            s_st = P.op("scalar", lambda e, xb3=xb3, pbuf=pbuf: e.activation(
                SCR[:], xb3[:], ACT.Identity, accum_out=stat_a(pbuf, 1, 2)))
            P.wait("vector", "s", s_st)
            P.op("vector", lambda e, pbuf=pbuf: e.tensor_single_scalar(
                stat_a(pbuf, 2, 3), stat_a(pbuf, 1, 2), 1.0 / D, ALU.mult))
            P.op("vector", lambda e, pbuf=pbuf: e.tensor_single_scalar(
                stat_a(pbuf, 3, 4), stat_a(pbuf, 0, 1), 1.0 / D, ALU.mult))
            P.op("vector", lambda e, pbuf=pbuf: e.tensor_mul(
                stat_a(pbuf, 4, 5), stat_a(pbuf, 2, 3), stat_a(pbuf, 2, 3)))
            v_var = P.op("vector", lambda e, pbuf=pbuf: e.tensor_sub(
                stat_a(pbuf, 5, 6), stat_a(pbuf, 3, 4), stat_a(pbuf, 4, 5)))
            P.wait("scalar", "v", v_var)
            s_sd = P.op("scalar", lambda e, pbuf=pbuf: e.activation(
                stat_a(pbuf, 6, 7), stat_a(pbuf, 5, 6), ACT.Sqrt, bias=EPSA))
            P.wait("vector", "s", s_sd)
            P.op("vector", lambda e, pbuf=pbuf: e.reciprocal(
                stat_a(pbuf, 7, 8), stat_a(pbuf, 6, 7)))
            if ft >= 2:
                P.dwait("vector", sq_hist.get(ft - 2, 0))
            v_xn = P.op("vector", lambda e, xb3=xb3, pbuf=pbuf: e.tensor_scalar(
                XNB[:, pbuf, :], xb3[:], stat_a(pbuf, 2, 3), stat_a(pbuf, 7, 8),
                ALU.subtract, ALU.mult))
            v_xn_hist[ft] = v_xn
            P.wait("sync", "v", v_xn)
            if own:
                dst3 = HF.rearrange("p j (t c) -> p j t c", c=128)[:, :, ft, :]
            else:
                if ft >= NTO + 2:
                    P.wait("sync", "t", t_kv_hist[ft - 2])
                dst3 = HTMP[:, pbuf, :, :]
            sq_t = P.dmat(dst3, XNB[:, pbuf, :])
            sq_hist[ft] = sq_t
            P.dwait("scalar", sq_t)
            for j in range(J):
                P.op("scalar", lambda e, dst3=dst3, j=j: e.activation(
                    dst3[:, j, :], dst3[:, j, :], ACT.Identity,
                    bias=SAF[:, j:j + 1], scale=CA1[:, j:j + 1]),
                    inc=(j == J - 1))
            v_mod_hist[ft] = P.c["s"]
            return dst3

        WQNA = [Uf[:, 5120 + k * 1024:5120 + (k + 1) * 1024] for k in range(4)]
        WFQA = [U[:, 18432:19456], U[:, 19456:20480]]
        WGSTG = [U[:, 20480:21504], U[:, 21504:22528]]
        wqa_tok = {}
        sqa_tok = {}
        sta_tok = {}
        vfin_hist = {}

        def wqa_src(m):
            if m < 64:
                return ((wg_h, wu_h)[m % 2])[(m // 2) * 128:(m // 2 + 1) * 128, :]
            mm = m - 64
            r, qq = mm // 4, mm % 4
            return wd_h[r * 128:(r + 1) * 128, qq * 1024:(qq + 1) * 1024]

        def emit_wq_unit(m):
            buf = WQNA[m % 4]
            P.dwait_tok("vector", wqa_tok[m])
            if m < 64:
                sp = SGP[:, m % 2:m % 2 + 1]
                sr = SGR[:, m % 2:m % 2 + 1]
            else:
                sp = SGP[:, 2:3]
                sr = SGR[:, 2:3]
            P.op("vector", lambda e, buf=buf, sp=sp: e.tensor_scalar(
                buf[:], buf[:], sp, CPX[:, 0:1], ALU.mult, ALU.add))
            P.op("vector", lambda e, buf=buf: e.tensor_scalar(
                buf[:], buf[:], CPX[:, 0:1], 1.0, ALU.subtract, ALU.min))
            if m >= 64:
                r = (m - 64) // 4
                P.op("vector", lambda e, buf=buf, r=r: e.tensor_scalar(
                    buf[:], buf[:], -1.0, GBF[:, r:r + 1], ALU.max, ALU.mult))
            if m >= 2:
                P.dwait_tok("vector", sqa_tok[m - 2])
            if m >= 64:
                vfin = P.op("vector", lambda e, m=m, buf=buf, sr=sr: e.tensor_scalar(
                    WFQA[m % 2][:], buf[:], sr, None, ALU.mult))
            else:
                vfin = P.op("vector", lambda e, m=m, buf=buf, sr=sr: e.tensor_scalar(
                    WFQA[m % 2][:], buf[:], -1.0, sr, ALU.max, ALU.mult))
            vfin_hist[m] = vfin
            if m + 4 < 96:
                P.wait("gpsimd", "v", vfin)
                P.dwait_tok("gpsimd", wqa_tok[m])
                wqa_tok[m + 4] = P.dma("gpsimd", ["sg", "sw"][m % 2],
                                       WQNA[m % 4][:], wqa_src(m + 4))
            P.wait("sync", "v", vfin)
            if m >= 2:
                P.dwait_tok("sync", sta_tok[m - 2])
            sqa_tok[m] = P.dmat(
                WGSTG[m % 2].rearrange("p (a c) -> p a c", c=128), WFQA[m % 2][:])
            P.dwait_tok("sync", sqa_tok[m])
            if m < 64:
                sta_tok[m] = P.dma("sync", "sx", wgq_hbm[m % 2, m // 2],
                                   WGSTG[m % 2][:])
            else:
                r, qq = (m - 64) // 4, (m - 64) % 4
                sta_tok[m] = P.dma(
                    "sync", "sx",
                    wdt_hbm[8 * qq:8 * (qq + 1), :, r * 128:(r + 1) * 128].rearrange(
                        "i p c -> p i c"),
                    WGSTG[m % 2].rearrange("p (a c) -> p a c", c=128))

        P.wait("gpsimd", "v", v_sgp)
        for uu in range(88, 96):
            P.dwait_tok("gpsimd", p1_tok[uu])
        for m0 in range(4):
            wqa_tok[m0] = P.dma("gpsimd", ["sg", "sw"][m0 % 2],
                                WQNA[m0 % 4][:], wqa_src(m0))
        wu_i = 0

        t_kv_hist = {}
        dst3s = {}
        ax_tok = {}
        P.wait("scalar", "v", max(v_prep_end, v_sgp))
        P.wait("scalar", "t", P.c["t"])
        for f0 in range(3):
            ax_tok[f0] = P.dma("scalar", "sa", XT3[f0][:],
                               x_h[f0 * 128:(f0 + 1) * 128, :])
        dst3s[0] = a_prep(0)
        dst3s[1] = a_prep(1)
        for ft in range(NTF):
            if ft + 3 < NTF:
                P.wait("scalar", "v", v_xn_hist.get(ft, 0))
                ax_tok[ft + 3] = P.dma(
                    "scalar", "sa", XT3[(ft + 3) % 3][:],
                    x_h[(ft + 3) * 128:(ft + 4) * 128, :])
            dst3 = dst3s.pop(ft)
            P.wait("tensor", "s", v_mod_hist[ft])
            if ft >= 1:
                P.wait("tensor", "v", ev_v.get(ft - 1, 0))
                P.wait("tensor", "s", ev_s.get(ft - 1, 0))
            for half in range(2):
                kb = bank(0 if half == 0 else 2)
                vb = bank(1 if half == 0 else 3)
                for k in range(J):
                    P.op("tensor", lambda e, k=k, half=half, dst3=dst3, kb=kb: e.matmul(
                        kb, dst3[:, k, :], WAT[:, k, 1, half * 4:(half + 1) * 4, :],
                        start=(k == 0), stop=(k == J - 1)), inc=False)
                    P.op("tensor", lambda e, k=k, half=half, dst3=dst3, vb=vb: e.matmul(
                        vb, dst3[:, k, :], WAT[:, k, 2, half * 4:(half + 1) * 4, :],
                        start=(k == 0), stop=(k == J - 1)), inc=(k == J - 1))
            t_kv = P.c["t"]
            t_kv_hist[ft] = t_kv
            P.wait("vector", "t", t_kv)
            ev_v1 = P.op("vector", lambda e: e.tensor_copy(KTM[:, 0:512], bank(0)))
            P.wait("scalar", "t", t_kv)
            ev_s1 = P.op("scalar", lambda e: e.copy(VTM[:, 0:512], bank(1)))
            ev_v[ft] = P.op("vector", lambda e: e.tensor_copy(KTM[:, 512:1024], bank(2)))
            ev_s[ft] = P.op("scalar", lambda e: e.copy(VTM[:, 512:1024], bank(3)))
            if ft + 2 < NTF:
                dst3s[ft + 2] = a_prep(ft + 2)
            st_flag = (ft == 0)
            sp_flag = (ft == NTF - 1)
            for hp in range(HP):
                if hp == 0:
                    P.wait("tensor", "v", ev_v1)
                    P.wait("tensor", "s", ev_s1)
                elif hp == 4:
                    P.wait("tensor", "v", ev_v[ft])
                    P.wait("tensor", "s", ev_s[ft])
                kkb = kk_bank(hp)
                first = st_flag and hp % 4 == 0
                last = sp_flag and hp % 4 == 3
                P.op("tensor", lambda e, hp=hp, kkb=kkb, first=first: e.matmul(
                    kkb[0:64, 0:64], KTM[:, hp * 128:hp * 128 + 64],
                    KTM[:, hp * 128:hp * 128 + 64],
                    start=first, stop=False, tile_position=(0, 0)), inc=False)
                P.op("tensor", lambda e, hp=hp, kkb=kkb, last=last: e.matmul(
                    kkb[0:64, 64:128], KTM[:, hp * 128:hp * 128 + 64],
                    VTM[:, hp * 128:hp * 128 + 64],
                    start=False, stop=last, tile_position=(0, 0)), inc=False)
                P.op("tensor", lambda e, hp=hp, kkb=kkb, first=first: e.matmul(
                    kkb[64:128, 0:64], KTM[:, hp * 128 + 64:hp * 128 + 128],
                    KTM[:, hp * 128 + 64:hp * 128 + 128],
                    start=first, stop=False, tile_position=(0, 64)), inc=False)
                P.op("tensor", lambda e, hp=hp, kkb=kkb, last=last: e.matmul(
                    kkb[64:128, 64:128], KTM[:, hp * 128 + 64:hp * 128 + 128],
                    VTM[:, hp * 128 + 64:hp * 128 + 128],
                    start=False, stop=last, tile_position=(0, 64)),
                    inc=(hp == HP - 1))
            t_kk_hist[ft] = P.c["t"]
            while wu_i < ((ft + 1) * 96) // 64:
                emit_wq_unit(wu_i)
                wu_i += 1

        t_kkdone = P.c["t"]

        def nss(i, a, b):
            return NSS[:, i * 8 + a:i * 8 + b]

        P.wait("vector", "t", t_kkdone)
        P.op("vector", lambda e: e.tensor_copy(EYFB[:], EYF[:]))
        for hp in range(HP):
            P.op("vector", lambda e, hp=hp: e.scalar_tensor_tensor(
                KKS[:, hp, 0:64], EYF[:], ALP[:, 0:1], kk_bank(hp)[:, 0:64],
                ALU.mult, ALU.add))
            P.op("vector", lambda e, hp=hp: e.tensor_copy(
                KKS[:, hp, 64:128], kk_bank(hp)[:, 64:128]))
        for hp in range(HP):
            P.op("vector", lambda e, hp=hp: e.tensor_copy(KTVB[:, hp, :], KKS[:, hp, 64:128]))
        for hp in range(HP):
            P.op("vector", lambda e, hp=hp: e.tensor_mul(
                SCR[:, hp * 64:(hp + 1) * 64], KKS[:, hp, 0:64], KKS[:, hp, 0:64]))
        P.vgap()
        for hp in range(HP):
            P.op("vector", lambda e, hp=hp: e.reduce_sum(
                nss(0, hp, hp + 1), SCR[:, hp * 64:(hp + 1) * 64], AXX.X))
        P.vgap()
        v_fro = P.op("vector", lambda e: e.tensor_copy(BST[:, 0:8], nss(0, 0, 8)))
        P.wait("tensor", "v", v_fro)
        P.dwait("tensor", ld_sel)
        P.op("tensor", lambda e: e.matmul(bank(5)[0:2, 0:8], HSEL[:], BST[:, 0:8],
                                          start=True, stop=True))
        t_fro = P.c["t"]
        P.wait("vector", "t", t_fro)
        P.op("vector", lambda e: e.reciprocal(ROWS[:, 0:8], bank(5)[0:2, 0:8]))
        P.vgap()
        v_fr2 = P.op("vector", lambda e: e.tensor_copy(ROWSB[:, 0:8], ROWS[:, 0:8]))
        P.wait("tensor", "v", v_fr2)
        P.op("tensor", lambda e: e.matmul(bank(5)[:, 0:8], G2SEL[:], ROWSB[:, 0:8],
                                          start=True, stop=True))
        t_a0 = P.c["t"]
        P.wait("vector", "t", t_a0)
        P.op("vector", lambda e: e.tensor_copy(nss(1, 0, 8), bank(5)[:, 0:8]))
        P.vgap()
        for hp in range(HP):
            P.op("vector", lambda e, hp=hp: e.tensor_scalar(
                XNS[:, hp, :], KKS[:, hp, 0:64], nss(1, hp, hp + 1), None, ALU.mult))
        for hp in range(HP):
            P.op("vector", lambda e, hp=hp: e.memset(ABLK[:, hp, :], 0.0))
        for hp in range(HP):
            P.op("vector", lambda e, hp=hp: e.tensor_copy(
                ABLK[0:64, hp, 0:64], KKS[0:64, hp, 0:64]))
            P.op("vector", lambda e, hp=hp: e.tensor_copy(
                ABLK[64:128, hp, 64:128], KKS[64:128, hp, 0:64]))

        for it in range(NS_ITERS):
            P.wait("tensor", "v", P.c["v"])
            for hp in range(HP):
                P.op("tensor", lambda e, hp=hp: e.matmul(
                    bank(4)[:, hp * 64:(hp + 1) * 64], ABLK[:, hp, :], XNS[:, hp, :],
                    start=(hp == 0), stop=(hp == HP - 1)), inc=(hp == HP - 1))
            t_y = P.c["t"]
            P.wait("vector", "t", t_y)
            for hp in range(HP):
                P.op("vector", lambda e, hp=hp: e.scalar_tensor_tensor(
                    ZNS[:, hp, :], EYFB[:], 2.0, bank(4)[:, hp * 64:(hp + 1) * 64],
                    ALU.mult, ALU.subtract))
            v_z = P.c["v"]
            P.wait("tensor", "v", v_z)
            for hp in range(HP):
                P.op("tensor", lambda e, hp=hp: e.matmul(
                    bank(5)[0:64, hp * 64:(hp + 1) * 64], XNS[0:64, hp, :],
                    ZNS[0:64, hp, :], start=(hp == 0), stop=(hp == HP - 1),
                    tile_position=(0, 0)), inc=False)
                P.op("tensor", lambda e, hp=hp: e.matmul(
                    bank(5)[64:128, hp * 64:(hp + 1) * 64], XNS[64:128, hp, :],
                    ZNS[64:128, hp, :], start=(hp == 0), stop=(hp == HP - 1),
                    tile_position=(64, 64)), inc=(hp == HP - 1))
            t_x2i = P.c["t"]
            P.wait("vector", "t", t_x2i)
            for hp in range(HP):
                P.op("vector", lambda e, hp=hp: e.tensor_copy(
                    XNS[:, hp, :], bank(5)[:, hp * 64:(hp + 1) * 64]))

        P.wait("tensor", "v", P.c["v"])
        for hp in range(HP):
            P.op("tensor", lambda e, hp=hp: e.matmul(
                bank(4)[0:64, hp * 64:(hp + 1) * 64], KTVB[0:64, hp, :],
                XNS[0:64, hp, :], start=(hp == 0), stop=(hp == HP - 1),
                tile_position=(0, 0)), inc=False)
            P.op("tensor", lambda e, hp=hp: e.matmul(
                bank(4)[64:128, hp * 64:(hp + 1) * 64], KTVB[64:128, hp, :],
                XNS[64:128, hp, :], start=(hp == 0), stop=(hp == HP - 1),
                tile_position=(64, 64)), inc=(hp == HP - 1))
        t_zt = P.c["t"]
        P.wait("vector", "t", t_zt)
        P.wait("vector", "v", v_itp)
        for hp in range(HP):
            P.op("vector", lambda e, hp=hp: e.tensor_scalar(
                SCR[:, hp * 64:(hp + 1) * 64], bank(4)[:, hp * 64:(hp + 1) * 64],
                ITP[:, 0:1], None, ALU.mult))
        P.vgap()
        for hp in range(HP):
            P.op("vector", lambda e, hp=hp: e.reduce_max(
                nss(1, hp, hp + 1), SCR[:, hp * 64:(hp + 1) * 64], AXX.X))
        P.vgap()
        v_mx = P.op("vector", lambda e: e.tensor_single_scalar(
            nss(0, 0, 8), nss(1, 0, 8), -1.0, ALU.mult))
        P.wait("scalar", "v", v_mx)
        for hp in range(HP):
            P.op("scalar", lambda e, hp=hp: e.activation(
                SCR[:, hp * 64:(hp + 1) * 64], SCR[:, hp * 64:(hp + 1) * 64],
                ACT.Exp, bias=nss(0, hp, hp + 1)))
        s_exp = P.c["s"]
        P.wait("vector", "s", s_exp)
        for hp in range(HP):
            P.op("vector", lambda e, hp=hp: e.reduce_sum(
                nss(2, hp, hp + 1), SCR[:, hp * 64:(hp + 1) * 64], AXX.X))
        P.vgap()
        P.op("vector", lambda e: e.reciprocal(nss(3, 0, 8), nss(2, 0, 8)))
        P.vgap()
        for hp in range(HP):
            P.op("vector", lambda e, hp=hp: e.tensor_scalar(
                SMB[:, hp, :], SCR[:, hp * 64:(hp + 1) * 64], nss(3, hp, hp + 1),
                None, ALU.mult))
        v_sm = P.c["v"]
        P.wait("tensor", "v", v_sm)
        v_mev = {}
        for hp in range(HP):
            bb = 4 + hp % 2
            if hp >= 2:
                P.wait("tensor", "v", v_mev[hp - 2])
            P.op("tensor", lambda e, hp=hp, bb=bb: e.transpose(
                bankb(bb)[0:64, 0:64], SMB[0:64, hp, :],
                EYE[0:64, 0:64]), inc=False)
            t_tp = P.op("tensor", lambda e, hp=hp, bb=bb: e.transpose(
                bankb(bb)[64:128, 64:128], SMB[64:128, hp, :],
                EYE[64:128, 64:128], tile_position=(64, 64)))
            P.wait("vector", "t", t_tp)
            P.op("vector", lambda e, hp=hp, bb=bb: e.tensor_copy(
                MAPB[0:64, hp, :], bankb(bb)[0:64, 0:64]), inc=False)
            v_mev[hp] = P.op("vector", lambda e, hp=hp, bb=bb: e.tensor_copy(
                MAPB[64:128, hp, :], bankb(bb)[64:128, 64:128]))
        v_map = P.c["v"]

        sa_h2q_st = {}
        sq_h2_hist = {}
        XTC = YTG.rearrange("p a b -> p (a b)").bitcast(F32).rearrange(
            "p (b c) -> p b c", b=4)
        v_x2_hist = {}
        v_ln_hist = {}
        xtc_tok = {}
        x2st_tok = {}
        P.g_h2_prev = 0
        P.wait("scalar", "v", P.c["v"])
        P.wait("sync", "v", P.c["v"])
        P.wait("gpsimd", "v", P.c["v"])
        P.wait("gpsimd", "t", P.c["t"])
        P.wait("gpsimd", "s", P.c["s"])
        P.wait("tensor", "v", P.c["v"])
        P.wait("tensor", "s", P.c["s"])
        P.wait("tensor", "g", P.c["g"])
        v_ao_prev = P.c["v"]
        s_zbb7_prev = P.c["s"]
        s_oev7_prev = P.c["s"]
        qb_of = [bank(0), bank(1), bank(6), bank(7)]
        X2T4 = [Uf[:, 7168:8192], Uf[:, 8192:9216],
                Uf[:, 1024:2048], Uf[:, 11264:12288]]
        MU2 = SYCOL[:, 0:2]
        VB2 = SYCOL[:, 2:4]
        SD2 = SYCOL[:, 4:6]
        RS2 = SYCOL[:, 6:8]
        sq_ao_hist = {}
        va_hist = {}
        sqrt_hist = {}
        P.qbarrier("scalar", "sq")
        P.qbarrier("sync", "sx")
        P.qbarrier("sync", "sq")

        def t4_partA(st, t4):
            ft = st * 4 + t4
            P.dwait_tok("vector", xtc_tok[(st, t4)])
            if ft >= 4:
                P.dwait_tok("vector", x2st_tok[ft - 4])
            P.dwait("vector", sq_ao_hist[st])
            aosl = AOTM[:, :, t4, :]
            v_x2 = P.op("vector", lambda e, t4=t4, aosl=aosl: e.tensor_add(
                X2T4[t4][:], XTC[:, t4, :], aosl))
            v_x2_hist[(st, t4)] = v_x2
            P.wait("sync", "v", v_x2)
            x2st_tok[ft] = P.dma("sync", "sx",
                                 x2_hbm[ft * 128:(ft + 1) * 128, :], X2T4[t4][:])
            P.op("vector", lambda e, t4=t4: e.tensor_mul(
                SCR[:], X2T4[t4][:], X2T4[t4][:]))
            P.op("vector", lambda e: e.reduce_sum(stat_a(0, 0, 1), SCR[:], AXX.X))
            P.op("vector", lambda e, t4=t4: e.reduce_sum(
                stat_a(0, 1, 2), X2T4[t4][:], AXX.X))
            P.op("vector", lambda e, t4=t4: e.tensor_single_scalar(
                MU2[:, t4 % 2:t4 % 2 + 1], stat_a(0, 1, 2), 1.0 / D, ALU.mult))
            P.op("vector", lambda e: e.tensor_single_scalar(
                stat_a(0, 3, 4), stat_a(0, 0, 1), 1.0 / D, ALU.mult))
            P.op("vector", lambda e, t4=t4: e.tensor_mul(
                stat_a(0, 4, 5), MU2[:, t4 % 2:t4 % 2 + 1], MU2[:, t4 % 2:t4 % 2 + 1]))
            va_hist[(st, t4)] = P.op("vector", lambda e, t4=t4: e.tensor_sub(
                VB2[:, t4 % 2:t4 % 2 + 1], stat_a(0, 3, 4), stat_a(0, 4, 5)))

        def t4_sqrt(st, pair):
            P.wait("scalar", "v", va_hist[(st, 2 * pair + 1)])
            sqrt_hist[(st, pair)] = P.op("scalar", lambda e: e.activation(
                SD2[:, 0:2], VB2[:, 0:2], ACT.Sqrt, bias=EPSA))

        def t4_partB(st, t4):
            ft = st * 4 + t4
            P.wait("vector", "s", sqrt_hist[(st, t4 // 2)])
            P.op("vector", lambda e, t4=t4: e.reciprocal(
                RS2[:, t4 % 2:t4 % 2 + 1], SD2[:, t4 % 2:t4 % 2 + 1]))
            v_ln = P.op("vector", lambda e, t4=t4: e.tensor_scalar(
                SCR[:], X2T4[t4][:], MU2[:, t4 % 2:t4 % 2 + 1],
                RS2[:, t4 % 2:t4 % 2 + 1], ALU.subtract, ALU.mult))
            P.op("vector", lambda e: e.tensor_mul(SCR[:], SCR[:], CB1BC[:]))
            P.vgap()
            P.op("vector", lambda e: e.tensor_add(X3T[:, 0, :], SCR[:], SBBC[:]))
            P.vgap()
            P.op("vector", lambda e: e.reduce_max(stat_a(1, 0, 1), X3T[:, 0, :], AXX.X,
                                                  apply_absolute_value=True))
            P.vgap()
            P.op("vector", lambda e: e.reciprocal(stat_a(1, 1, 2), stat_a(1, 0, 1)))
            P.vgap()
            P.op("vector", lambda e: e.tensor_single_scalar(
                stat_a(1, 2, 3), stat_a(1, 1, 2), 127.0, ALU.mult))
            P.op("vector", lambda e: e.tensor_single_scalar(
                stat_a(1, 3, 4), stat_a(1, 0, 1), 1.0 / 127.0, ALU.mult))
            P.op("vector", lambda e, t4=t4: e.tensor_scalar(
                SCR[:], X3T[:, 0, :], stat_a(1, 2, 3), CPX[:, 0:1],
                ALU.mult, ALU.add))
            P.vgap()
            v_h2q = P.op("vector", lambda e, t4=t4: e.tensor_scalar(
                H2T1[:], SCR[:], CPX[:, 0:1], stat_a(1, 3, 4),
                ALU.subtract, ALU.mult))
            P.wait("sync", "v", v_h2q)
            if t4 == 0 and st >= 1:
                P.dwait("sync", sa_h2q_st[st - 1])
            sq_h2 = P.dmat(H2FM[:, t4 * 8:(t4 + 1) * 8, :], H2T1[:])
            sq_h2_hist[st] = sq_h2
            P.dwait("vector", sq_h2)

        def t4_store(st):
            P.dwait("sync", sq_h2_hist[st])
            sa_h2q_st[st] = P.dma("sync", "sx", h2q_hbm[st], H2FM[:])

        for t4 in range(4):
            xtc_tok[(0, t4)] = P.dma(
                "scalar", "sa", XTC[:, t4, :], x_h[t4 * 128:(t4 + 1) * 128, :])
        for st in range(NST):
            cols = slice(st * 512, (st + 1) * 512)
            if st >= 1:
                t4_partA(st - 1, 0)
                t4_partA(st - 1, 1)
            s_qev = {}
            P.wait("tensor", "s", s_zbb7_prev)
            for j in range(J):
                qb = qb_of[j % 4]
                if j >= 4:
                    P.wait("tensor", "s", s_qev[j - 4])
                for k in range(J):
                    P.op("tensor", lambda e, j=j, k=k, qb=qb, cols=cols: e.matmul(
                        qb, WAT[:, k, 0, j, :], HF[:, k, cols],
                        start=(k == 0), stop=(k == J - 1)), inc=(k == J - 1))
                t_q = P.c["t"]
                P.wait("scalar", "t", t_q)
                s_qev[j] = P.op("scalar", lambda e, j=j, qb=qb: e.activation(
                    QF[:, j, :], qb, ACT.Copy))
            if st >= 1:
                t4_sqrt(st - 1, 0)
                t4_partB(st - 1, 0)
                t4_partB(st - 1, 1)
            s_oev = {}
            P.wait("tensor", "s", s_oev7_prev)
            for j in range(J):
                ob = bank(2 + j % 2)
                if j >= 2:
                    P.wait("tensor", "s", s_oev[j - 2])
                P.wait("tensor", "s", s_qev[j])
                P.op("tensor", lambda e, j=j, ob=ob: e.matmul(
                    ob[0:64, :], MAPB[0:64, j, :], QF[0:64, j, :],
                    start=True, stop=True, tile_position=(0, 0)), inc=False)
                P.op("tensor", lambda e, j=j, ob=ob: e.matmul(
                    ob[64:128, :], MAPB[64:128, j, :], QF[64:128, j, :],
                    start=True, stop=True, tile_position=(64, 64)), inc=True)
                t_o = P.c["t"]
                P.wait("scalar", "t", t_o)
                s_oev[j] = P.op("scalar", lambda e, j=j, ob=ob: e.activation(
                    OF[:, j, :], ob, ACT.Copy))
            s_oev7_prev = s_oev[J - 1]
            if st >= 1:
                t4_partA(st - 1, 2)
                t4_partA(st - 1, 3)
                t4_sqrt(st - 1, 1)
                t4_partB(st - 1, 2)
                t4_partB(st - 1, 3)
                t4_store(st - 1)
            for t4 in range(4):
                if st >= 1:
                    P.wait("scalar", "v", v_x2_hist[(st - 1, t4)])
                    xtc_tok[(st, t4)] = P.dma(
                        "scalar", "sa", XTC[:, t4, :],
                        x_h[(st * 4 + t4) * 128:(st * 4 + t4 + 1) * 128, :])
            s_zbb = {}
            s_sq = {}
            t_stats = {}
            P.wait("tensor", "v", v_ao_prev)
            P.wait("tensor", "s", s_oev[J - 1])
            for j in range(J):
                wb = qb_of[j % 4]
                if j < 4:
                    P.wait("tensor", "s", s_qev[j + 4])
                else:
                    P.wait("tensor", "s", s_zbb[j - 4])
                for k in range(J):
                    P.op("tensor", lambda e, j=j, k=k, wb=wb: e.matmul(
                        wb, WAT[:, k, 3, j, :], OF[:, k, :],
                        start=(k == 0), stop=False), inc=False)
                P.op("tensor", lambda e, j=j, wb=wb, cols=cols: e.matmul(
                    wb, EYE[:], HF[:, j, cols], start=False, stop=True), inc=True)
                t_w = P.c["t"]
                if j >= 2:
                    P.wait("scalar", "t", t_stats[j - 2])
                P.wait("scalar", "t", t_w)
                s_zbb[j] = P.op("scalar", lambda e, j=j, wb=wb: e.activation(
                    ZBB[:, j, :], wb, ACT.Identity, bias=BOF[:, j:j + 1]))
                s_sq[j] = P.op("scalar", lambda e, j=j: e.activation(
                    SQB[:, j % 2, :], ZBB[:, j, :], ACT.Square))
                P.wait("tensor", "s", s_sq[j])
                P.op("tensor", lambda e, j=j: e.matmul(
                    bank(4)[0:1, :], ONEC[:], ZBB[:, j, :],
                    start=(j == 0), stop=(j == J - 1)), inc=False)
                P.op("tensor", lambda e, j=j: e.matmul(
                    bank(5)[0:1, :], ONEC[:], SQB[:, j % 2, :],
                    start=(j == 0), stop=(j == J - 1)), inc=(j == J - 1))
                t_stats[j] = P.c["t"]
            s_zbb7_prev = s_sq[J - 1]
            t_st = P.c["t"]
            P.wait("vector", "t", t_st)
            P.op("vector", lambda e: e.tensor_single_scalar(st1(0), bank(4)[0:1, :], 1.0 / D, ALU.mult))
            P.op("vector", lambda e: e.tensor_single_scalar(st1(1), bank(5)[0:1, :], 1.0 / D, ALU.mult))
            P.vgap()
            P.op("vector", lambda e: e.tensor_mul(st1(2), st1(0), st1(0)))
            P.vgap()
            v_vr = P.op("vector", lambda e: e.tensor_sub(st1(1), st1(1), st1(2)))
            P.wait("scalar", "v", v_vr)
            s_sd2 = P.op("scalar", lambda e: e.activation(st1(2), st1(1), ACT.Sqrt, bias=EPSA[0:1, 0:1]))
            P.wait("vector", "s", s_sd2)
            P.op("vector", lambda e: e.reciprocal(st1(1), st1(2)))
            P.vgap()
            P.op("vector", lambda e: e.tensor_mul(st1(2), st1(0), st1(1)))
            P.vgap()
            P.op("vector", lambda e: e.tensor_copy(st1b(0), st1(1)))
            v_r2 = P.op("vector", lambda e: e.tensor_copy(st1b(1), st1(2)))
            P.wait("tensor", "v", v_r2)
            P.op("tensor", lambda e: e.matmul(
                bank(4), ONE1[0:1, 0:128], st1b(1), start=True, stop=True), inc=False)
            P.op("tensor", lambda e: e.matmul(
                bank(5), ONE1[0:1, 0:128], st1b(0), start=True, stop=True), inc=True)
            t_bc = P.c["t"]
            P.wait("vector", "t", t_bc)
            if st >= 1:
                P.dwait("vector", sq_ao_hist[st - 1])
            for j in range(J):
                P.op("vector", lambda e, j=j: e.tensor_mul(
                    SCR[:, 0:512], ZBB[:, j, :], bank(5)))
                P.op("vector", lambda e, j=j: e.tensor_sub(
                    SCR[:, 0:512], SCR[:, 0:512], bank(4)))
                vj = P.op("vector", lambda e, j=j: e.tensor_scalar(
                    AOC[:, j * 512:(j + 1) * 512], SCR[:, 0:512],
                    G1F[:, j:j + 1], G0F[:, j:j + 1], ALU.mult, ALU.add))
                P.wait("sync", "v", vj)
                sq_ao = P.dmat(AOTM[:, j, :, :], AOC[:, j * 512:(j + 1) * 512])
            v_ao = P.c["v"]
            v_ao_prev = v_ao
            sq_ao_hist[st] = sq_ao

        t4_partA(NST - 1, 0)
        t4_partA(NST - 1, 1)
        t4_sqrt(NST - 1, 0)
        t4_partB(NST - 1, 0)
        t4_partB(NST - 1, 1)
        t4_partA(NST - 1, 2)
        t4_partA(NST - 1, 3)
        t4_sqrt(NST - 1, 1)
        t4_partB(NST - 1, 2)
        t4_partB(NST - 1, 3)
        t4_store(NST - 1)

        WGT = HF
        WUT = WAT.rearrange("p a b c d -> p (a b c d)").rearrange(
            "p (k m) -> p k m", k=J)
        wgt_r = WGT.rearrange("p k (i c) -> p k i c", c=128)
        wut_r = WUT.rearrange("p k (i c) -> p k i c", c=128)
        h2q_r = h2q_hbm.rearrange("s p (t k) c -> s p t k c", k=J)
        s_gs = None
        v_up = None
        g_y_prev = 0
        s_up_hist = {}
        g_y_hist = {}
        s_up_save = {}
        g_y_save = {}
        sq_wq_last = None
        v_wq_by = {0: 0, 1: 0}
        sg_yq = {}
        t_gu_hist = {}
        P.wait("gpsimd", "v", P.c["v"])
        P.wait("gpsimd", "t", P.c["t"])
        P.wait("gpsimd", "s", P.c["s"])
        P.qbarrier("gpsimd", "sq")
        P.qbarrier("gpsimd", "sa")
        P.qbarrier("gpsimd", "sx")
        P.qbarrier("scalar", "sx")
        P.wait("scalar", "g", P.c["g"])
        P.wait("scalar", "v", P.c["v"])
        P.wait("scalar", "t", P.c["t"])
        P.wait("sync", "t", P.c["t"])
        P.wait("sync", "s", P.c["s"])
        WQN8 = [Uf[:, 2048:3072], Uf[:, 3072:4096], Uf[:, 5640:6664],
                Uf[:, 6664:7688], Uf[:, 7688:8712], Uf[:, 8712:9736],
                Uf[:, 9736:10760]]
        SCRQ2 = Uf[:, 10760:11784]
        WFQ2 = [WFQ, U[:, 2048:3072]]

        def wsrc_of(m):
            return ((wg_h, wu_h)[m % 2])[(m // 2) * 128:(m // 2 + 1) * 128, :]

        wq_tok = {}
        sq_wq_hist = {}
        g_wq_hist = {}
        g_wq_last = 0
        sa_h2t_tok = {}
        sa_h2t_tok[0] = P.dma("scalar", "sa", H2QT[:, :, :, :],
                              h2q_r[0, :, 0:2, :, :])
        for tg in range(1):
            if tg >= 1:
                P.dwait("gpsimd", sg_yq[tg - 1])
            sa_h2t = sa_h2t_tok[tg]
            h2b = H2QT if tg % 2 == 0 else H2QT2
            if tg == 0:
                wst_tok = {}
                for ii in range(4):
                    for which in range(2):
                        wst_tok[2 * ii + which] = P.dma(
                            "scalar", "sa", (wgt_r, wut_r)[which][:, :, ii, :],
                            wgq_hbm[which, ii])
            for i in range(NI):
                if tg == 0:
                    P.dwait_tok("tensor", wst_tok[2 * i])
                    P.dwait_tok("tensor", wst_tok[2 * i + 1])
                P.dwait_tok("tensor", sa_h2t)
                gb = bank(0 + 2 * (i % 2), TGW)
                ub = bank(1 + 2 * (i % 2), TGW)
                GSi = GS if i % 2 == 0 else GSB
                UPSi = UPS if i % 2 == 0 else UPSB
                if i >= 2:
                    P.wait("tensor", "s", s_up_save[i - 2])
                elif tg >= 1:
                    P.wait("tensor", "s", s_up_hist[30 + i])
                for k in range(J):
                    rhs = h2b[:, :, k, :]
                    P.op("tensor", lambda e, i=i, k=k, rhs=rhs, gb=gb: e.matmul(
                        gb, wgt_r[:, k, i, :], rhs,
                        start=(k == 0), stop=(k == J - 1)), inc=False)
                    P.op("tensor", lambda e, i=i, k=k, rhs=rhs, ub=ub: e.matmul(
                        ub, wut_r[:, k, i, :], rhs,
                        start=(k == 0), stop=(k == J - 1)), inc=(k == J - 1))
                t_gu = P.c["t"]
                if tg == 0 and i + 4 < NI:
                    P.wait("scalar", "t", t_gu)
                    P.dwait_tok("scalar", wst_tok[2 * i])
                    P.dwait_tok("scalar", wst_tok[2 * i + 1])
                    for which in range(2):
                        wst_tok[2 * (i + 4) + which] = P.dma(
                            "scalar", "sa", (wgt_r, wut_r)[which][:, :, i + 4, :],
                            wgq_hbm[which, i + 4])
                P.wait("scalar", "t", t_gu)
                if i >= 2:
                    P.wait("scalar", "g", g_y_save[i - 2])
                elif tg >= 1:
                    P.wait("scalar", "g", g_y_hist[30 + i])
                s_gs = P.op("scalar", lambda e, GSi=GSi, gb=gb: e.activation(
                    GSi[:], gb, ACT.Silu))
                s_up = P.op("scalar", lambda e, UPSi=UPSi, ub=ub: e.activation(
                    UPSi[:], ub, ACT.Copy))
                s_up_save[i] = s_up
                P.wait("gpsimd", "s", s_up)
                g_y = P.op("gpsimd", lambda e, i=i, GSi=GSi, UPSi=UPSi: e.tensor_mul(
                    YTG[:, i, :], GSi[:], UPSi[:]))
                g_y_save[i] = g_y
                g_y_prev = g_y
                P.wait("vector", "g", g_y)
                if i == 0:
                    P.op("vector", lambda e: e.tensor_copy(MXT[:], YTG[:, 0, :]))
                    P.op("vector", lambda e: e.tensor_copy(MNT[:], YTG[:, 0, :]))
                else:
                    P.op("vector", lambda e, i=i: e.tensor_max(
                        MXT[:], MXT[:], YTG[:, i, :]))
                    P.op("vector", lambda e, i=i: e.tensor_tensor(
                        MNT[:], MNT[:], YTG[:, i, :], ALU.min))
            s_up_hist = s_up_save
            g_y_hist = g_y_save
            s_up_save = {}
            g_y_save = {}
            t_gu_hist[tg] = P.c["t"]
            if tg + 1 < NTG:
                if tg >= 1:
                    P.wait("scalar", "t", t_gu_hist[tg - 1])
                h2bn = H2QT if (tg + 1) % 2 == 0 else H2QT2
                sa_h2t_tok[tg + 1] = P.dma(
                    "scalar", "sa", h2bn[:, :, :, :],
                    h2q_r[(tg + 1) // 2, :, 2 * ((tg + 1) % 2):2 * ((tg + 1) % 2) + 2, :, :])
            P.op("vector", lambda e: e.tensor_single_scalar(
                MNT[:], MNT[:], -1.0, ALU.mult))
            P.op("vector", lambda e: e.tensor_max(MXT[:], MXT[:], MNT[:]))
            g_ymax = P.c["v"]
            P.wait("tensor", "v", g_ymax)
            for t4 in range(2):
                bb = 4 + t4
                P.op("tensor", lambda e, t4=t4, bb=bb: e.transpose(
                    bankb(bb)[:, 0:128], MXT[:, t4 * 128:(t4 + 1) * 128], EYE[:]),
                    inc=True)
            t_mx = P.c["t"]
            P.wait("vector", "t", t_mx)
            for t4 in range(2):
                bb = 4 + t4
                P.op("vector", lambda e, t4=t4, bb=bb: e.reduce_max(
                    SYCOL[:, t4:t4 + 1], bankb(bb)[:, 0:128], AXX.X))
            for t4 in range(2):
                P.op("vector", lambda e, t4=t4: e.reciprocal(
                    SYCOL[:, t4 + 4:t4 + 5], SYCOL[:, t4:t4 + 1]))
            P.vgap()
            for t4 in range(2):
                P.op("vector", lambda e, t4=t4, tg=tg: e.tensor_single_scalar(
                    SYR[:, tg * 2 + t4:tg * 2 + t4 + 1], SYCOL[:, t4:t4 + 1],
                    1.0 / 127.0, ALU.mult))
                P.op("vector", lambda e, t4=t4: e.tensor_single_scalar(
                    SYCOL[:, t4:t4 + 1], SYCOL[:, t4 + 4:t4 + 5], 127.0, ALU.mult))
            P.vgap()
            v_syb = P.op("vector", lambda e: e.tensor_copy(SYKT[:, 0:2], SYCOL[:, 0:2]))
            P.wait("tensor", "v", v_syb)
            for t4 in range(2):
                P.op("tensor", lambda e, t4=t4: e.matmul(
                    bank(5)[0:1, t4 * 128:(t4 + 1) * 128], SYKT[:, t4:t4 + 1], EYE[:],
                    start=(t4 == 0), stop=(t4 == 1)), inc=(t4 == 1))
            t_syr = P.c["t"]
            P.wait("vector", "t", t_syr)
            v_row = P.op("vector", lambda e: e.tensor_copy(SYROW, bank(5)[0:1, 0:TGW]))
            P.wait("tensor", "v", v_row)
            P.op("tensor", lambda e: e.matmul(bank(5, TGW), ONE1[0:1, 0:128],
                                              SYROW, start=True, stop=True))
            t_syb = P.c["t"]
            P.wait("vector", "t", t_syb)
            v_sybs = P.op("vector", lambda e: e.tensor_copy(SYBS[:], bank(5, TGW)))
            P.wait("gpsimd", "v", v_sybs)
            v_rnd = {}
            for i in range(NI):
                if i >= 2:
                    P.wait("gpsimd", "v", v_rnd[i - 2])
                g_z = P.op("gpsimd", lambda e, i=i: e.tensor_mul(
                    ZQ[:, i % 2, :], YTG[:, i, :], SYBS[:]))
                P.wait("vector", "g", g_z)
                v_rnd[i] = P.op("vector", lambda e, i=i: e.tensor_scalar(
                    YTG[:, i, :], ZQ[:, i % 2, :], CPX[:, 0:1], CPX[:, 0:1],
                    ALU.add, ALU.subtract))
            v_yq = P.c["v"]
            P.wait("gpsimd", "v", v_yq)
            sg_yq[tg] = P.dma("gpsimd", "sg", yq_hbm[tg], YTG[:, :, :])

        v_yq_prev = v_yq
        g_zq_prev = P.c["g"]
        YTGflat2 = YTG.rearrange("p a b -> p (a b)")
        GS2 = [U[:, 9216:11264].rearrange("p (a c) -> p a c", c=512),
               U[:, 11272:13320].rearrange("p (a c) -> p a c", c=512)]
        YSTG = [U[:, 18432:19456].rearrange("p (a c) -> p a c", c=128),
                U[:, 19456:20480].rearrange("p (a c) -> p a c", c=128)]
        su2 = {}
        gy2 = {}
        yst_tok_prev = {}
        for tg in range(1, NTG):
            P.dwait("gpsimd", sg_yq.get(tg - 1, 0))
            P.qbarrier("gpsimd", "sx")
            h2b = H2QT if tg % 2 == 0 else H2QT2
            P.dwait_tok("tensor", sa_h2t_tok[tg])
            for t in range(2):
                for bb in range(4):
                    u = t * 4 + bb
                    gu = tg * 8 + u
                    bs = 4 * (u % 2)
                    bks = [bank(bs), bank(bs + 1), bank(bs + 2), bank(bs + 3)]
                    if gu >= 10:
                        P.wait("tensor", "s", su2.get(gu - 2, 0))
                    elif u == 0:
                        P.wait("tensor", "s", s_up_hist[31])
                    else:
                        P.wait("tensor", "v", v_yq_prev)
                    for k in range(J):
                        lhs = h2b[:, t, k, :]
                        for bi in range(2):
                            b = 2 * bb + bi
                            P.op("tensor", lambda e, lhs=lhs, b=b, bi=bi, k=k, bks=bks: e.matmul(
                                bks[bi], lhs, WGT[:, k, b * 512:(b + 1) * 512],
                                start=(k == 0), stop=(k == J - 1)), inc=False)
                            P.op("tensor", lambda e, lhs=lhs, b=b, bi=bi, k=k, bks=bks: e.matmul(
                                bks[2 + bi], lhs, WUT[:, k, b * 512:(b + 1) * 512],
                                start=(k == 0), stop=(k == J - 1)),
                                inc=(k == J - 1 and bi == 1))
                    t_gu = P.c["t"]
                    P.wait("scalar", "t", t_gu)
                    if gu >= 10:
                        P.wait("scalar", "g", gy2.get(gu - 2, 0))
                    elif tg == 1:
                        P.wait("scalar", "g", g_zq_prev)
                        P.wait("scalar", "v", v_yq_prev)
                    gsb = GS2[u % 2]
                    for bi in range(2):
                        P.op("scalar", lambda e, gsb=gsb, bi=bi, bks=bks: e.activation(
                            gsb[:, bi, :], bks[bi], ACT.Silu), inc=False)
                        P.op("scalar", lambda e, gsb=gsb, bi=bi, bks=bks: e.activation(
                            gsb[:, 2 + bi, :], bks[2 + bi], ACT.Copy),
                            inc=(bi == 1))
                    s_up2 = P.c["s"]
                    su2[gu] = s_up2
                    P.wait("gpsimd", "s", s_up2)
                    for bi in range(2):
                        b = 2 * bb + bi
                        P.op("gpsimd", lambda e, t=t, b=b, bi=bi, gsb=gsb: e.tensor_mul(
                            YTGflat2[:, t * 4096 + b * 512:t * 4096 + (b + 1) * 512],
                            gsb[:, bi, :], gsb[:, 2 + bi, :]), inc=(bi == 1))
                    g_y2 = P.c["g"]
                    gy2[gu] = g_y2
                    P.wait("vector", "g", g_y2)
                    for bi in range(2):
                        b = 2 * bb + bi
                        if u == 0 and bi == 0:
                            P.op("vector", lambda e, t=t, b=b: e.reduce_max(
                                SYCOL[:, 4 + t:5 + t],
                                YTGflat2[:, t * 4096 + b * 512:t * 4096 + (b + 1) * 512],
                                AXX.X, apply_absolute_value=True))
                        else:
                            P.op("vector", lambda e, t=t, b=b: e.reduce_max(
                                SYCOL[:, 6:7],
                                YTGflat2[:, t * 4096 + b * 512:t * 4096 + (b + 1) * 512],
                                AXX.X, apply_absolute_value=True))
                            P.vgap()
                            P.op("vector", lambda e, t=t: e.tensor_max(
                                SYCOL[:, 4 + t:5 + t], SYCOL[:, 4 + t:5 + t],
                                SYCOL[:, 6:7]))
                yt = YTGflat2[:, t * 4096:(t + 1) * 4096]
                P.op("vector", lambda e, t=t, tg=tg: e.tensor_single_scalar(
                    SYR[:, 2 * tg + t:2 * tg + t + 1], SYCOL[:, 4 + t:5 + t],
                    1.0 / 127.0, ALU.mult))
                P.op("vector", lambda e, t=t: e.reciprocal(
                    SYCOL[:, 7:8], SYCOL[:, 4 + t:5 + t]))
                P.vgap()
                P.op("vector", lambda e: e.tensor_single_scalar(
                    SYCOL[:, 7:8], SYCOL[:, 7:8], 127.0, ALU.mult))
                P.vgap()
                P.op("vector", lambda e, yt=yt: e.tensor_scalar(
                    yt, yt, SYCOL[:, 7:8], None, ALU.mult))
                P.vgap()
                v_rq = P.op("vector", lambda e, yt=yt: e.tensor_scalar(
                    yt, yt, CPX[:, 0:1], CPX[:, 0:1], ALU.add, ALU.subtract))
                P.wait("sync", "v", v_rq)
                yst_tok = {}
                for cc in range(4):
                    if cc >= 2:
                        P.dwait_tok("sync", yst_tok[cc - 2])
                    elif t == 1 or tg >= 2:
                        P.dwait_tok("sync", yst_tok_prev.get(cc + 2, 0))
                    sqy = P.dmat(YSTG[cc % 2][:, :, :],
                                 yt[:, cc * 1024:(cc + 1) * 1024])
                    P.dwait_tok("sync", sqy)
                    yst_tok[cc] = P.dma(
                        "sync", "sx",
                        yq_hbm[tg][:, 8 * cc:8 * cc + 8, t * 128:(t + 1) * 128],
                        YSTG[cc % 2][:, :, :])
                yst_tok_prev = yst_tok
                if t == 1:
                    v_yq_prev = v_rq
                    g_zq_prev = gy2[tg * 8 + 7]
            t_gu_hist[tg] = P.c["t"]
            if tg + 1 < NTG:
                P.wait("scalar", "t", t_gu_hist[tg - 1])
                h2bn = H2QT if (tg + 1) % 2 == 0 else H2QT2
                sa_h2t_tok[tg + 1] = P.dma(
                    "scalar", "sa", h2bn[:, :, :, :],
                    h2q_r[(tg + 1) // 2, :, 2 * ((tg + 1) % 2):2 * ((tg + 1) % 2) + 2, :, :])
            sg_yq[tg] = 0

        sa_out_hist = {}
        P.wait("gpsimd", "v", P.c["v"])
        P.wait("gpsimd", "s", P.c["s"])
        P.qbarrier("gpsimd", "sg")
        P.qbarrier("gpsimd", "sx")
        P.wait("scalar", "v", P.c["v"])
        P.wait("scalar", "g", P.c["g"])
        P.qbarrier("scalar", "sg")
        P.qbarrier("scalar", "sq")
        P.qbarrier("sync", "sa")
        P.qbarrier("sync", "sx")
        WDTS4 = U[:, 4096:8192].rearrange("p (b c) -> p b c", b=4)
        YTG2B = [YTG.rearrange("p a b -> p (a b)"), U[:, 8192:16384]]
        X2TB = Uf[:, 9216:11264].rearrange("p (b c) -> p b c", b=2)
        t_dn_prev = 0
        v_x3_prev = 0
        y_tok = {}
        for j in range(2):
            y_tok[j] = P.dma("gpsimd", ("sg", "sw")[j], YTG2B[j][:], yq_hbm[j])
        for pp in range(8):
            P.wait("tensor", "v", v_x3_prev)
            P.dwait_tok("tensor", y_tok[0])
            P.dwait_tok("tensor", y_tok[1])
            wds_tok = {}
            for w0 in range(4):
                eng0 = ("scalar", "sa") if w0 % 2 == 0 else ("gpsimd", "sg")
                P.wait(eng0[0], "t", t_dn_prev)
                wds_tok[w0] = P.dma(eng0[0], eng0[1], WDTS4[:, w0, :], wdt_hbm[w0])
            yv = [YTG2B[0].rearrange("p (i c) -> p i c", c=256),
                  YTG2B[1].rearrange("p (i c) -> p i c", c=256)]
            t_dn_i = {}
            for i in range(NI):
                P.dwait_tok("tensor", wds_tok[i])
                for j in range(2):
                    for t4 in range(2):
                        dpb = PS[:, 4 * j + 2 * t4:4 * j + 2 * t4 + 2, :].rearrange(
                            "p a b -> p (a b)")
                        for half in range(2):
                            P.op("tensor", lambda e, i=i, j=j, t4=t4, half=half, dpb=dpb, yv=yv: e.matmul(
                                dpb[:, half * 512:(half + 1) * 512],
                                yv[j][:, i, t4 * 128:(t4 + 1) * 128],
                                WDTS4[:, i % 4, half * 512:(half + 1) * 512],
                                start=(i == 0), stop=(i == NI - 1)),
                                inc=(half == 1 and t4 == 1 and j == 1))
                t_dn_i[i] = P.c["t"]
                if i + 4 < NI:
                    eng0 = ("scalar", "sa") if (i + 4) % 2 == 0 else ("gpsimd", "sg")
                    P.wait(eng0[0], "t", t_dn_i[i])
                    if i >= 1:
                        P.dwait_tok(eng0[0], wds_tok[i - 1])
                    wds_tok[i + 4] = P.dma(eng0[0], eng0[1],
                                           WDTS4[:, (i + 4) % 4, :], wdt_hbm[i + 4])
            t_dn = P.c["t"]
            t_dn_prev = t_dn
            if pp + 1 < 8:
                for j in range(2):
                    P.wait("gpsimd", "t", t_dn)
                    P.dwait_tok("gpsimd", y_tok[j])
                    y_tok[j] = P.dma("gpsimd", ("sg", "sw")[j], YTG2B[j][:],
                                     yq_hbm[2 * (pp + 1) + j])
            P.wait("vector", "t", t_dn)
            for j in range(2):
                tg = 2 * pp + j
                for t4 in range(2):
                    ft = tg * 2 + t4
                    P.wait("sync", "v", P.c["v"])
                    sa_x2r = P.dma("sync", "sq", X2TB[:, t4, :],
                                   x2_hbm[ft * 128:(ft + 1) * 128, :])
                    P.dwait("vector", sa_x2r)
                    if ft >= 2:
                        P.dwait("vector", sa_out_hist.get(ft - 2, 0))
                    dpb = PS[:, 4 * j + 2 * t4:4 * j + 2 * t4 + 2, :].rearrange(
                        "p a b -> p (a b)")
                    v_x3 = P.op("vector", lambda e, t4=t4, tg=tg, dpb=dpb: e.scalar_tensor_tensor(
                        X3T[:, t4 % 2, :], dpb, SYR[:, tg * 2 + t4:tg * 2 + t4 + 1],
                        X2TB[:, t4, :], ALU.mult, ALU.add))
                    P.wait("sync", "v", v_x3)
                    v_x3_prev = v_x3
                    sa_out_hist[ft] = P.dma("sync", "sq",
                                            out_h[ft * 128:(ft + 1) * 128, :],
                                            X3T[:, t4 % 2, :])

        P.qbarrier("sync", "sa")
        P.qbarrier("sync", "sg")
        P.qbarrier("sync", "sq")
        P.qbarrier("sync", "sw")
        P.qbarrier("sync", "sx")
        P.wait("sync", "v", P.c["v"])
        P.wait("sync", "t", P.c["t"])
        P.wait("sync", "s", P.c["s"])
        P.wait("sync", "g", P.c["g"])

        @block.sync
        def _(e):
            P.run("sync", e)

        @block.vector
        def _(e):
            P.run("vector", e)

        @block.scalar
        def _(e):
            P.run("scalar", e)

        @block.tensor
        def _(e):
            P.run("tensor", e)

        @block.gpsimd
        def _(e):
            P.run("gpsimd", e)

    return nc


_NC_CACHE = None


def kernel(L0_attn_x, L0_attn_Wq, L0_attn_Wk, L0_attn_Wv, L0_attn_Wo,
           L0_attn_bo, L0_attn_alpha, L0_attn_temperature, L0_attn_gamma,
           L0_attn_beta, L0_ffn_w_gate, L0_ffn_w_up, L0_ffn_w_down,
           L0_c, L0_wmod):
    global _NC_CACHE
    x = np.ascontiguousarray(np.asarray(L0_attn_x, dtype=np.float32))
    B, N, Dm = x.shape
    if _NC_CACHE is None:
        _NC_CACHE = build_nc()
    nc = _NC_CACHE

    eye = np.eye(128, dtype=np.float32).astype(BF)
    eyf = np.concatenate([np.eye(64, dtype=np.float32)] * 2, axis=0)
    hsel = np.zeros((128, 2), np.float32)
    hsel[0:64, 0] = 1.0
    hsel[64:128, 1] = 1.0
    hsel = hsel.astype(BF)
    g2sel = np.zeros((2, 128), np.float32)
    g2sel[0, 0:64] = 1.0
    g2sel[1, 64:128] = 1.0
    g2sel = g2sel.astype(BF)

    shared = {
        "wq": np.ascontiguousarray(L0_attn_Wq, np.float32),
        "wk": np.ascontiguousarray(L0_attn_Wk, np.float32),
        "wv": np.ascontiguousarray(L0_attn_Wv, np.float32),
        "wo": np.ascontiguousarray(L0_attn_Wo, np.float32),
        "bo": np.ascontiguousarray(np.asarray(L0_attn_bo, np.float32).reshape(1, Dm)),
        "alpha": np.ascontiguousarray(np.asarray(L0_attn_alpha, np.float32).reshape(1, 1)),
        "temp": np.ascontiguousarray(np.asarray(L0_attn_temperature, np.float32).reshape(1, 1)),
        "gamma": np.ascontiguousarray(np.asarray(L0_attn_gamma, np.float32).reshape(1, Dm)),
        "beta": np.ascontiguousarray(np.asarray(L0_attn_beta, np.float32).reshape(1, Dm)),
        "wg": np.ascontiguousarray(L0_ffn_w_gate, np.float32),
        "wu": np.ascontiguousarray(L0_ffn_w_up, np.float32),
        "wd": np.ascontiguousarray(L0_ffn_w_down, np.float32),
        "wmod": np.ascontiguousarray(L0_wmod, np.float32),
        "eye": eye,
        "eyf": eyf,
        "hsel": hsel,
        "g2sel": g2sel,
    }
    carr = np.ascontiguousarray(np.asarray(L0_c, np.float32))

    in_maps = []
    for core in range(8):
        b = core // 2
        half = core % 2
        xb = x[b]
        xr = np.concatenate([xb[half * TOWN:(half + 1) * TOWN],
                             xb[(1 - half) * TOWN:(2 - half) * TOWN]], axis=0)
        m = dict(shared)
        m["x"] = np.ascontiguousarray(xr)
        m["cvec"] = np.ascontiguousarray(carr[b].reshape(1, Dm))
        in_maps.append(m)

    res = run_bass_kernel_spmd(nc, in_maps, core_ids=list(range(8)))
    out = np.empty((B, N, Dm), dtype=np.float32)
    for core in range(8):
        b = core // 2
        half = core % 2
        out[b, half * TOWN:(half + 1) * TOWN] = res.results[core]["out"]
    return out


# version 18
# speedup vs baseline: 1.0097x; 1.0097x over previous
from contextlib import ExitStack

import numpy as np
import ml_dtypes
import concourse.bass as bass
import concourse.mybir as mybir
from concourse.bass_utils import run_bass_kernel_spmd

F32 = mybir.dt.float32
BF16 = mybir.dt.bfloat16
ALU = mybir.AluOpType
ACT = mybir.ActivationFunctionType
AXX = mybir.AxisListType
BF = ml_dtypes.bfloat16

D = 1024
J = 8
TFULL = 8192
TOWN = 4096
NTF = 64
NTO = 32
NST = 8
HP = 8
INNER = 4096
NI = 32
NTG = 16
TGW = 256
NS_ITERS = 18

C23 = float(np.float32(1.5 * 2 ** 23))
C52 = float(np.float32(1.5 * 2 ** 52))
EPS_LN = 1e-5


class Prog:
    ENGS = ("sync", "vector", "scalar", "tensor", "gpsimd")
    POOL = {"sa": 8, "sg": 8, "sq": 4, "sw": 8, "sx": 4}

    def __init__(self):
        self.q = {e: [] for e in self.ENGS}
        self.c = {"v": 0, "s": 0, "t": 0, "g": 0}
        for qn, n in self.POOL.items():
            for i in range(n):
                self.c[f"{qn}{i}"] = 0
        self.rr = {qn: 0 for qn in self.POOL}
        self.sem = {}
        self.esem = {"vector": "v", "scalar": "s", "tensor": "t", "gpsimd": "g"}
        self.lastw = {}

    def emit(self, eng, fn):
        self.q[eng].append(fn)

    def wait(self, eng, sem, val):
        if val is None or val <= 0:
            return
        key = (eng, sem)
        if self.lastw.get(key, -1) >= val:
            return
        self.lastw[key] = val
        s = self.sem[sem]
        self.emit(eng, lambda e: e.wait_ge(s, val))

    def dwait(self, eng, tok):
        if tok is None or tok == 0:
            return
        self.qbarrier(eng, tok[0][:2])

    def dwait_tok(self, eng, tok):
        if tok is None or tok == 0:
            return
        self.wait(eng, tok[0], tok[1])

    def dma(self, eng, queue, out, in_, **kw):
        i = self.rr[queue]
        self.rr[queue] = (i + 1) % self.POOL[queue]
        sn = f"{queue}{i}"
        self.wait(eng, sn, self.c[sn])
        s = self.sem[sn]
        self.emit(eng, lambda e: e.dma_start(out=out, in_=in_, **kw).then_inc(s, 16))
        self.c[sn] += 16
        return (sn, self.c[sn])

    def dmat(self, out, in_):
        i = self.rr["sq"]
        self.rr["sq"] = (i + 1) % self.POOL["sq"]
        sn = f"sq{i}"
        self.wait("sync", sn, self.c[sn])
        s = self.sem[sn]
        self.emit("sync", lambda e: e.dma_start_transpose(out, in_).then_inc(s, 16))
        self.c[sn] += 16
        return (sn, self.c[sn])

    def qbarrier(self, eng, queue):
        for i in range(self.POOL[queue]):
            self.wait(eng, f"{queue}{i}", self.c[f"{queue}{i}"])

    def op(self, eng, fn, inc=True, selfwait=True):
        sk = self.esem[eng]
        if selfwait and eng != "tensor":
            self.wait(eng, sk, self.c[sk])
        if inc:
            s = self.sem[sk]
            self.emit(eng, lambda e: fn(e).then_inc(s, 1))
            self.c[sk] += 1
            return self.c[sk]
        self.emit(eng, lambda e: fn(e))
        return None

    def vgap(self):
        pass

    def run(self, eng, e):
        for fn in self.q[eng]:
            fn(e)


def build_nc():
    nc = bass.Bass("TRN2", target_bir_lowering=False, debug=False)

    x_h = nc.dram_tensor("x", [TFULL, D], F32, kind="ExternalInput").ap()
    wq_h = nc.dram_tensor("wq", [D, D], F32, kind="ExternalInput").ap()
    wk_h = nc.dram_tensor("wk", [D, D], F32, kind="ExternalInput").ap()
    wv_h = nc.dram_tensor("wv", [D, D], F32, kind="ExternalInput").ap()
    wo_h = nc.dram_tensor("wo", [D, D], F32, kind="ExternalInput").ap()
    bo_h = nc.dram_tensor("bo", [1, D], F32, kind="ExternalInput").ap()
    alpha_h = nc.dram_tensor("alpha", [1, 1], F32, kind="ExternalInput").ap()
    temp_h = nc.dram_tensor("temp", [1, 1], F32, kind="ExternalInput").ap()
    gamma_h = nc.dram_tensor("gamma", [1, D], F32, kind="ExternalInput").ap()
    beta_h = nc.dram_tensor("beta", [1, D], F32, kind="ExternalInput").ap()
    wg_h = nc.dram_tensor("wg", [INNER, D], F32, kind="ExternalInput").ap()
    wu_h = nc.dram_tensor("wu", [INNER, D], F32, kind="ExternalInput").ap()
    wd_h = nc.dram_tensor("wd", [D, INNER], F32, kind="ExternalInput").ap()
    cvec_h = nc.dram_tensor("cvec", [1, D], F32, kind="ExternalInput").ap()
    wmod_h = nc.dram_tensor("wmod", [D, 6 * D], F32, kind="ExternalInput").ap()
    eye_h = nc.dram_tensor("eye", [128, 128], BF16, kind="ExternalInput").ap()
    eyf_h = nc.dram_tensor("eyf", [128, 64], F32, kind="ExternalInput").ap()
    hsel_h = nc.dram_tensor("hsel", [128, 2], BF16, kind="ExternalInput").ap()
    g2sel_h = nc.dram_tensor("g2sel", [2, 128], BF16, kind="ExternalInput").ap()

    out_h = nc.dram_tensor("out", [TOWN, D], F32, kind="ExternalOutput").ap()

    x2_hbm = nc.dram_tensor("x2s", [TOWN, D], F32).ap()
    h2q_hbm = nc.dram_tensor("h2qs", [NST, 128, NI, 128], BF16).ap()
    yq_hbm = nc.dram_tensor("yqs", [NTG, 128, NI, TGW], BF16).ap()
    wdt_hbm = nc.dram_tensor("wdts", [NI, 128, D], BF16).ap()
    wgq_hbm = nc.dram_tensor("wgq", [2, NI, 128, D], BF16).ap()

    P = Prog()

    with ExitStack() as stack:
        def sb(name, shape, dt):
            return stack.enter_context(nc.sbuf_tensor(name, shape, dt))

        HF = sb("HF", [128, J, TOWN], BF16)
        WAT = sb("WAT", [128, J, 4, J, 128], BF16)
        YTG = sb("YTG", [128, NI, TGW], BF16)
        U = sb("U", [128, 24576], BF16)
        SCR = sb("SCR", [128, D], F32)
        STAT = sb("STAT", [128, 16], F32)
        EYE = sb("EYE", [128, 128], BF16)
        EYF = sb("EYF", [128, 64], F32)
        ONE1 = sb("ONE1", [1, 128], BF16)
        N1R = sb("N1R", [1, 512], BF16)
        ONEC = sb("ONEC", [128, 1], BF16)
        CPX = sb("CPX", [128, 1], F32)
        DDT = sb("DDT", [128, 4], F32)
        CA1 = sb("CA1", [128, J], F32)
        SAF = sb("SAF", [128, J], F32)
        BOF = sb("BOF", [128, J], F32)
        GBF = sb("GBF", [128, J], F32)
        CB1BC = sb("CB1BC", [128, D], F32)
        SBBC = sb("SBBC", [128, D], F32)
        ALP = sb("ALP", [128, 1], F32)
        ITP = sb("ITP", [128, 1], F32)
        CSF = sb("CSF", [128, J], BF16)
        MAPB = sb("MAPB", [128, HP, 64], BF16)
        BST = sb("BST", [128, 8], BF16)
        NSS = sb("NSS", [128, 32], F32)
        HSEL = sb("HSEL", [128, 2], BF16)
        G2SEL = sb("G2SEL", [2, 128], BF16)
        SYR = sb("SYR", [128, NTG * 2], F32)
        SYCOL = sb("SYCOL", [128, 8], F32)
        SGP = sb("SGP", [128, 4], F32)
        SGR = sb("SGR", [128, 4], F32)
        G1F0 = sb("G1F0", [128, 16], F32)
        PS = stack.enter_context(nc.psum_tensor("PS", [128, 8, 512], F32))
        for k in list(P.c.keys()):
            P.sem[k] = stack.enter_context(nc.semaphore(k))
        block = stack.enter_context(nc.Block())

        Uf = U.bitcast(F32)
        XNB = U[:, 0:2048].rearrange("p (b c) -> p b c", b=2)
        HTMP = U[:, 2048:4096].rearrange("p (b j c) -> p b j c", b=2, j=J)
        KTM = U[:, 4096:5120]
        VTM = U[:, 5120:6144]
        XT = Uf[:, 3072:5120].rearrange("p (b c) -> p b c", b=2)
        XT3 = [Uf[:, 3072:4096], Uf[:, 4096:5120], Uf[:, 11264:12288]]
        KKS = Uf[:, 0:1024].rearrange("p (h c) -> p h c", h=HP)
        ABLK = U[:, 2048:3072].rearrange("p (h c) -> p h c", h=HP)
        XNS = U[:, 3072:3584].rearrange("p (h c) -> p h c", h=HP)
        ZNS = U[:, 3584:4096].rearrange("p (h c) -> p h c", h=HP)
        KTVB = U[:, 4096:4608].rearrange("p (h c) -> p h c", h=HP)
        SMB = U[:, 4608:5120].rearrange("p (h c) -> p h c", h=HP)
        X3T = Uf[:, 0:2048].rearrange("p (b c) -> p b c", b=2)
        ST1 = Uf[0:1, 5120:6656]
        H2T1 = U[:, 13312:14336]
        X2T = Uf[:, 7168:9216].rearrange("p (b c) -> p b c", b=2)
        H2FM = U[:, 18432:22528].rearrange("p (i c) -> p i c", i=NI)
        SQB = U[:, 4096:5120].rearrange("p (b c) -> p b c", b=2)
        H2QT = U[:, 0:2048].rearrange("p (t k c) -> p t k c", t=2, k=J)
        H2QT2 = U[:, 2048:4096].rearrange("p (t k c) -> p t k c", t=2, k=J)
        WFNAT = Uf[:, 2048:4096].rearrange("p (b c) -> p b c", b=2)
        WFQ = U[:, 8192:9216]
        GS = U[:, 9216:9216 + TGW]
        UPS = U[:, 9472:9472 + TGW]
        MXT = U[:, 9728:9728 + TGW]
        MNT = U[:, 10752:10752 + TGW]
        SYBS = U[:, 9984:9984 + TGW]
        ZQ = U[:, 10240:10752].rearrange("p (b c) -> p b c", b=2)
        GSB = U[:, 23568:23824]
        UPSB = U[:, 23824:24080]
        YQB = U[:, 10496:11008].rearrange("p (b c) -> p b c", b=2)
        SYKT = U[:, 11008:11016]
        WDQ = U[:, 11280:15376]
        WDNAT = Uf[:, 7688:9736]
        WDDEST = U[:, 19472:23568].rearrange("p (i c) -> p i c", i=NI)
        WDTS = U[:, 4096:6144].rearrange("p (b c) -> p b c", b=2)
        WDTS8 = U[:, 4096:12288].rearrange("p (b c) -> p b c", b=8)
        YQI = U[:, 6144:6656].rearrange("p (b c) -> p b c", b=2)
        ABSS = Uf[:, 9216:9312]
        PREPROW = U[:, 8192:9216][0:1, 0:1024]
        EYFB = U[:, 5120:5184]
        ROWS = STAT[0:2, 0:8]
        ROWSB = BST[0:2, 0:8]
        ST1B = U[:, 5120:6144][0:1, 0:1024]
        SYROW = U[:, 11016:11272][0:1, 0:256]
        G1R = U[:, 22528:23552][0:1, 0:1024]
        G0R = U[:, 23552:24576][0:1, 0:1024]

        QF = WAT[:, :, 1, 0:4, :].rearrange("p k r c -> p k (r c)")
        OF = WAT[:, :, 1, 4:8, :].rearrange("p k r c -> p k (r c)")
        ZBB = WAT[:, :, 2, 0:4, :].rearrange("p k r c -> p k (r c)")
        AOTM = WAT[:, :, 2, 4:8, :]
        AOC = U[:, 6144:10240]
        EPSA = DDT[:, 0:1]
        SDIV = DDT[:, 1:2]
        HFf = HF.rearrange("p j c -> p (j c)").bitcast(F32)
        MODR = HFf[0:1, 0:11 * D]
        CROW = HFf[0:1, 11 * D:12 * D]
        YTGflat = YTG.rearrange("p a b -> p (a b)")
        WNATv = YTGflat.rearrange("p (r f) -> p r f", r=J)
        WMODSv = YTGflat[:, 0:4096].rearrange("p (k c) -> p k c", k=J)

        def bank(i, w=512):
            return PS[:, i, 0:w]

        def bankb(i):
            return PS[:, i, :].bitcast(BF16)

        def st1(k):
            return ST1[0:1, (k % 3) * 512:(k % 3) * 512 + 512]

        def st1b(k):
            return ST1B[0:1, k * 512:(k + 1) * 512]

        ld_eye = P.dma("gpsimd", "sg", EYE[:], eye_h)
        ld_eyf = P.dma("gpsimd", "sg", EYF[:], eyf_h)
        ld_c = P.dma("gpsimd", "sg", CROW, cvec_h)
        P.dma("gpsimd", "sg", MODR[0:1, 6 * D:7 * D], bo_h)
        P.dma("gpsimd", "sg", MODR[0:1, 7 * D:7 * D + 1], alpha_h)
        P.dma("gpsimd", "sg", MODR[0:1, 7 * D + 4:7 * D + 5], temp_h)
        P.dma("gpsimd", "sg", MODR[0:1, 8 * D:9 * D], gamma_h)
        ld_small = P.dma("gpsimd", "sg", MODR[0:1, 9 * D:10 * D], beta_h)
        P.dma("gpsimd", "sg", HSEL[:], hsel_h)
        ld_sel = P.dma("gpsimd", "sg", G2SEL[:], g2sel_h)

        P.op("vector", lambda e: e.memset(DDT[:, 0:1], 2.0 ** -40))
        P.vgap()
        P.op("vector", lambda e: e.tensor_single_scalar(DDT[:, 1:2], DDT[:, 0:1], 1.0, ALU.add))
        P.vgap()
        P.op("vector", lambda e: e.tensor_single_scalar(DDT[:, 2:3], DDT[:, 1:2], 1.0, ALU.subtract))
        P.vgap()
        P.op("vector", lambda e: e.tensor_single_scalar(DDT[:, 3:4], DDT[:, 2:3], 2.0 ** 40, ALU.mult))
        P.vgap()
        P.op("vector", lambda e: e.tensor_scalar(
            CPX[:], DDT[:, 3:4], C52 - C23, C23, ALU.mult, ALU.add))
        P.vgap()
        P.op("vector", lambda e: e.memset(DDT[:, 0:1], EPS_LN))
        P.op("vector", lambda e: e.memset(DDT[:, 1:2], 1.0 / D))
        P.op("vector", lambda e: e.memset(ONE1[:], 1.0))
        P.op("vector", lambda e: e.memset(N1R[:], -1.0))
        P.op("vector", lambda e: e.memset(ONEC[:], 1.0))

        sq_wat = 0
        for wi, wsrc in enumerate((wq_h, wk_h, wv_h, wo_h)):
            if wi > 0:
                P.dwait("gpsimd", sq_wat)
            g_w = P.dma("gpsimd", "sg", WNATv[:, :, :],
                        wsrc.rearrange("(r p) f -> p r f", p=128))
            P.dwait("sync", g_w)
            for r in range(J):
                sq_wat = P.dmat(WAT[:, :, wi, r, :], WNATv[:, r, :])

        P.dwait("scalar", ld_c)
        s_csil = P.op("scalar", lambda e: e.activation(CROW, CROW, ACT.Silu))
        P.wait("tensor", "s", s_csil)
        P.dwait("tensor", ld_eyf)
        for j in range(J):
            P.op("tensor", lambda e, j=j: e.matmul(
                bank(7)[:, j:j + 1], CROW[0:1, j * 128:(j + 1) * 128],
                EYF[0:1, 0:1], start=(j == 0), stop=(j == J - 1)), inc=(j == J - 1))
        t_ctr = P.c["t"]
        P.wait("vector", "t", t_ctr)
        v_csf = P.op("vector", lambda e: e.tensor_copy(CSF[:], bank(7)[:, 0:J]))

        v_modev = v_csf
        P.dwait("gpsimd", sq_wat)
        for n in range(12):
            g_wm = P.dma("gpsimd", "sg", WMODSv,
                         wmod_h.rearrange("(k p) m -> p k m", p=128)[:, :, n * 512:(n + 1) * 512])
            P.dwait("tensor", g_wm)
            P.wait("tensor", "v", v_modev)
            for k in range(J):
                P.op("tensor", lambda e, k=k: e.matmul(
                    bank(6)[0:1, :], CSF[:, k:k + 1], WMODSv[:, k, :],
                    start=(k == 0), stop=(k == J - 1)), inc=(k == J - 1))
            t_mm = P.c["t"]
            P.wait("vector", "t", t_mm)
            v_modev = P.op("vector", lambda e, n=n: e.tensor_copy(
                MODR[0:1, n * 512:(n + 1) * 512], bank(6)[0:1, :]))
            P.wait("gpsimd", "t", t_mm)

        P.wait("scalar", "v", v_modev)
        for base in (2, 5):
            row = MODR[0:1, base * D:(base + 1) * D]
            sgn = MODR[0:1, 10 * D:11 * D]
            P.wait("scalar", "v", P.c["v"])
            P.op("scalar", lambda e, row=row, sgn=sgn: e.activation(sgn, row, ACT.Sign))
            s_sg = P.op("scalar", lambda e, row=row: e.activation(PREPROW, row, ACT.Abs))
            P.wait("vector", "s", s_sg)
            P.op("vector", lambda e: e.tensor_single_scalar(PREPROW, PREPROW, 0.25, ALU.max))
            P.op("vector", lambda e, row=row, sgn=sgn: e.tensor_mul(row, PREPROW, sgn))
        P.op("vector", lambda e: e.tensor_mul(G1R[:], MODR[0:1, 2 * D:3 * D], MODR[0:1, 8 * D:9 * D]))
        v_rows = P.op("vector", lambda e: e.tensor_mul(G0R[:], MODR[0:1, 2 * D:3 * D], MODR[0:1, 9 * D:10 * D]))

        last_fme = v_rows
        for (ri, dstt, addv) in ((1, CA1, 1.0), (0, SAF, 0.0), (6, BOF, 0.0), (5, GBF, 0.0)):
            P.wait("tensor", "v", last_fme)
            for j in range(J):
                P.op("tensor", lambda e, ri=ri, j=j: e.matmul(
                    bank(7)[:, j:j + 1],
                    MODR[0:1, ri * D + j * 128: ri * D + (j + 1) * 128],
                    EYF[0:1, 0:1], start=(j == 0), stop=(j == J - 1)), inc=(j == J - 1))
            t_tr = P.c["t"]
            P.wait("vector", "t", t_tr)
            last_fme = P.op("vector", lambda e, dstt=dstt, addv=addv: e.tensor_scalar(
                dstt[:], bank(7)[:, 0:J], addv, None, ALU.add))

        P.dwait("tensor", ld_eye)
        P.wait("tensor", "v", last_fme)
        for j in range(J):
            P.op("tensor", lambda e, j=j: e.matmul(
                bank(7)[:, j:j + 1], G1R[0:1, j * 128:(j + 1) * 128], EYE[0:1, 0:1],
                start=(j == 0), stop=False), inc=False)
            P.op("tensor", lambda e, j=j: e.matmul(
                bank(7)[:, 8 + j:9 + j], G0R[0:1, j * 128:(j + 1) * 128], EYE[0:1, 0:1],
                start=False, stop=(j == J - 1)), inc=(j == J - 1))
        t_g1f = P.c["t"]
        P.wait("vector", "t", t_g1f)
        P.op("vector", lambda e: e.tensor_copy(G1F0[:], bank(7)[:, 0:16]))
        G1F = G1F0[:, 0:8]
        G0F = G1F0[:, 8:16]
        v_cbx = last_fme
        for dst, ri, addv in ((CB1BC, 4, 1.0), (SBBC, 3, 0.0)):
            for half in range(2):
                P.op("vector", lambda e, ri=ri, addv=addv, half=half: e.tensor_scalar(
                    PREPROW[0:1, 0:512], MODR[0:1, ri * D + half * 512:ri * D + (half + 1) * 512],
                    addv, None, ALU.add))
                v_cbx = P.c["v"]
                P.wait("tensor", "v", v_cbx)
                P.op("tensor", lambda e: e.matmul(
                    bank(5), ONE1[0:1, 0:128], PREPROW[0:1, 0:512], start=True, stop=True))
                t_x = P.c["t"]
                P.wait("vector", "t", t_x)
                v_cbx = P.op("vector", lambda e, dst=dst, half=half: e.tensor_copy(
                    dst[:, half * 512:(half + 1) * 512], bank(5)))

        P.dwait("vector", ld_small)
        P.op("vector", lambda e: e.reciprocal(MODR[0:1, 7 * D + 8:7 * D + 9],
                                              MODR[0:1, 7 * D + 4:7 * D + 5]))
        P.vgap()
        P.op("vector", lambda e: e.tensor_copy(PREPROW[0:1, 0:1], MODR[0:1, 7 * D:7 * D + 1]))
        P.op("vector", lambda e: e.tensor_copy(PREPROW[0:1, 1:2], MODR[0:1, 7 * D + 8:7 * D + 9]))
        v_al = P.c["v"]
        P.wait("tensor", "v", v_al)
        P.op("tensor", lambda e: e.matmul(bank(5)[:, 0:2], ONE1[0:1, 0:128],
                                          PREPROW[0:1, 0:2], start=True, stop=True))
        t_al = P.c["t"]
        P.wait("vector", "t", t_al)
        P.op("vector", lambda e: e.tensor_copy(ALP[:], bank(5)[:, 0:1]))
        v_itp = P.op("vector", lambda e: e.tensor_copy(ITP[:], bank(5)[:, 1:2]))

        P1B = [Uf[:, k * 1024:(k + 1) * 1024] for k in range(8)]

        def p1_src(u):
            if u < 64:
                return ((wg_h, wu_h)[u % 2])[(u // 2) * 128:(u // 2 + 1) * 128, :]
            mm = u - 64
            r, qq = mm // 4, mm % 4
            return wd_h[r * 128:(r + 1) * 128, qq * 1024:(qq + 1) * 1024]

        def p1_col(u):
            if u < 64:
                return (u % 2) * 32 + u // 2
            return 64 + (u - 64)

        P1RING = ["sg", "sw"]
        p1_tok = {}
        s_abs_hist = {}
        P.wait("gpsimd", "t", P.c["t"])
        P.wait("gpsimd", "v", P.c["v"])
        for u0 in range(8):
            p1_tok[u0] = P.dma("gpsimd", P1RING[u0 % 2], P1B[u0][:], p1_src(u0))
        for u in range(96):
            P.dwait_tok("vector", p1_tok[u])
            na = p1_col(u)
            s_abs = P.op("vector", lambda e, u=u, na=na: e.reduce_sum(
                ABSS[:, na:na + 1], P1B[u % 8][:], AXX.X, apply_absolute_value=True))
            s_abs_hist[u] = s_abs
            if u + 8 < 96:
                P.wait("gpsimd", "v", s_abs)
                P.dwait_tok("gpsimd", p1_tok[u])
                p1_tok[u + 8] = P.dma("gpsimd", P1RING[(u + 8) % 2], P1B[u % 8][:],
                                      p1_src(u + 8))
        s_abs = s_abs_hist[95]
        for mi, (a, b) in enumerate(((0, 32), (32, 64), (64, 96))):
            P.op("vector", lambda e, a=a, b=b, mi=mi: e.reduce_sum(
                STAT[:, mi:mi + 1], ABSS[:, a:b], AXX.X))
        P.vgap()
        v_ab = P.op("vector", lambda e: e.tensor_copy(BST[:, 0:3], STAT[:, 0:3]))
        P.wait("tensor", "v", v_ab)
        P.op("tensor", lambda e: e.matmul(bank(5)[0:1, 0:3], ONEC[:], BST[:, 0:3],
                                          start=True, stop=True))
        t_ab = P.c["t"]
        P.wait("vector", "t", t_ab)
        P.op("vector", lambda e: e.tensor_scalar(
            st1(0)[0:1, 0:3], bank(5)[0:1, 0:3], 1.0 / (INNER * D), 1e-5,
            ALU.mult, ALU.max))
        P.vgap()
        P.op("vector", lambda e: e.reciprocal(st1(1)[0:1, 0:3], st1(0)[0:1, 0:3]))
        P.vgap()
        P.op("vector", lambda e: e.tensor_copy(ROWSB[0:1, 0:3], st1(1)[0:1, 0:3]))
        v_sc = P.op("vector", lambda e: e.tensor_copy(ROWSB[0:1, 3:6], st1(0)[0:1, 0:3]))
        P.wait("tensor", "v", v_sc)
        P.op("tensor", lambda e: e.matmul(bank(5)[:, 0:6], ONE1[0:1, 0:128],
                                          ROWSB[0:1, 0:6], start=True, stop=True))
        t_sc = P.c["t"]
        P.wait("vector", "t", t_sc)
        P.op("vector", lambda e: e.tensor_copy(SGP[:, 0:3], bank(5)[:, 0:3]))
        v_sgp = P.op("vector", lambda e: e.tensor_copy(SGR[:, 0:3], bank(5)[:, 3:6]))

        def kk_bank(hp):
            b = 6 + hp // 4
            off = (hp % 4) * 128
            return PS[:, b, off:off + 128]

        def stat_a(pbuf, a, b):
            return STAT[:, pbuf * 8 + a:pbuf * 8 + b]

        sq_hist = {}
        v_kv_prev = None
        t_kk_hist = {}
        v_xn_hist = {}

        v_prep_end = P.c["v"]
        s_prep_end = P.c["s"]
        t_prep_end = P.c["t"]

        sq_hist = {}
        v_mod_hist = {}
        ev_v = {}
        ev_s = {}
        t_kk_hist = {}
        v_xn_hist = {}

        def a_prep(ft):
            pbuf = ft % 2
            xb3 = XT3[ft % 3]
            own = ft < NTO
            if ft >= 2:
                P.wait("scalar", "v", v_xn_hist.get(ft - 2, 0))
            P.dwait_tok("scalar", ax_tok[ft])
            P.op("scalar", lambda e, xb3=xb3, pbuf=pbuf: e.activation(
                SCR[:], xb3[:], ACT.Square, accum_out=stat_a(pbuf, 0, 1)))
            s_st = P.op("scalar", lambda e, xb3=xb3, pbuf=pbuf: e.activation(
                SCR[:], xb3[:], ACT.Identity, accum_out=stat_a(pbuf, 1, 2)))
            P.wait("vector", "s", s_st)
            P.op("vector", lambda e, pbuf=pbuf: e.tensor_single_scalar(
                stat_a(pbuf, 2, 3), stat_a(pbuf, 1, 2), 1.0 / D, ALU.mult))
            P.op("vector", lambda e, pbuf=pbuf: e.tensor_single_scalar(
                stat_a(pbuf, 3, 4), stat_a(pbuf, 0, 1), 1.0 / D, ALU.mult))
            P.op("vector", lambda e, pbuf=pbuf: e.tensor_mul(
                stat_a(pbuf, 4, 5), stat_a(pbuf, 2, 3), stat_a(pbuf, 2, 3)))
            v_var = P.op("vector", lambda e, pbuf=pbuf: e.tensor_sub(
                stat_a(pbuf, 5, 6), stat_a(pbuf, 3, 4), stat_a(pbuf, 4, 5)))
            P.wait("scalar", "v", v_var)
            s_sd = P.op("scalar", lambda e, pbuf=pbuf: e.activation(
                stat_a(pbuf, 6, 7), stat_a(pbuf, 5, 6), ACT.Sqrt, bias=EPSA))
            P.wait("vector", "s", s_sd)
            P.op("vector", lambda e, pbuf=pbuf: e.reciprocal(
                stat_a(pbuf, 7, 8), stat_a(pbuf, 6, 7)))
            if ft >= 2:
                P.dwait("vector", sq_hist.get(ft - 2, 0))
            v_xn = P.op("vector", lambda e, xb3=xb3, pbuf=pbuf: e.tensor_scalar(
                XNB[:, pbuf, :], xb3[:], stat_a(pbuf, 2, 3), stat_a(pbuf, 7, 8),
                ALU.subtract, ALU.mult))
            v_xn_hist[ft] = v_xn
            P.wait("sync", "v", v_xn)
            if own:
                dst3 = HF.rearrange("p j (t c) -> p j t c", c=128)[:, :, ft, :]
            else:
                if ft >= NTO + 2:
                    P.wait("sync", "t", t_kv_hist[ft - 2])
                dst3 = HTMP[:, pbuf, :, :]
            sq_t = P.dmat(dst3, XNB[:, pbuf, :])
            sq_hist[ft] = sq_t
            P.dwait("scalar", sq_t)
            for j in range(J):
                P.op("scalar", lambda e, dst3=dst3, j=j: e.activation(
                    dst3[:, j, :], dst3[:, j, :], ACT.Identity,
                    bias=SAF[:, j:j + 1], scale=CA1[:, j:j + 1]),
                    inc=(j == J - 1))
            v_mod_hist[ft] = P.c["s"]
            return dst3

        WQNA = [Uf[:, 5120 + k * 1024:5120 + (k + 1) * 1024] for k in range(4)]
        WFQA = [U[:, 18432:19456], U[:, 19456:20480]]
        WGSTG = [U[:, 20480:21504], U[:, 21504:22528]]
        wqa_tok = {}
        sqa_tok = {}
        sta_tok = {}
        vfin_hist = {}

        def wqa_src(m):
            if m < 64:
                return ((wg_h, wu_h)[m % 2])[(m // 2) * 128:(m // 2 + 1) * 128, :]
            mm = m - 64
            r, qq = mm // 4, mm % 4
            return wd_h[r * 128:(r + 1) * 128, qq * 1024:(qq + 1) * 1024]

        def emit_wq_unit(m):
            buf = WQNA[m % 4]
            P.dwait_tok("vector", wqa_tok[m])
            if m < 64:
                sp = SGP[:, m % 2:m % 2 + 1]
                sr = SGR[:, m % 2:m % 2 + 1]
            else:
                sp = SGP[:, 2:3]
                sr = SGR[:, 2:3]
            P.op("vector", lambda e, buf=buf, sp=sp: e.tensor_scalar(
                buf[:], buf[:], sp, CPX[:, 0:1], ALU.mult, ALU.add))
            P.op("vector", lambda e, buf=buf: e.tensor_scalar(
                buf[:], buf[:], CPX[:, 0:1], 1.0, ALU.subtract, ALU.min))
            if m >= 64:
                r = (m - 64) // 4
                P.op("vector", lambda e, buf=buf, r=r: e.tensor_scalar(
                    buf[:], buf[:], -1.0, GBF[:, r:r + 1], ALU.max, ALU.mult))
            if m >= 2:
                P.dwait_tok("vector", sqa_tok[m - 2])
            if m >= 64:
                vfin = P.op("vector", lambda e, m=m, buf=buf, sr=sr: e.tensor_scalar(
                    WFQA[m % 2][:], buf[:], sr, None, ALU.mult))
            else:
                vfin = P.op("vector", lambda e, m=m, buf=buf, sr=sr: e.tensor_scalar(
                    WFQA[m % 2][:], buf[:], -1.0, sr, ALU.max, ALU.mult))
            vfin_hist[m] = vfin
            if m + 4 < 96:
                P.wait("gpsimd", "v", vfin)
                P.dwait_tok("gpsimd", wqa_tok[m])
                wqa_tok[m + 4] = P.dma("gpsimd", ["sg", "sw"][m % 2],
                                       WQNA[m % 4][:], wqa_src(m + 4))
            P.wait("sync", "v", vfin)
            if m >= 2:
                P.dwait_tok("sync", sta_tok[m - 2])
            sqa_tok[m] = P.dmat(
                WGSTG[m % 2].rearrange("p (a c) -> p a c", c=128), WFQA[m % 2][:])
            P.dwait_tok("sync", sqa_tok[m])
            if m < 64:
                sta_tok[m] = P.dma("sync", "sx", wgq_hbm[m % 2, m // 2],
                                   WGSTG[m % 2][:])
            else:
                r, qq = (m - 64) // 4, (m - 64) % 4
                sta_tok[m] = P.dma(
                    "sync", "sx",
                    wdt_hbm[8 * qq:8 * (qq + 1), :, r * 128:(r + 1) * 128].rearrange(
                        "i p c -> p i c"),
                    WGSTG[m % 2].rearrange("p (a c) -> p a c", c=128))

        P.wait("gpsimd", "v", v_sgp)
        for uu in range(88, 96):
            P.dwait_tok("gpsimd", p1_tok[uu])
        for m0 in range(4):
            wqa_tok[m0] = P.dma("gpsimd", ["sg", "sw"][m0 % 2],
                                WQNA[m0 % 4][:], wqa_src(m0))
        wu_i = 0

        t_kv_hist = {}
        dst3s = {}
        ax_tok = {}
        P.wait("scalar", "v", max(v_prep_end, v_sgp))
        P.wait("scalar", "t", P.c["t"])
        for f0 in range(3):
            ax_tok[f0] = P.dma("scalar", "sa", XT3[f0][:],
                               x_h[f0 * 128:(f0 + 1) * 128, :])
        dst3s[0] = a_prep(0)
        dst3s[1] = a_prep(1)
        for ft in range(NTF):
            if ft + 3 < NTF:
                P.wait("scalar", "v", v_xn_hist.get(ft, 0))
                ax_tok[ft + 3] = P.dma(
                    "scalar", "sa", XT3[(ft + 3) % 3][:],
                    x_h[(ft + 3) * 128:(ft + 4) * 128, :])
            dst3 = dst3s.pop(ft)
            P.wait("tensor", "s", v_mod_hist[ft])
            if ft >= 1:
                P.wait("tensor", "s", ev_s.get(ft - 1, 0))
            for half in range(2):
                kb = bank(0 if half == 0 else 2)
                vb = bank(1 if half == 0 else 3)
                for k in range(J):
                    P.op("tensor", lambda e, k=k, half=half, dst3=dst3, kb=kb: e.matmul(
                        kb, dst3[:, k, :], WAT[:, k, 1, half * 4:(half + 1) * 4, :],
                        start=(k == 0), stop=(k == J - 1)), inc=False)
                    P.op("tensor", lambda e, k=k, half=half, dst3=dst3, vb=vb: e.matmul(
                        vb, dst3[:, k, :], WAT[:, k, 2, half * 4:(half + 1) * 4, :],
                        start=(k == 0), stop=(k == J - 1)), inc=(k == J - 1))
            t_kv = P.c["t"]
            t_kv_hist[ft] = t_kv
            P.wait("scalar", "t", t_kv)
            P.op("scalar", lambda e: e.copy(KTM[:, 0:512], bank(0)))
            ev_s1 = P.op("scalar", lambda e: e.copy(VTM[:, 0:512], bank(1)))
            P.op("scalar", lambda e: e.copy(KTM[:, 512:1024], bank(2)))
            ev_s[ft] = P.op("scalar", lambda e: e.copy(VTM[:, 512:1024], bank(3)))
            ev_v[ft] = ev_s[ft]
            ev_v1 = ev_s1
            if ft + 2 < NTF:
                dst3s[ft + 2] = a_prep(ft + 2)
            st_flag = (ft == 0)
            sp_flag = (ft == NTF - 1)
            for hp in range(HP):
                if hp == 0:
                    P.wait("tensor", "s", ev_s1)
                elif hp == 4:
                    P.wait("tensor", "s", ev_s[ft])
                kkb = kk_bank(hp)
                first = st_flag and hp % 4 == 0
                last = sp_flag and hp % 4 == 3
                P.op("tensor", lambda e, hp=hp, kkb=kkb, first=first: e.matmul(
                    kkb[0:64, 0:64], KTM[:, hp * 128:hp * 128 + 64],
                    KTM[:, hp * 128:hp * 128 + 64],
                    start=first, stop=False, tile_position=(0, 0)), inc=False)
                P.op("tensor", lambda e, hp=hp, kkb=kkb, last=last: e.matmul(
                    kkb[0:64, 64:128], KTM[:, hp * 128:hp * 128 + 64],
                    VTM[:, hp * 128:hp * 128 + 64],
                    start=False, stop=last, tile_position=(0, 0)), inc=False)
                P.op("tensor", lambda e, hp=hp, kkb=kkb, first=first: e.matmul(
                    kkb[64:128, 0:64], KTM[:, hp * 128 + 64:hp * 128 + 128],
                    KTM[:, hp * 128 + 64:hp * 128 + 128],
                    start=first, stop=False, tile_position=(0, 64)), inc=False)
                P.op("tensor", lambda e, hp=hp, kkb=kkb, last=last: e.matmul(
                    kkb[64:128, 64:128], KTM[:, hp * 128 + 64:hp * 128 + 128],
                    VTM[:, hp * 128 + 64:hp * 128 + 128],
                    start=False, stop=last, tile_position=(0, 64)),
                    inc=(hp == HP - 1))
            t_kk_hist[ft] = P.c["t"]
            while wu_i < ((ft + 1) * 96) // 64:
                emit_wq_unit(wu_i)
                wu_i += 1

        t_kkdone = P.c["t"]

        def nss(i, a, b):
            return NSS[:, i * 8 + a:i * 8 + b]

        P.wait("vector", "t", t_kkdone)
        P.op("vector", lambda e: e.tensor_copy(EYFB[:], EYF[:]))
        for hp in range(HP):
            P.op("vector", lambda e, hp=hp: e.scalar_tensor_tensor(
                KKS[:, hp, 0:64], EYF[:], ALP[:, 0:1], kk_bank(hp)[:, 0:64],
                ALU.mult, ALU.add))
            P.op("vector", lambda e, hp=hp: e.tensor_copy(
                KKS[:, hp, 64:128], kk_bank(hp)[:, 64:128]))
        for hp in range(HP):
            P.op("vector", lambda e, hp=hp: e.tensor_copy(KTVB[:, hp, :], KKS[:, hp, 64:128]))
        for hp in range(HP):
            P.op("vector", lambda e, hp=hp: e.tensor_mul(
                SCR[:, hp * 64:(hp + 1) * 64], KKS[:, hp, 0:64], KKS[:, hp, 0:64]))
        P.vgap()
        for hp in range(HP):
            P.op("vector", lambda e, hp=hp: e.reduce_sum(
                nss(0, hp, hp + 1), SCR[:, hp * 64:(hp + 1) * 64], AXX.X))
        P.vgap()
        v_fro = P.op("vector", lambda e: e.tensor_copy(BST[:, 0:8], nss(0, 0, 8)))
        P.wait("tensor", "v", v_fro)
        P.dwait("tensor", ld_sel)
        P.op("tensor", lambda e: e.matmul(bank(5)[0:2, 0:8], HSEL[:], BST[:, 0:8],
                                          start=True, stop=True))
        t_fro = P.c["t"]
        P.wait("vector", "t", t_fro)
        P.op("vector", lambda e: e.reciprocal(ROWS[:, 0:8], bank(5)[0:2, 0:8]))
        P.vgap()
        v_fr2 = P.op("vector", lambda e: e.tensor_copy(ROWSB[:, 0:8], ROWS[:, 0:8]))
        P.wait("tensor", "v", v_fr2)
        P.op("tensor", lambda e: e.matmul(bank(5)[:, 0:8], G2SEL[:], ROWSB[:, 0:8],
                                          start=True, stop=True))
        t_a0 = P.c["t"]
        P.wait("vector", "t", t_a0)
        P.op("vector", lambda e: e.tensor_copy(nss(1, 0, 8), bank(5)[:, 0:8]))
        P.vgap()
        for hp in range(HP):
            P.op("vector", lambda e, hp=hp: e.tensor_scalar(
                XNS[:, hp, :], KKS[:, hp, 0:64], nss(1, hp, hp + 1), None, ALU.mult))
        for hp in range(HP):
            P.op("vector", lambda e, hp=hp: e.memset(ABLK[:, hp, :], 0.0))
        for hp in range(HP):
            P.op("vector", lambda e, hp=hp: e.tensor_copy(
                ABLK[0:64, hp, 0:64], KKS[0:64, hp, 0:64]))
            P.op("vector", lambda e, hp=hp: e.tensor_copy(
                ABLK[64:128, hp, 64:128], KKS[64:128, hp, 0:64]))

        for it in range(NS_ITERS):
            P.wait("tensor", "v", P.c["v"])
            for hp in range(HP):
                P.op("tensor", lambda e, hp=hp: e.matmul(
                    bank(4)[:, hp * 64:(hp + 1) * 64], ABLK[:, hp, :], XNS[:, hp, :],
                    start=(hp == 0), stop=(hp == HP - 1)), inc=(hp == HP - 1))
            t_y = P.c["t"]
            P.wait("vector", "t", t_y)
            for hp in range(HP):
                P.op("vector", lambda e, hp=hp: e.scalar_tensor_tensor(
                    ZNS[:, hp, :], EYFB[:], 2.0, bank(4)[:, hp * 64:(hp + 1) * 64],
                    ALU.mult, ALU.subtract))
            v_z = P.c["v"]
            P.wait("tensor", "v", v_z)
            for hp in range(HP):
                P.op("tensor", lambda e, hp=hp: e.matmul(
                    bank(5)[0:64, hp * 64:(hp + 1) * 64], XNS[0:64, hp, :],
                    ZNS[0:64, hp, :], start=(hp == 0), stop=(hp == HP - 1),
                    tile_position=(0, 0)), inc=False)
                P.op("tensor", lambda e, hp=hp: e.matmul(
                    bank(5)[64:128, hp * 64:(hp + 1) * 64], XNS[64:128, hp, :],
                    ZNS[64:128, hp, :], start=(hp == 0), stop=(hp == HP - 1),
                    tile_position=(64, 64)), inc=(hp == HP - 1))
            t_x2i = P.c["t"]
            P.wait("vector", "t", t_x2i)
            for hp in range(HP):
                P.op("vector", lambda e, hp=hp: e.tensor_copy(
                    XNS[:, hp, :], bank(5)[:, hp * 64:(hp + 1) * 64]))

        P.wait("tensor", "v", P.c["v"])
        for hp in range(HP):
            P.op("tensor", lambda e, hp=hp: e.matmul(
                bank(4)[0:64, hp * 64:(hp + 1) * 64], KTVB[0:64, hp, :],
                XNS[0:64, hp, :], start=(hp == 0), stop=(hp == HP - 1),
                tile_position=(0, 0)), inc=False)
            P.op("tensor", lambda e, hp=hp: e.matmul(
                bank(4)[64:128, hp * 64:(hp + 1) * 64], KTVB[64:128, hp, :],
                XNS[64:128, hp, :], start=(hp == 0), stop=(hp == HP - 1),
                tile_position=(64, 64)), inc=(hp == HP - 1))
        t_zt = P.c["t"]
        P.wait("vector", "t", t_zt)
        P.wait("vector", "v", v_itp)
        for hp in range(HP):
            P.op("vector", lambda e, hp=hp: e.tensor_scalar(
                SCR[:, hp * 64:(hp + 1) * 64], bank(4)[:, hp * 64:(hp + 1) * 64],
                ITP[:, 0:1], None, ALU.mult))
        P.vgap()
        for hp in range(HP):
            P.op("vector", lambda e, hp=hp: e.reduce_max(
                nss(1, hp, hp + 1), SCR[:, hp * 64:(hp + 1) * 64], AXX.X))
        P.vgap()
        v_mx = P.op("vector", lambda e: e.tensor_single_scalar(
            nss(0, 0, 8), nss(1, 0, 8), -1.0, ALU.mult))
        P.wait("scalar", "v", v_mx)
        for hp in range(HP):
            P.op("scalar", lambda e, hp=hp: e.activation(
                SCR[:, hp * 64:(hp + 1) * 64], SCR[:, hp * 64:(hp + 1) * 64],
                ACT.Exp, bias=nss(0, hp, hp + 1)))
        s_exp = P.c["s"]
        P.wait("vector", "s", s_exp)
        for hp in range(HP):
            P.op("vector", lambda e, hp=hp: e.reduce_sum(
                nss(2, hp, hp + 1), SCR[:, hp * 64:(hp + 1) * 64], AXX.X))
        P.vgap()
        P.op("vector", lambda e: e.reciprocal(nss(3, 0, 8), nss(2, 0, 8)))
        P.vgap()
        for hp in range(HP):
            P.op("vector", lambda e, hp=hp: e.tensor_scalar(
                SMB[:, hp, :], SCR[:, hp * 64:(hp + 1) * 64], nss(3, hp, hp + 1),
                None, ALU.mult))
        v_sm = P.c["v"]
        P.wait("tensor", "v", v_sm)
        v_mev = {}
        for hp in range(HP):
            bb = 4 + hp % 2
            if hp >= 2:
                P.wait("tensor", "v", v_mev[hp - 2])
            P.op("tensor", lambda e, hp=hp, bb=bb: e.transpose(
                bankb(bb)[0:64, 0:64], SMB[0:64, hp, :],
                EYE[0:64, 0:64]), inc=False)
            t_tp = P.op("tensor", lambda e, hp=hp, bb=bb: e.transpose(
                bankb(bb)[64:128, 64:128], SMB[64:128, hp, :],
                EYE[64:128, 64:128], tile_position=(64, 64)))
            P.wait("vector", "t", t_tp)
            P.op("vector", lambda e, hp=hp, bb=bb: e.tensor_copy(
                MAPB[0:64, hp, :], bankb(bb)[0:64, 0:64]), inc=False)
            v_mev[hp] = P.op("vector", lambda e, hp=hp, bb=bb: e.tensor_copy(
                MAPB[64:128, hp, :], bankb(bb)[64:128, 64:128]))
        v_map = P.c["v"]

        sa_h2q_st = {}
        sq_h2_hist = {}
        XTC = YTG.rearrange("p a b -> p (a b)").bitcast(F32).rearrange(
            "p (b c) -> p b c", b=4)
        v_x2_hist = {}
        v_ln_hist = {}
        xtc_tok = {}
        x2st_tok = {}
        P.g_h2_prev = 0
        P.wait("scalar", "v", P.c["v"])
        P.wait("sync", "v", P.c["v"])
        P.wait("gpsimd", "v", P.c["v"])
        P.wait("gpsimd", "t", P.c["t"])
        P.wait("gpsimd", "s", P.c["s"])
        P.wait("tensor", "v", P.c["v"])
        P.wait("tensor", "s", P.c["s"])
        P.wait("tensor", "g", P.c["g"])
        v_ao_prev = P.c["v"]
        s_zbb7_prev = P.c["s"]
        s_oev7_prev = P.c["s"]
        qb_of = [bank(0), bank(1), bank(6), bank(7)]
        X2T4 = [Uf[:, 7168:8192], Uf[:, 8192:9216],
                Uf[:, 1024:2048], Uf[:, 11264:12288]]
        MU2 = SYCOL[:, 0:2]
        VB2 = SYCOL[:, 2:4]
        SD2 = SYCOL[:, 4:6]
        RS2 = SYCOL[:, 6:8]
        sq_ao_hist = {}
        va_hist = {}
        sqrt_hist = {}
        P.qbarrier("scalar", "sq")
        P.qbarrier("sync", "sx")
        P.qbarrier("sync", "sq")

        def t4_partA(st, t4):
            ft = st * 4 + t4
            P.dwait_tok("vector", xtc_tok[(st, t4)])
            if ft >= 4:
                P.dwait_tok("vector", x2st_tok[ft - 4])
            P.dwait("vector", sq_ao_hist[st])
            aosl = AOTM[:, :, t4, :]
            v_x2 = P.op("vector", lambda e, t4=t4, aosl=aosl: e.tensor_add(
                X2T4[t4][:], XTC[:, t4, :], aosl))
            v_x2_hist[(st, t4)] = v_x2
            P.wait("sync", "v", v_x2)
            x2st_tok[ft] = P.dma("sync", "sx",
                                 x2_hbm[ft * 128:(ft + 1) * 128, :], X2T4[t4][:])
            P.op("vector", lambda e, t4=t4: e.tensor_mul(
                SCR[:], X2T4[t4][:], X2T4[t4][:]))
            P.op("vector", lambda e: e.reduce_sum(stat_a(0, 0, 1), SCR[:], AXX.X))
            P.op("vector", lambda e, t4=t4: e.reduce_sum(
                stat_a(0, 1, 2), X2T4[t4][:], AXX.X))
            P.op("vector", lambda e, t4=t4: e.tensor_single_scalar(
                MU2[:, t4 % 2:t4 % 2 + 1], stat_a(0, 1, 2), 1.0 / D, ALU.mult))
            P.op("vector", lambda e: e.tensor_single_scalar(
                stat_a(0, 3, 4), stat_a(0, 0, 1), 1.0 / D, ALU.mult))
            P.op("vector", lambda e, t4=t4: e.tensor_mul(
                stat_a(0, 4, 5), MU2[:, t4 % 2:t4 % 2 + 1], MU2[:, t4 % 2:t4 % 2 + 1]))
            va_hist[(st, t4)] = P.op("vector", lambda e, t4=t4: e.tensor_sub(
                VB2[:, t4 % 2:t4 % 2 + 1], stat_a(0, 3, 4), stat_a(0, 4, 5)))

        def t4_sqrt(st, pair):
            P.wait("scalar", "v", va_hist[(st, 2 * pair + 1)])
            sqrt_hist[(st, pair)] = P.op("scalar", lambda e: e.activation(
                SD2[:, 0:2], VB2[:, 0:2], ACT.Sqrt, bias=EPSA))

        def t4_partB(st, t4):
            ft = st * 4 + t4
            P.wait("vector", "s", sqrt_hist[(st, t4 // 2)])
            P.op("vector", lambda e, t4=t4: e.reciprocal(
                RS2[:, t4 % 2:t4 % 2 + 1], SD2[:, t4 % 2:t4 % 2 + 1]))
            v_ln = P.op("vector", lambda e, t4=t4: e.tensor_scalar(
                SCR[:], X2T4[t4][:], MU2[:, t4 % 2:t4 % 2 + 1],
                RS2[:, t4 % 2:t4 % 2 + 1], ALU.subtract, ALU.mult))
            P.op("vector", lambda e: e.tensor_mul(SCR[:], SCR[:], CB1BC[:]))
            P.vgap()
            P.op("vector", lambda e: e.tensor_add(X3T[:, 0, :], SCR[:], SBBC[:]))
            P.vgap()
            P.op("vector", lambda e: e.reduce_max(stat_a(1, 0, 1), X3T[:, 0, :], AXX.X,
                                                  apply_absolute_value=True))
            P.vgap()
            P.op("vector", lambda e: e.reciprocal(stat_a(1, 1, 2), stat_a(1, 0, 1)))
            P.vgap()
            P.op("vector", lambda e: e.tensor_single_scalar(
                stat_a(1, 2, 3), stat_a(1, 1, 2), 127.0, ALU.mult))
            P.op("vector", lambda e: e.tensor_single_scalar(
                stat_a(1, 3, 4), stat_a(1, 0, 1), 1.0 / 127.0, ALU.mult))
            P.op("vector", lambda e, t4=t4: e.tensor_scalar(
                SCR[:], X3T[:, 0, :], stat_a(1, 2, 3), CPX[:, 0:1],
                ALU.mult, ALU.add))
            P.vgap()
            v_h2q = P.op("vector", lambda e, t4=t4: e.tensor_scalar(
                H2T1[:], SCR[:], CPX[:, 0:1], stat_a(1, 3, 4),
                ALU.subtract, ALU.mult))
            P.wait("sync", "v", v_h2q)
            if t4 == 0 and st >= 1:
                P.dwait("sync", sa_h2q_st[st - 1])
            sq_h2 = P.dmat(H2FM[:, t4 * 8:(t4 + 1) * 8, :], H2T1[:])
            sq_h2_hist[st] = sq_h2
            P.dwait("vector", sq_h2)

        def t4_store(st):
            P.dwait("sync", sq_h2_hist[st])
            sa_h2q_st[st] = P.dma("sync", "sx", h2q_hbm[st], H2FM[:])

        for t4 in range(4):
            xtc_tok[(0, t4)] = P.dma(
                "scalar", "sa", XTC[:, t4, :], x_h[t4 * 128:(t4 + 1) * 128, :])
        for st in range(NST):
            cols = slice(st * 512, (st + 1) * 512)
            if st >= 1:
                t4_partA(st - 1, 0)
                t4_partA(st - 1, 1)
            s_qev = {}
            P.wait("tensor", "s", s_zbb7_prev)
            for j in range(J):
                qb = qb_of[j % 4]
                if j >= 4:
                    P.wait("tensor", "s", s_qev[j - 4])
                for k in range(J):
                    P.op("tensor", lambda e, j=j, k=k, qb=qb, cols=cols: e.matmul(
                        qb, WAT[:, k, 0, j, :], HF[:, k, cols],
                        start=(k == 0), stop=(k == J - 1)), inc=(k == J - 1))
                t_q = P.c["t"]
                P.wait("scalar", "t", t_q)
                s_qev[j] = P.op("scalar", lambda e, j=j, qb=qb: e.activation(
                    QF[:, j, :], qb, ACT.Copy))
            s_oev = {}
            P.wait("tensor", "s", s_oev7_prev)
            for j in range(J):
                ob = bank(2 + j % 2)
                if j >= 2:
                    P.wait("tensor", "s", s_oev[j - 2])
                P.wait("tensor", "s", s_qev[j])
                P.op("tensor", lambda e, j=j, ob=ob: e.matmul(
                    ob[0:64, :], MAPB[0:64, j, :], QF[0:64, j, :],
                    start=True, stop=True, tile_position=(0, 0)), inc=False)
                P.op("tensor", lambda e, j=j, ob=ob: e.matmul(
                    ob[64:128, :], MAPB[64:128, j, :], QF[64:128, j, :],
                    start=True, stop=True, tile_position=(64, 64)), inc=True)
                t_o = P.c["t"]
                P.wait("scalar", "t", t_o)
                s_oev[j] = P.op("scalar", lambda e, j=j, ob=ob: e.activation(
                    OF[:, j, :], ob, ACT.Copy))
            s_oev7_prev = s_oev[J - 1]
            if st >= 1:
                t4_sqrt(st - 1, 0)
                t4_partB(st - 1, 0)
                t4_partB(st - 1, 1)
            s_zbb = {}
            s_sq = {}
            t_stats = {}
            P.wait("tensor", "v", v_ao_prev)
            P.wait("tensor", "s", s_oev[J - 1])
            for j in range(J):
                wb = qb_of[j % 4]
                if j < 4:
                    P.wait("tensor", "s", s_qev[j + 4])
                else:
                    P.wait("tensor", "s", s_zbb[j - 4])
                for k in range(J):
                    P.op("tensor", lambda e, j=j, k=k, wb=wb: e.matmul(
                        wb, WAT[:, k, 3, j, :], OF[:, k, :],
                        start=(k == 0), stop=False), inc=False)
                P.op("tensor", lambda e, j=j, wb=wb, cols=cols: e.matmul(
                    wb, EYE[:], HF[:, j, cols], start=False, stop=True), inc=True)
                t_w = P.c["t"]
                if j >= 2:
                    P.wait("scalar", "t", t_stats[j - 2])
                P.wait("scalar", "t", t_w)
                s_zbb[j] = P.op("scalar", lambda e, j=j, wb=wb: e.activation(
                    ZBB[:, j, :], wb, ACT.Identity, bias=BOF[:, j:j + 1]))
                s_sq[j] = P.op("scalar", lambda e, j=j: e.activation(
                    SQB[:, j % 2, :], ZBB[:, j, :], ACT.Square))
                P.wait("tensor", "s", s_sq[j])
                P.op("tensor", lambda e, j=j: e.matmul(
                    bank(4)[0:1, :], ONEC[:], ZBB[:, j, :],
                    start=(j == 0), stop=(j == J - 1)), inc=False)
                P.op("tensor", lambda e, j=j: e.matmul(
                    bank(5)[0:1, :], ONEC[:], SQB[:, j % 2, :],
                    start=(j == 0), stop=(j == J - 1)), inc=(j == J - 1))
                t_stats[j] = P.c["t"]
            s_zbb7_prev = s_sq[J - 1]
            t_st = P.c["t"]
            P.wait("vector", "t", t_st)
            P.op("vector", lambda e: e.tensor_single_scalar(st1(0), bank(4)[0:1, :], 1.0 / D, ALU.mult))
            P.op("vector", lambda e: e.tensor_single_scalar(st1(1), bank(5)[0:1, :], 1.0 / D, ALU.mult))
            P.vgap()
            P.op("vector", lambda e: e.tensor_mul(st1(2), st1(0), st1(0)))
            P.vgap()
            v_vr = P.op("vector", lambda e: e.tensor_sub(st1(1), st1(1), st1(2)))
            P.wait("scalar", "v", v_vr)
            s_sd2 = P.op("scalar", lambda e: e.activation(st1(2), st1(1), ACT.Sqrt, bias=EPSA[0:1, 0:1]))
            P.wait("vector", "s", s_sd2)
            P.op("vector", lambda e: e.reciprocal(st1(1), st1(2)))
            P.vgap()
            P.op("vector", lambda e: e.tensor_mul(st1(2), st1(0), st1(1)))
            P.vgap()
            P.op("vector", lambda e: e.tensor_copy(st1b(0), st1(1)))
            v_r2 = P.op("vector", lambda e: e.tensor_copy(st1b(1), st1(2)))
            P.wait("tensor", "v", v_r2)
            P.op("tensor", lambda e: e.matmul(
                bank(4), ONE1[0:1, 0:128], st1b(1), start=True, stop=True), inc=False)
            P.op("tensor", lambda e: e.matmul(
                bank(5), ONE1[0:1, 0:128], st1b(0), start=True, stop=True), inc=True)
            t_bc = P.c["t"]
            if st >= 1:
                t4_partA(st - 1, 2)
                t4_partA(st - 1, 3)
                t4_sqrt(st - 1, 1)
            P.wait("vector", "t", t_bc)
            if st >= 1:
                P.dwait("vector", sq_ao_hist[st - 1])
            for j in range(J):
                P.op("vector", lambda e, j=j: e.tensor_mul(
                    SCR[:, 0:512], ZBB[:, j, :], bank(5)))
                P.op("vector", lambda e, j=j: e.tensor_sub(
                    SCR[:, 0:512], SCR[:, 0:512], bank(4)))
                vj = P.op("vector", lambda e, j=j: e.tensor_scalar(
                    AOC[:, j * 512:(j + 1) * 512], SCR[:, 0:512],
                    G1F[:, j:j + 1], G0F[:, j:j + 1], ALU.mult, ALU.add))
                P.wait("sync", "v", vj)
                sq_ao = P.dmat(AOTM[:, j, :, :], AOC[:, j * 512:(j + 1) * 512])
            v_ao = P.c["v"]
            v_ao_prev = v_ao
            sq_ao_hist[st] = sq_ao
            if st >= 1:
                t4_partB(st - 1, 2)
                t4_partB(st - 1, 3)
                t4_store(st - 1)
            for t4 in range(4):
                if st >= 1:
                    P.wait("scalar", "v", v_x2_hist[(st - 1, t4)])
                    xtc_tok[(st, t4)] = P.dma(
                        "scalar", "sa", XTC[:, t4, :],
                        x_h[(st * 4 + t4) * 128:(st * 4 + t4 + 1) * 128, :])

        t4_partA(NST - 1, 0)
        t4_partA(NST - 1, 1)
        t4_sqrt(NST - 1, 0)
        t4_partB(NST - 1, 0)
        t4_partB(NST - 1, 1)
        t4_partA(NST - 1, 2)
        t4_partA(NST - 1, 3)
        t4_sqrt(NST - 1, 1)
        t4_partB(NST - 1, 2)
        t4_partB(NST - 1, 3)
        t4_store(NST - 1)

        WGT = HF
        WUT = WAT.rearrange("p a b c d -> p (a b c d)").rearrange(
            "p (k m) -> p k m", k=J)
        wgt_r = WGT.rearrange("p k (i c) -> p k i c", c=128)
        wut_r = WUT.rearrange("p k (i c) -> p k i c", c=128)
        h2q_r = h2q_hbm.rearrange("s p (t k) c -> s p t k c", k=J)
        s_gs = None
        v_up = None
        g_y_prev = 0
        s_up_hist = {}
        g_y_hist = {}
        s_up_save = {}
        g_y_save = {}
        sq_wq_last = None
        v_wq_by = {0: 0, 1: 0}
        sg_yq = {}
        t_gu_hist = {}
        P.wait("gpsimd", "v", P.c["v"])
        P.wait("gpsimd", "t", P.c["t"])
        P.wait("gpsimd", "s", P.c["s"])
        P.qbarrier("gpsimd", "sq")
        P.qbarrier("gpsimd", "sa")
        P.qbarrier("gpsimd", "sx")
        P.qbarrier("scalar", "sx")
        P.wait("scalar", "g", P.c["g"])
        P.wait("scalar", "v", P.c["v"])
        P.wait("scalar", "t", P.c["t"])
        P.wait("sync", "t", P.c["t"])
        P.wait("sync", "s", P.c["s"])
        WQN8 = [Uf[:, 2048:3072], Uf[:, 3072:4096], Uf[:, 5640:6664],
                Uf[:, 6664:7688], Uf[:, 7688:8712], Uf[:, 8712:9736],
                Uf[:, 9736:10760]]
        SCRQ2 = Uf[:, 10760:11784]
        WFQ2 = [WFQ, U[:, 2048:3072]]

        def wsrc_of(m):
            return ((wg_h, wu_h)[m % 2])[(m // 2) * 128:(m // 2 + 1) * 128, :]

        wq_tok = {}
        sq_wq_hist = {}
        g_wq_hist = {}
        g_wq_last = 0
        sa_h2t_tok = {}
        sa_h2t_tok[0] = P.dma("scalar", "sa", H2QT[:, :, :, :],
                              h2q_r[0, :, 0:2, :, :])
        for tg in range(1):
            if tg >= 1:
                P.dwait("gpsimd", sg_yq[tg - 1])
            sa_h2t = sa_h2t_tok[tg]
            h2b = H2QT if tg % 2 == 0 else H2QT2
            if tg == 0:
                wst_tok = {}
                for ii in range(4):
                    for which in range(2):
                        wst_tok[2 * ii + which] = P.dma(
                            "scalar", "sa", (wgt_r, wut_r)[which][:, :, ii, :],
                            wgq_hbm[which, ii])
            for i in range(NI):
                if tg == 0:
                    P.dwait_tok("tensor", wst_tok[2 * i])
                    P.dwait_tok("tensor", wst_tok[2 * i + 1])
                P.dwait_tok("tensor", sa_h2t)
                gb = bank(0 + 2 * (i % 2), TGW)
                ub = bank(1 + 2 * (i % 2), TGW)
                GSi = GS if i % 2 == 0 else GSB
                UPSi = UPS if i % 2 == 0 else UPSB
                if i >= 2:
                    P.wait("tensor", "s", s_up_save[i - 2])
                elif tg >= 1:
                    P.wait("tensor", "s", s_up_hist[30 + i])
                for k in range(J):
                    rhs = h2b[:, :, k, :]
                    P.op("tensor", lambda e, i=i, k=k, rhs=rhs, gb=gb: e.matmul(
                        gb, wgt_r[:, k, i, :], rhs,
                        start=(k == 0), stop=(k == J - 1)), inc=False)
                    P.op("tensor", lambda e, i=i, k=k, rhs=rhs, ub=ub: e.matmul(
                        ub, wut_r[:, k, i, :], rhs,
                        start=(k == 0), stop=(k == J - 1)), inc=(k == J - 1))
                t_gu = P.c["t"]
                if tg == 0 and i + 4 < NI:
                    P.wait("scalar", "t", t_gu)
                    P.dwait_tok("scalar", wst_tok[2 * i])
                    P.dwait_tok("scalar", wst_tok[2 * i + 1])
                    for which in range(2):
                        wst_tok[2 * (i + 4) + which] = P.dma(
                            "scalar", "sa", (wgt_r, wut_r)[which][:, :, i + 4, :],
                            wgq_hbm[which, i + 4])
                P.wait("scalar", "t", t_gu)
                if i >= 2:
                    P.wait("scalar", "g", g_y_save[i - 2])
                elif tg >= 1:
                    P.wait("scalar", "g", g_y_hist[30 + i])
                s_gs = P.op("scalar", lambda e, GSi=GSi, gb=gb: e.activation(
                    GSi[:], gb, ACT.Silu))
                s_up = P.op("scalar", lambda e, UPSi=UPSi, ub=ub: e.activation(
                    UPSi[:], ub, ACT.Copy))
                s_up_save[i] = s_up
                P.wait("gpsimd", "s", s_up)
                g_y = P.op("gpsimd", lambda e, i=i, GSi=GSi, UPSi=UPSi: e.tensor_mul(
                    YTG[:, i, :], GSi[:], UPSi[:]))
                g_y_save[i] = g_y
                g_y_prev = g_y
                P.wait("vector", "g", g_y)
                if i == 0:
                    P.op("vector", lambda e: e.tensor_copy(MXT[:], YTG[:, 0, :]))
                    P.op("vector", lambda e: e.tensor_copy(MNT[:], YTG[:, 0, :]))
                else:
                    P.op("vector", lambda e, i=i: e.tensor_max(
                        MXT[:], MXT[:], YTG[:, i, :]))
                    P.op("vector", lambda e, i=i: e.tensor_tensor(
                        MNT[:], MNT[:], YTG[:, i, :], ALU.min))
            s_up_hist = s_up_save
            g_y_hist = g_y_save
            s_up_save = {}
            g_y_save = {}
            t_gu_hist[tg] = P.c["t"]
            if tg + 1 < NTG:
                if tg >= 1:
                    P.wait("scalar", "t", t_gu_hist[tg - 1])
                h2bn = H2QT if (tg + 1) % 2 == 0 else H2QT2
                sa_h2t_tok[tg + 1] = P.dma(
                    "scalar", "sa", h2bn[:, :, :, :],
                    h2q_r[(tg + 1) // 2, :, 2 * ((tg + 1) % 2):2 * ((tg + 1) % 2) + 2, :, :])
            P.op("vector", lambda e: e.tensor_single_scalar(
                MNT[:], MNT[:], -1.0, ALU.mult))
            P.op("vector", lambda e: e.tensor_max(MXT[:], MXT[:], MNT[:]))
            g_ymax = P.c["v"]
            P.wait("tensor", "v", g_ymax)
            for t4 in range(2):
                bb = 4 + t4
                P.op("tensor", lambda e, t4=t4, bb=bb: e.transpose(
                    bankb(bb)[:, 0:128], MXT[:, t4 * 128:(t4 + 1) * 128], EYE[:]),
                    inc=True)
            t_mx = P.c["t"]
            P.wait("vector", "t", t_mx)
            for t4 in range(2):
                bb = 4 + t4
                P.op("vector", lambda e, t4=t4, bb=bb: e.reduce_max(
                    SYCOL[:, t4:t4 + 1], bankb(bb)[:, 0:128], AXX.X))
            for t4 in range(2):
                P.op("vector", lambda e, t4=t4: e.reciprocal(
                    SYCOL[:, t4 + 4:t4 + 5], SYCOL[:, t4:t4 + 1]))
            P.vgap()
            for t4 in range(2):
                P.op("vector", lambda e, t4=t4, tg=tg: e.tensor_single_scalar(
                    SYR[:, tg * 2 + t4:tg * 2 + t4 + 1], SYCOL[:, t4:t4 + 1],
                    1.0 / 127.0, ALU.mult))
                P.op("vector", lambda e, t4=t4: e.tensor_single_scalar(
                    SYCOL[:, t4:t4 + 1], SYCOL[:, t4 + 4:t4 + 5], 127.0, ALU.mult))
            P.vgap()
            v_syb = P.op("vector", lambda e: e.tensor_copy(SYKT[:, 0:2], SYCOL[:, 0:2]))
            P.wait("tensor", "v", v_syb)
            for t4 in range(2):
                P.op("tensor", lambda e, t4=t4: e.matmul(
                    bank(5)[0:1, t4 * 128:(t4 + 1) * 128], SYKT[:, t4:t4 + 1], EYE[:],
                    start=(t4 == 0), stop=(t4 == 1)), inc=(t4 == 1))
            t_syr = P.c["t"]
            P.wait("vector", "t", t_syr)
            v_row = P.op("vector", lambda e: e.tensor_copy(SYROW, bank(5)[0:1, 0:TGW]))
            P.wait("tensor", "v", v_row)
            P.op("tensor", lambda e: e.matmul(bank(5, TGW), ONE1[0:1, 0:128],
                                              SYROW, start=True, stop=True))
            t_syb = P.c["t"]
            P.wait("vector", "t", t_syb)
            v_sybs = P.op("vector", lambda e: e.tensor_copy(SYBS[:], bank(5, TGW)))
            P.wait("gpsimd", "v", v_sybs)
            v_rnd = {}
            for i in range(NI):
                if i >= 2:
                    P.wait("gpsimd", "v", v_rnd[i - 2])
                g_z = P.op("gpsimd", lambda e, i=i: e.tensor_mul(
                    ZQ[:, i % 2, :], YTG[:, i, :], SYBS[:]))
                P.wait("vector", "g", g_z)
                v_rnd[i] = P.op("vector", lambda e, i=i: e.tensor_scalar(
                    YTG[:, i, :], ZQ[:, i % 2, :], CPX[:, 0:1], CPX[:, 0:1],
                    ALU.add, ALU.subtract))
            v_yq = P.c["v"]
            P.wait("gpsimd", "v", v_yq)
            sg_yq[tg] = P.dma("gpsimd", "sg", yq_hbm[tg], YTG[:, :, :])

        v_yq_prev = v_yq
        g_zq_prev = P.c["g"]
        YTGflat2 = YTG.rearrange("p a b -> p (a b)")
        GS2 = [U[:, 9216:11264].rearrange("p (a c) -> p a c", c=512),
               U[:, 11272:13320].rearrange("p (a c) -> p a c", c=512)]
        YSTG = [U[:, 18432:19456].rearrange("p (a c) -> p a c", c=128),
                U[:, 19456:20480].rearrange("p (a c) -> p a c", c=128)]
        su2 = {}
        gy2 = {}
        yst_tok_prev = {}
        for tg in range(1, NTG):
            P.dwait("gpsimd", sg_yq.get(tg - 1, 0))
            P.qbarrier("gpsimd", "sx")
            h2b = H2QT if tg % 2 == 0 else H2QT2
            P.dwait_tok("tensor", sa_h2t_tok[tg])
            for t in range(2):
                for bb in range(4):
                    u = t * 4 + bb
                    gu = tg * 8 + u
                    bs = 4 * (u % 2)
                    bks = [bank(bs), bank(bs + 1), bank(bs + 2), bank(bs + 3)]
                    if gu >= 10:
                        P.wait("tensor", "s", su2.get(gu - 2, 0))
                    elif u == 0:
                        P.wait("tensor", "s", s_up_hist[31])
                    else:
                        P.wait("tensor", "v", v_yq_prev)
                    for k in range(J):
                        lhs = h2b[:, t, k, :]
                        for bi in range(2):
                            b = 2 * bb + bi
                            P.op("tensor", lambda e, lhs=lhs, b=b, bi=bi, k=k, bks=bks: e.matmul(
                                bks[bi], lhs, WGT[:, k, b * 512:(b + 1) * 512],
                                start=(k == 0), stop=(k == J - 1)), inc=False)
                            P.op("tensor", lambda e, lhs=lhs, b=b, bi=bi, k=k, bks=bks: e.matmul(
                                bks[2 + bi], lhs, WUT[:, k, b * 512:(b + 1) * 512],
                                start=(k == 0), stop=(k == J - 1)),
                                inc=(k == J - 1 and bi == 1))
                    t_gu = P.c["t"]
                    P.wait("scalar", "t", t_gu)
                    if gu >= 10:
                        P.wait("scalar", "g", gy2.get(gu - 2, 0))
                    elif tg == 1:
                        P.wait("scalar", "g", g_zq_prev)
                        P.wait("scalar", "v", v_yq_prev)
                    gsb = GS2[u % 2]
                    for bi in range(2):
                        P.op("scalar", lambda e, gsb=gsb, bi=bi, bks=bks: e.activation(
                            gsb[:, bi, :], bks[bi], ACT.Silu), inc=False)
                        P.op("scalar", lambda e, gsb=gsb, bi=bi, bks=bks: e.activation(
                            gsb[:, 2 + bi, :], bks[2 + bi], ACT.Copy),
                            inc=(bi == 1))
                    s_up2 = P.c["s"]
                    su2[gu] = s_up2
                    P.wait("gpsimd", "s", s_up2)
                    for bi in range(2):
                        b = 2 * bb + bi
                        P.op("gpsimd", lambda e, t=t, b=b, bi=bi, gsb=gsb: e.tensor_mul(
                            YTGflat2[:, t * 4096 + b * 512:t * 4096 + (b + 1) * 512],
                            gsb[:, bi, :], gsb[:, 2 + bi, :]), inc=(bi == 1))
                    g_y2 = P.c["g"]
                    gy2[gu] = g_y2
                    P.wait("vector", "g", g_y2)
                    for bi in range(2):
                        b = 2 * bb + bi
                        if u == 0 and bi == 0:
                            P.op("vector", lambda e, t=t, b=b: e.reduce_max(
                                SYCOL[:, 4 + t:5 + t],
                                YTGflat2[:, t * 4096 + b * 512:t * 4096 + (b + 1) * 512],
                                AXX.X, apply_absolute_value=True))
                        else:
                            P.op("vector", lambda e, t=t, b=b: e.reduce_max(
                                SYCOL[:, 6:7],
                                YTGflat2[:, t * 4096 + b * 512:t * 4096 + (b + 1) * 512],
                                AXX.X, apply_absolute_value=True))
                            P.vgap()
                            P.op("vector", lambda e, t=t: e.tensor_max(
                                SYCOL[:, 4 + t:5 + t], SYCOL[:, 4 + t:5 + t],
                                SYCOL[:, 6:7]))
                yt = YTGflat2[:, t * 4096:(t + 1) * 4096]
                P.op("vector", lambda e, t=t, tg=tg: e.tensor_single_scalar(
                    SYR[:, 2 * tg + t:2 * tg + t + 1], SYCOL[:, 4 + t:5 + t],
                    1.0 / 127.0, ALU.mult))
                P.op("vector", lambda e, t=t: e.reciprocal(
                    SYCOL[:, 7:8], SYCOL[:, 4 + t:5 + t]))
                P.vgap()
                P.op("vector", lambda e: e.tensor_single_scalar(
                    SYCOL[:, 7:8], SYCOL[:, 7:8], 127.0, ALU.mult))
                P.vgap()
                P.op("vector", lambda e, yt=yt: e.tensor_scalar(
                    yt, yt, SYCOL[:, 7:8], None, ALU.mult))
                P.vgap()
                v_rq = P.op("vector", lambda e, yt=yt: e.tensor_scalar(
                    yt, yt, CPX[:, 0:1], CPX[:, 0:1], ALU.add, ALU.subtract))
                P.wait("sync", "v", v_rq)
                yst_tok = {}
                for cc in range(4):
                    if cc >= 2:
                        P.dwait_tok("sync", yst_tok[cc - 2])
                    elif t == 1 or tg >= 2:
                        P.dwait_tok("sync", yst_tok_prev.get(cc + 2, 0))
                    sqy = P.dmat(YSTG[cc % 2][:, :, :],
                                 yt[:, cc * 1024:(cc + 1) * 1024])
                    P.dwait_tok("sync", sqy)
                    yst_tok[cc] = P.dma(
                        "sync", "sx",
                        yq_hbm[tg][:, 8 * cc:8 * cc + 8, t * 128:(t + 1) * 128],
                        YSTG[cc % 2][:, :, :])
                yst_tok_prev = yst_tok
                if t == 1:
                    v_yq_prev = v_rq
                    g_zq_prev = gy2[tg * 8 + 7]
            t_gu_hist[tg] = P.c["t"]
            if tg + 1 < NTG:
                P.wait("scalar", "t", t_gu_hist[tg - 1])
                h2bn = H2QT if (tg + 1) % 2 == 0 else H2QT2
                sa_h2t_tok[tg + 1] = P.dma(
                    "scalar", "sa", h2bn[:, :, :, :],
                    h2q_r[(tg + 1) // 2, :, 2 * ((tg + 1) % 2):2 * ((tg + 1) % 2) + 2, :, :])
            sg_yq[tg] = 0

        sa_out_hist = {}
        P.wait("gpsimd", "v", P.c["v"])
        P.wait("gpsimd", "s", P.c["s"])
        P.qbarrier("gpsimd", "sg")
        P.qbarrier("gpsimd", "sx")
        P.wait("scalar", "v", P.c["v"])
        P.wait("scalar", "g", P.c["g"])
        P.qbarrier("scalar", "sg")
        P.qbarrier("scalar", "sq")
        P.qbarrier("sync", "sa")
        P.qbarrier("sync", "sx")
        WDTS4 = U[:, 4096:8192].rearrange("p (b c) -> p b c", b=4)
        YTG2B = [YTG.rearrange("p a b -> p (a b)"), U[:, 8192:16384]]
        X2TB = Uf[:, 9216:11264].rearrange("p (b c) -> p b c", b=2)
        t_dn_prev = 0
        v_x3_prev = 0
        y_tok = {}
        for j in range(2):
            y_tok[j] = P.dma("gpsimd", ("sg", "sw")[j], YTG2B[j][:], yq_hbm[j])
        for pp in range(8):
            P.wait("tensor", "v", v_x3_prev)
            P.dwait_tok("tensor", y_tok[0])
            P.dwait_tok("tensor", y_tok[1])
            wds_tok = {}
            for w0 in range(4):
                eng0 = ("scalar", "sa") if w0 % 2 == 0 else ("gpsimd", "sg")
                P.wait(eng0[0], "t", t_dn_prev)
                wds_tok[w0] = P.dma(eng0[0], eng0[1], WDTS4[:, w0, :], wdt_hbm[w0])
            yv = [YTG2B[0].rearrange("p (i c) -> p i c", c=256),
                  YTG2B[1].rearrange("p (i c) -> p i c", c=256)]
            t_dn_i = {}
            for i in range(NI):
                P.dwait_tok("tensor", wds_tok[i])
                for j in range(2):
                    for t4 in range(2):
                        dpb = PS[:, 4 * j + 2 * t4:4 * j + 2 * t4 + 2, :].rearrange(
                            "p a b -> p (a b)")
                        for half in range(2):
                            P.op("tensor", lambda e, i=i, j=j, t4=t4, half=half, dpb=dpb, yv=yv: e.matmul(
                                dpb[:, half * 512:(half + 1) * 512],
                                yv[j][:, i, t4 * 128:(t4 + 1) * 128],
                                WDTS4[:, i % 4, half * 512:(half + 1) * 512],
                                start=(i == 0), stop=(i == NI - 1)),
                                inc=(half == 1 and t4 == 1 and j == 1))
                t_dn_i[i] = P.c["t"]
                if i + 4 < NI:
                    eng0 = ("scalar", "sa") if (i + 4) % 2 == 0 else ("gpsimd", "sg")
                    P.wait(eng0[0], "t", t_dn_i[i])
                    if i >= 1:
                        P.dwait_tok(eng0[0], wds_tok[i - 1])
                    wds_tok[i + 4] = P.dma(eng0[0], eng0[1],
                                           WDTS4[:, (i + 4) % 4, :], wdt_hbm[i + 4])
            t_dn = P.c["t"]
            t_dn_prev = t_dn
            if pp + 1 < 8:
                for j in range(2):
                    P.wait("gpsimd", "t", t_dn)
                    P.dwait_tok("gpsimd", y_tok[j])
                    y_tok[j] = P.dma("gpsimd", ("sg", "sw")[j], YTG2B[j][:],
                                     yq_hbm[2 * (pp + 1) + j])
            P.wait("vector", "t", t_dn)
            for j in range(2):
                tg = 2 * pp + j
                for t4 in range(2):
                    ft = tg * 2 + t4
                    P.wait("sync", "v", P.c["v"])
                    sa_x2r = P.dma("sync", "sq", X2TB[:, t4, :],
                                   x2_hbm[ft * 128:(ft + 1) * 128, :])
                    P.dwait("vector", sa_x2r)
                    if ft >= 2:
                        P.dwait("vector", sa_out_hist.get(ft - 2, 0))
                    dpb = PS[:, 4 * j + 2 * t4:4 * j + 2 * t4 + 2, :].rearrange(
                        "p a b -> p (a b)")
                    v_x3 = P.op("vector", lambda e, t4=t4, tg=tg, dpb=dpb: e.scalar_tensor_tensor(
                        X3T[:, t4 % 2, :], dpb, SYR[:, tg * 2 + t4:tg * 2 + t4 + 1],
                        X2TB[:, t4, :], ALU.mult, ALU.add))
                    P.wait("sync", "v", v_x3)
                    v_x3_prev = v_x3
                    sa_out_hist[ft] = P.dma("sync", "sq",
                                            out_h[ft * 128:(ft + 1) * 128, :],
                                            X3T[:, t4 % 2, :])

        P.qbarrier("sync", "sa")
        P.qbarrier("sync", "sg")
        P.qbarrier("sync", "sq")
        P.qbarrier("sync", "sw")
        P.qbarrier("sync", "sx")
        P.wait("sync", "v", P.c["v"])
        P.wait("sync", "t", P.c["t"])
        P.wait("sync", "s", P.c["s"])
        P.wait("sync", "g", P.c["g"])

        @block.sync
        def _(e):
            P.run("sync", e)

        @block.vector
        def _(e):
            P.run("vector", e)

        @block.scalar
        def _(e):
            P.run("scalar", e)

        @block.tensor
        def _(e):
            P.run("tensor", e)

        @block.gpsimd
        def _(e):
            P.run("gpsimd", e)

    return nc


_NC_CACHE = None


def kernel(L0_attn_x, L0_attn_Wq, L0_attn_Wk, L0_attn_Wv, L0_attn_Wo,
           L0_attn_bo, L0_attn_alpha, L0_attn_temperature, L0_attn_gamma,
           L0_attn_beta, L0_ffn_w_gate, L0_ffn_w_up, L0_ffn_w_down,
           L0_c, L0_wmod):
    global _NC_CACHE
    x = np.ascontiguousarray(np.asarray(L0_attn_x, dtype=np.float32))
    B, N, Dm = x.shape
    if _NC_CACHE is None:
        _NC_CACHE = build_nc()
    nc = _NC_CACHE

    eye = np.eye(128, dtype=np.float32).astype(BF)
    eyf = np.concatenate([np.eye(64, dtype=np.float32)] * 2, axis=0)
    hsel = np.zeros((128, 2), np.float32)
    hsel[0:64, 0] = 1.0
    hsel[64:128, 1] = 1.0
    hsel = hsel.astype(BF)
    g2sel = np.zeros((2, 128), np.float32)
    g2sel[0, 0:64] = 1.0
    g2sel[1, 64:128] = 1.0
    g2sel = g2sel.astype(BF)

    shared = {
        "wq": np.ascontiguousarray(L0_attn_Wq, np.float32),
        "wk": np.ascontiguousarray(L0_attn_Wk, np.float32),
        "wv": np.ascontiguousarray(L0_attn_Wv, np.float32),
        "wo": np.ascontiguousarray(L0_attn_Wo, np.float32),
        "bo": np.ascontiguousarray(np.asarray(L0_attn_bo, np.float32).reshape(1, Dm)),
        "alpha": np.ascontiguousarray(np.asarray(L0_attn_alpha, np.float32).reshape(1, 1)),
        "temp": np.ascontiguousarray(np.asarray(L0_attn_temperature, np.float32).reshape(1, 1)),
        "gamma": np.ascontiguousarray(np.asarray(L0_attn_gamma, np.float32).reshape(1, Dm)),
        "beta": np.ascontiguousarray(np.asarray(L0_attn_beta, np.float32).reshape(1, Dm)),
        "wg": np.ascontiguousarray(L0_ffn_w_gate, np.float32),
        "wu": np.ascontiguousarray(L0_ffn_w_up, np.float32),
        "wd": np.ascontiguousarray(L0_ffn_w_down, np.float32),
        "wmod": np.ascontiguousarray(L0_wmod, np.float32),
        "eye": eye,
        "eyf": eyf,
        "hsel": hsel,
        "g2sel": g2sel,
    }
    carr = np.ascontiguousarray(np.asarray(L0_c, np.float32))

    in_maps = []
    for core in range(8):
        b = core // 2
        half = core % 2
        xb = x[b]
        xr = np.concatenate([xb[half * TOWN:(half + 1) * TOWN],
                             xb[(1 - half) * TOWN:(2 - half) * TOWN]], axis=0)
        m = dict(shared)
        m["x"] = np.ascontiguousarray(xr)
        m["cvec"] = np.ascontiguousarray(carr[b].reshape(1, Dm))
        in_maps.append(m)

    res = run_bass_kernel_spmd(nc, in_maps, core_ids=list(range(8)))
    out = np.empty((B, N, Dm), dtype=np.float32)
    for core in range(8):
        b = core // 2
        half = core % 2
        out[b, half * TOWN:(half + 1) * TOWN] = res.results[core]["out"]
    return out


# version 19
# speedup vs baseline: 1.0112x; 1.0015x over previous
from contextlib import ExitStack

import numpy as np
import ml_dtypes
import concourse.bass as bass
import concourse.mybir as mybir
from concourse.bass_utils import run_bass_kernel_spmd

F32 = mybir.dt.float32
BF16 = mybir.dt.bfloat16
ALU = mybir.AluOpType
ACT = mybir.ActivationFunctionType
AXX = mybir.AxisListType
BF = ml_dtypes.bfloat16

D = 1024
J = 8
TFULL = 8192
TOWN = 4096
NTF = 64
NTO = 32
NST = 8
HP = 8
INNER = 4096
NI = 32
NTG = 16
TGW = 256
NS_ITERS = 18

C23 = float(np.float32(1.5 * 2 ** 23))
C52 = float(np.float32(1.5 * 2 ** 52))
EPS_LN = 1e-5


class Prog:
    ENGS = ("sync", "vector", "scalar", "tensor", "gpsimd")
    POOL = {"sa": 8, "sg": 8, "sq": 4, "sw": 8, "sx": 4}

    def __init__(self):
        self.q = {e: [] for e in self.ENGS}
        self.c = {"v": 0, "s": 0, "t": 0, "g": 0}
        for qn, n in self.POOL.items():
            for i in range(n):
                self.c[f"{qn}{i}"] = 0
        self.rr = {qn: 0 for qn in self.POOL}
        self.sem = {}
        self.esem = {"vector": "v", "scalar": "s", "tensor": "t", "gpsimd": "g"}
        self.lastw = {}

    def emit(self, eng, fn):
        self.q[eng].append(fn)

    def wait(self, eng, sem, val):
        if val is None or val <= 0:
            return
        key = (eng, sem)
        if self.lastw.get(key, -1) >= val:
            return
        self.lastw[key] = val
        s = self.sem[sem]
        self.emit(eng, lambda e: e.wait_ge(s, val))

    def dwait(self, eng, tok):
        if tok is None or tok == 0:
            return
        self.qbarrier(eng, tok[0][:2])

    def dwait_tok(self, eng, tok):
        if tok is None or tok == 0:
            return
        self.wait(eng, tok[0], tok[1])

    def dma(self, eng, queue, out, in_, **kw):
        i = self.rr[queue]
        self.rr[queue] = (i + 1) % self.POOL[queue]
        sn = f"{queue}{i}"
        self.wait(eng, sn, self.c[sn])
        s = self.sem[sn]
        self.emit(eng, lambda e: e.dma_start(out=out, in_=in_, **kw).then_inc(s, 16))
        self.c[sn] += 16
        return (sn, self.c[sn])

    def dmat(self, out, in_):
        i = self.rr["sq"]
        self.rr["sq"] = (i + 1) % self.POOL["sq"]
        sn = f"sq{i}"
        self.wait("sync", sn, self.c[sn])
        s = self.sem[sn]
        self.emit("sync", lambda e: e.dma_start_transpose(out, in_).then_inc(s, 16))
        self.c[sn] += 16
        return (sn, self.c[sn])

    def qbarrier(self, eng, queue):
        for i in range(self.POOL[queue]):
            self.wait(eng, f"{queue}{i}", self.c[f"{queue}{i}"])

    def op(self, eng, fn, inc=True, selfwait=True):
        sk = self.esem[eng]
        if selfwait and eng != "tensor":
            self.wait(eng, sk, self.c[sk])
        if inc:
            s = self.sem[sk]
            self.emit(eng, lambda e: fn(e).then_inc(s, 1))
            self.c[sk] += 1
            return self.c[sk]
        self.emit(eng, lambda e: fn(e))
        return None

    def vgap(self):
        pass

    def run(self, eng, e):
        for fn in self.q[eng]:
            fn(e)


def build_nc():
    nc = bass.Bass("TRN2", target_bir_lowering=False, debug=False)

    x_h = nc.dram_tensor("x", [TFULL, D], F32, kind="ExternalInput").ap()
    wq_h = nc.dram_tensor("wq", [D, D], F32, kind="ExternalInput").ap()
    wk_h = nc.dram_tensor("wk", [D, D], F32, kind="ExternalInput").ap()
    wv_h = nc.dram_tensor("wv", [D, D], F32, kind="ExternalInput").ap()
    wo_h = nc.dram_tensor("wo", [D, D], F32, kind="ExternalInput").ap()
    bo_h = nc.dram_tensor("bo", [1, D], F32, kind="ExternalInput").ap()
    alpha_h = nc.dram_tensor("alpha", [1, 1], F32, kind="ExternalInput").ap()
    temp_h = nc.dram_tensor("temp", [1, 1], F32, kind="ExternalInput").ap()
    gamma_h = nc.dram_tensor("gamma", [1, D], F32, kind="ExternalInput").ap()
    beta_h = nc.dram_tensor("beta", [1, D], F32, kind="ExternalInput").ap()
    wg_h = nc.dram_tensor("wg", [INNER, D], F32, kind="ExternalInput").ap()
    wu_h = nc.dram_tensor("wu", [INNER, D], F32, kind="ExternalInput").ap()
    wd_h = nc.dram_tensor("wd", [D, INNER], F32, kind="ExternalInput").ap()
    cvec_h = nc.dram_tensor("cvec", [1, D], F32, kind="ExternalInput").ap()
    wmod_h = nc.dram_tensor("wmod", [D, 6 * D], F32, kind="ExternalInput").ap()
    eye_h = nc.dram_tensor("eye", [128, 128], BF16, kind="ExternalInput").ap()
    eyf_h = nc.dram_tensor("eyf", [128, 64], F32, kind="ExternalInput").ap()
    hsel_h = nc.dram_tensor("hsel", [128, 2], BF16, kind="ExternalInput").ap()
    g2sel_h = nc.dram_tensor("g2sel", [2, 128], BF16, kind="ExternalInput").ap()

    out_h = nc.dram_tensor("out", [TOWN, D], F32, kind="ExternalOutput").ap()

    x2_hbm = nc.dram_tensor("x2s", [TOWN, D], F32).ap()
    h2q_hbm = nc.dram_tensor("h2qs", [NST, 128, NI, 128], BF16).ap()
    yq_hbm = nc.dram_tensor("yqs", [NTG, 128, NI, TGW], BF16).ap()
    wdt_hbm = nc.dram_tensor("wdts", [NI, 128, D], BF16).ap()
    wgq_hbm = nc.dram_tensor("wgq", [2, NI, 128, D], BF16).ap()

    P = Prog()

    with ExitStack() as stack:
        def sb(name, shape, dt):
            return stack.enter_context(nc.sbuf_tensor(name, shape, dt))

        HF = sb("HF", [128, J, TOWN], BF16)
        WAT = sb("WAT", [128, J, 4, J, 128], BF16)
        YTG = sb("YTG", [128, NI, TGW], BF16)
        U = sb("U", [128, 24576], BF16)
        SCR = sb("SCR", [128, D], F32)
        STAT = sb("STAT", [128, 16], F32)
        EYE = sb("EYE", [128, 128], BF16)
        EYF = sb("EYF", [128, 64], F32)
        ONE1 = sb("ONE1", [1, 128], BF16)
        N1R = sb("N1R", [1, 512], BF16)
        ONEC = sb("ONEC", [128, 1], BF16)
        CPX = sb("CPX", [128, 1], F32)
        DDT = sb("DDT", [128, 4], F32)
        CA1 = sb("CA1", [128, J], F32)
        SAF = sb("SAF", [128, J], F32)
        BOF = sb("BOF", [128, J], F32)
        GBF = sb("GBF", [128, J], F32)
        CB1BC = sb("CB1BC", [128, D], F32)
        SBBC = sb("SBBC", [128, D], F32)
        ALP = sb("ALP", [128, 1], F32)
        ITP = sb("ITP", [128, 1], F32)
        CSF = sb("CSF", [128, J], BF16)
        MAPB = sb("MAPB", [128, HP, 64], BF16)
        BST = sb("BST", [128, 8], BF16)
        NSS = sb("NSS", [128, 32], F32)
        HSEL = sb("HSEL", [128, 2], BF16)
        G2SEL = sb("G2SEL", [2, 128], BF16)
        SYR = sb("SYR", [128, NTG * 2], F32)
        SYCOL = sb("SYCOL", [128, 8], F32)
        SGP = sb("SGP", [128, 4], F32)
        SGR = sb("SGR", [128, 4], F32)
        G1F0 = sb("G1F0", [128, 16], F32)
        PS = stack.enter_context(nc.psum_tensor("PS", [128, 8, 512], F32))
        for k in list(P.c.keys()):
            P.sem[k] = stack.enter_context(nc.semaphore(k))
        block = stack.enter_context(nc.Block())

        Uf = U.bitcast(F32)
        XNB = U[:, 0:2048].rearrange("p (b c) -> p b c", b=2)
        HTMP = U[:, 2048:4096].rearrange("p (b j c) -> p b j c", b=2, j=J)
        KTM = U[:, 4096:5120]
        VTM = U[:, 5120:6144]
        XT = Uf[:, 3072:5120].rearrange("p (b c) -> p b c", b=2)
        XT3 = [Uf[:, 3072:4096], Uf[:, 4096:5120], Uf[:, 11264:12288]]
        KKS = Uf[:, 0:1024].rearrange("p (h c) -> p h c", h=HP)
        ABLK = U[:, 2048:3072].rearrange("p (h c) -> p h c", h=HP)
        XNS = U[:, 3072:3584].rearrange("p (h c) -> p h c", h=HP)
        ZNS = U[:, 3584:4096].rearrange("p (h c) -> p h c", h=HP)
        KTVB = U[:, 4096:4608].rearrange("p (h c) -> p h c", h=HP)
        SMB = U[:, 4608:5120].rearrange("p (h c) -> p h c", h=HP)
        X3T = Uf[:, 0:2048].rearrange("p (b c) -> p b c", b=2)
        ST1 = Uf[0:1, 5120:6656]
        H2T1 = U[:, 13312:14336]
        X2T = Uf[:, 7168:9216].rearrange("p (b c) -> p b c", b=2)
        H2FM = U[:, 18432:22528].rearrange("p (i c) -> p i c", i=NI)
        SQB = U[:, 4096:5120].rearrange("p (b c) -> p b c", b=2)
        H2QT = U[:, 0:2048].rearrange("p (t k c) -> p t k c", t=2, k=J)
        H2QT2 = U[:, 2048:4096].rearrange("p (t k c) -> p t k c", t=2, k=J)
        WFNAT = Uf[:, 2048:4096].rearrange("p (b c) -> p b c", b=2)
        WFQ = U[:, 8192:9216]
        GS = U[:, 9216:9216 + TGW]
        UPS = U[:, 9472:9472 + TGW]
        MXT = U[:, 9728:9728 + TGW]
        MNT = U[:, 10752:10752 + TGW]
        SYBS = U[:, 9984:9984 + TGW]
        ZQ = U[:, 10240:10752].rearrange("p (b c) -> p b c", b=2)
        GSB = U[:, 23568:23824]
        UPSB = U[:, 23824:24080]
        YQB = U[:, 10496:11008].rearrange("p (b c) -> p b c", b=2)
        SYKT = U[:, 11008:11016]
        WDQ = U[:, 11280:15376]
        WDNAT = Uf[:, 7688:9736]
        WDDEST = U[:, 19472:23568].rearrange("p (i c) -> p i c", i=NI)
        WDTS = U[:, 4096:6144].rearrange("p (b c) -> p b c", b=2)
        WDTS8 = U[:, 4096:12288].rearrange("p (b c) -> p b c", b=8)
        YQI = U[:, 6144:6656].rearrange("p (b c) -> p b c", b=2)
        ABSS = Uf[:, 9216:9312]
        PREPROW = U[:, 8192:9216][0:1, 0:1024]
        EYFB = U[:, 5120:5184]
        ROWS = STAT[0:2, 0:8]
        ROWSB = BST[0:2, 0:8]
        ST1B = U[:, 5120:6144][0:1, 0:1024]
        SYROW = U[:, 11016:11272][0:1, 0:256]
        G1R = U[:, 22528:23552][0:1, 0:1024]
        G0R = U[:, 23552:24576][0:1, 0:1024]

        QF = WAT[:, :, 1, 0:4, :].rearrange("p k r c -> p k (r c)")
        OF = WAT[:, :, 1, 4:8, :].rearrange("p k r c -> p k (r c)")
        ZBB = WAT[:, :, 2, 0:4, :].rearrange("p k r c -> p k (r c)")
        AOTM = WAT[:, :, 2, 4:8, :]
        AOC = U[:, 6144:10240]
        EPSA = DDT[:, 0:1]
        SDIV = DDT[:, 1:2]
        HFf = HF.rearrange("p j c -> p (j c)").bitcast(F32)
        MODR = HFf[0:1, 0:11 * D]
        CROW = HFf[0:1, 11 * D:12 * D]
        YTGflat = YTG.rearrange("p a b -> p (a b)")
        WNATv = YTGflat.rearrange("p (r f) -> p r f", r=J)
        WMODSv = YTGflat[:, 0:4096].rearrange("p (k c) -> p k c", k=J)

        def bank(i, w=512):
            return PS[:, i, 0:w]

        def bankb(i):
            return PS[:, i, :].bitcast(BF16)

        def st1(k):
            return ST1[0:1, (k % 3) * 512:(k % 3) * 512 + 512]

        def st1b(k):
            return ST1B[0:1, k * 512:(k + 1) * 512]

        ld_eye = P.dma("gpsimd", "sg", EYE[:], eye_h)
        ld_eyf = P.dma("gpsimd", "sg", EYF[:], eyf_h)
        ld_c = P.dma("gpsimd", "sg", CROW, cvec_h)
        P.dma("gpsimd", "sg", MODR[0:1, 6 * D:7 * D], bo_h)
        P.dma("gpsimd", "sg", MODR[0:1, 7 * D:7 * D + 1], alpha_h)
        P.dma("gpsimd", "sg", MODR[0:1, 7 * D + 4:7 * D + 5], temp_h)
        P.dma("gpsimd", "sg", MODR[0:1, 8 * D:9 * D], gamma_h)
        ld_small = P.dma("gpsimd", "sg", MODR[0:1, 9 * D:10 * D], beta_h)
        P.dma("gpsimd", "sg", HSEL[:], hsel_h)
        ld_sel = P.dma("gpsimd", "sg", G2SEL[:], g2sel_h)

        P.op("vector", lambda e: e.memset(DDT[:, 0:1], 2.0 ** -40))
        P.vgap()
        P.op("vector", lambda e: e.tensor_single_scalar(DDT[:, 1:2], DDT[:, 0:1], 1.0, ALU.add))
        P.vgap()
        P.op("vector", lambda e: e.tensor_single_scalar(DDT[:, 2:3], DDT[:, 1:2], 1.0, ALU.subtract))
        P.vgap()
        P.op("vector", lambda e: e.tensor_single_scalar(DDT[:, 3:4], DDT[:, 2:3], 2.0 ** 40, ALU.mult))
        P.vgap()
        P.op("vector", lambda e: e.tensor_scalar(
            CPX[:], DDT[:, 3:4], C52 - C23, C23, ALU.mult, ALU.add))
        P.vgap()
        P.op("vector", lambda e: e.memset(DDT[:, 0:1], EPS_LN))
        P.op("vector", lambda e: e.memset(DDT[:, 1:2], 1.0 / D))
        P.op("vector", lambda e: e.memset(ONE1[:], 1.0))
        P.op("vector", lambda e: e.memset(N1R[:], -1.0))
        P.op("vector", lambda e: e.memset(ONEC[:], 1.0))

        sq_wat = 0
        for wi, wsrc in enumerate((wq_h, wk_h, wv_h, wo_h)):
            if wi > 0:
                P.dwait("gpsimd", sq_wat)
            g_w = P.dma("gpsimd", "sg", WNATv[:, :, :],
                        wsrc.rearrange("(r p) f -> p r f", p=128))
            P.dwait("sync", g_w)
            for r in range(J):
                sq_wat = P.dmat(WAT[:, :, wi, r, :], WNATv[:, r, :])

        P.dwait("scalar", ld_c)
        s_csil = P.op("scalar", lambda e: e.activation(CROW, CROW, ACT.Silu))
        P.wait("tensor", "s", s_csil)
        P.dwait("tensor", ld_eyf)
        for j in range(J):
            P.op("tensor", lambda e, j=j: e.matmul(
                bank(7)[:, j:j + 1], CROW[0:1, j * 128:(j + 1) * 128],
                EYF[0:1, 0:1], start=(j == 0), stop=(j == J - 1)), inc=(j == J - 1))
        t_ctr = P.c["t"]
        P.wait("vector", "t", t_ctr)
        v_csf = P.op("vector", lambda e: e.tensor_copy(CSF[:], bank(7)[:, 0:J]))

        v_modev = v_csf
        P.dwait("gpsimd", sq_wat)
        for n in range(12):
            g_wm = P.dma("gpsimd", "sg", WMODSv,
                         wmod_h.rearrange("(k p) m -> p k m", p=128)[:, :, n * 512:(n + 1) * 512])
            P.dwait("tensor", g_wm)
            P.wait("tensor", "v", v_modev)
            for k in range(J):
                P.op("tensor", lambda e, k=k: e.matmul(
                    bank(6)[0:1, :], CSF[:, k:k + 1], WMODSv[:, k, :],
                    start=(k == 0), stop=(k == J - 1)), inc=(k == J - 1))
            t_mm = P.c["t"]
            P.wait("vector", "t", t_mm)
            v_modev = P.op("vector", lambda e, n=n: e.tensor_copy(
                MODR[0:1, n * 512:(n + 1) * 512], bank(6)[0:1, :]))
            P.wait("gpsimd", "t", t_mm)

        P.wait("scalar", "v", v_modev)
        for base in (2, 5):
            row = MODR[0:1, base * D:(base + 1) * D]
            sgn = MODR[0:1, 10 * D:11 * D]
            P.wait("scalar", "v", P.c["v"])
            P.op("scalar", lambda e, row=row, sgn=sgn: e.activation(sgn, row, ACT.Sign))
            s_sg = P.op("scalar", lambda e, row=row: e.activation(PREPROW, row, ACT.Abs))
            P.wait("vector", "s", s_sg)
            P.op("vector", lambda e: e.tensor_single_scalar(PREPROW, PREPROW, 0.25, ALU.max))
            P.op("vector", lambda e, row=row, sgn=sgn: e.tensor_mul(row, PREPROW, sgn))
        P.op("vector", lambda e: e.tensor_mul(G1R[:], MODR[0:1, 2 * D:3 * D], MODR[0:1, 8 * D:9 * D]))
        v_rows = P.op("vector", lambda e: e.tensor_mul(G0R[:], MODR[0:1, 2 * D:3 * D], MODR[0:1, 9 * D:10 * D]))

        last_fme = v_rows
        for (ri, dstt, addv) in ((1, CA1, 1.0), (0, SAF, 0.0), (6, BOF, 0.0), (5, GBF, 0.0)):
            P.wait("tensor", "v", last_fme)
            for j in range(J):
                P.op("tensor", lambda e, ri=ri, j=j: e.matmul(
                    bank(7)[:, j:j + 1],
                    MODR[0:1, ri * D + j * 128: ri * D + (j + 1) * 128],
                    EYF[0:1, 0:1], start=(j == 0), stop=(j == J - 1)), inc=(j == J - 1))
            t_tr = P.c["t"]
            P.wait("vector", "t", t_tr)
            last_fme = P.op("vector", lambda e, dstt=dstt, addv=addv: e.tensor_scalar(
                dstt[:], bank(7)[:, 0:J], addv, None, ALU.add))

        P.dwait("tensor", ld_eye)
        P.wait("tensor", "v", last_fme)
        for j in range(J):
            P.op("tensor", lambda e, j=j: e.matmul(
                bank(7)[:, j:j + 1], G1R[0:1, j * 128:(j + 1) * 128], EYE[0:1, 0:1],
                start=(j == 0), stop=False), inc=False)
            P.op("tensor", lambda e, j=j: e.matmul(
                bank(7)[:, 8 + j:9 + j], G0R[0:1, j * 128:(j + 1) * 128], EYE[0:1, 0:1],
                start=False, stop=(j == J - 1)), inc=(j == J - 1))
        t_g1f = P.c["t"]
        P.wait("vector", "t", t_g1f)
        P.op("vector", lambda e: e.tensor_copy(G1F0[:], bank(7)[:, 0:16]))
        G1F = G1F0[:, 0:8]
        G0F = G1F0[:, 8:16]
        v_cbx = last_fme
        for dst, ri, addv in ((CB1BC, 4, 1.0), (SBBC, 3, 0.0)):
            for half in range(2):
                P.op("vector", lambda e, ri=ri, addv=addv, half=half: e.tensor_scalar(
                    PREPROW[0:1, 0:512], MODR[0:1, ri * D + half * 512:ri * D + (half + 1) * 512],
                    addv, None, ALU.add))
                v_cbx = P.c["v"]
                P.wait("tensor", "v", v_cbx)
                P.op("tensor", lambda e: e.matmul(
                    bank(5), ONE1[0:1, 0:128], PREPROW[0:1, 0:512], start=True, stop=True))
                t_x = P.c["t"]
                P.wait("vector", "t", t_x)
                v_cbx = P.op("vector", lambda e, dst=dst, half=half: e.tensor_copy(
                    dst[:, half * 512:(half + 1) * 512], bank(5)))

        P.dwait("vector", ld_small)
        P.op("vector", lambda e: e.reciprocal(MODR[0:1, 7 * D + 8:7 * D + 9],
                                              MODR[0:1, 7 * D + 4:7 * D + 5]))
        P.vgap()
        P.op("vector", lambda e: e.tensor_copy(PREPROW[0:1, 0:1], MODR[0:1, 7 * D:7 * D + 1]))
        P.op("vector", lambda e: e.tensor_copy(PREPROW[0:1, 1:2], MODR[0:1, 7 * D + 8:7 * D + 9]))
        v_al = P.c["v"]
        P.wait("tensor", "v", v_al)
        P.op("tensor", lambda e: e.matmul(bank(5)[:, 0:2], ONE1[0:1, 0:128],
                                          PREPROW[0:1, 0:2], start=True, stop=True))
        t_al = P.c["t"]
        P.wait("vector", "t", t_al)
        P.op("vector", lambda e: e.tensor_copy(ALP[:], bank(5)[:, 0:1]))
        v_itp = P.op("vector", lambda e: e.tensor_copy(ITP[:], bank(5)[:, 1:2]))

        P1B = [Uf[:, k * 1024:(k + 1) * 1024] for k in range(8)]

        def p1_src(u):
            if u < 64:
                return ((wg_h, wu_h)[u % 2])[(u // 2) * 128:(u // 2 + 1) * 128, :]
            mm = u - 64
            r, qq = mm // 4, mm % 4
            return wd_h[r * 128:(r + 1) * 128, qq * 1024:(qq + 1) * 1024]

        def p1_col(u):
            if u < 64:
                return (u % 2) * 32 + u // 2
            return 64 + (u - 64)

        P1RING = [("gpsimd", "sg"), ("gpsimd", "sw"), ("scalar", "sa")]
        p1_tok = {}
        s_abs_hist = {}
        P.wait("gpsimd", "t", P.c["t"])
        P.wait("gpsimd", "v", P.c["v"])
        P.wait("scalar", "t", P.c["t"])
        P.wait("scalar", "v", P.c["v"])
        for u0 in range(8):
            er = P1RING[u0 % 3]
            p1_tok[u0] = P.dma(er[0], er[1], P1B[u0][:], p1_src(u0))
        for u in range(96):
            P.dwait_tok("vector", p1_tok[u])
            na = p1_col(u)
            s_abs = P.op("vector", lambda e, u=u, na=na: e.reduce_sum(
                ABSS[:, na:na + 1], P1B[u % 8][:], AXX.X, apply_absolute_value=True))
            s_abs_hist[u] = s_abs
            if u + 8 < 96:
                er = P1RING[(u + 8) % 3]
                P.wait(er[0], "v", s_abs)
                P.dwait_tok(er[0], p1_tok[u])
                p1_tok[u + 8] = P.dma(er[0], er[1], P1B[u % 8][:],
                                      p1_src(u + 8))
        s_abs = s_abs_hist[95]
        for mi, (a, b) in enumerate(((0, 32), (32, 64), (64, 96))):
            P.op("vector", lambda e, a=a, b=b, mi=mi: e.reduce_sum(
                STAT[:, mi:mi + 1], ABSS[:, a:b], AXX.X))
        P.vgap()
        v_ab = P.op("vector", lambda e: e.tensor_copy(BST[:, 0:3], STAT[:, 0:3]))
        P.wait("tensor", "v", v_ab)
        P.op("tensor", lambda e: e.matmul(bank(5)[0:1, 0:3], ONEC[:], BST[:, 0:3],
                                          start=True, stop=True))
        t_ab = P.c["t"]
        P.wait("vector", "t", t_ab)
        P.op("vector", lambda e: e.tensor_scalar(
            st1(0)[0:1, 0:3], bank(5)[0:1, 0:3], 1.0 / (INNER * D), 1e-5,
            ALU.mult, ALU.max))
        P.vgap()
        P.op("vector", lambda e: e.reciprocal(st1(1)[0:1, 0:3], st1(0)[0:1, 0:3]))
        P.vgap()
        P.op("vector", lambda e: e.tensor_copy(ROWSB[0:1, 0:3], st1(1)[0:1, 0:3]))
        v_sc = P.op("vector", lambda e: e.tensor_copy(ROWSB[0:1, 3:6], st1(0)[0:1, 0:3]))
        P.wait("tensor", "v", v_sc)
        P.op("tensor", lambda e: e.matmul(bank(5)[:, 0:6], ONE1[0:1, 0:128],
                                          ROWSB[0:1, 0:6], start=True, stop=True))
        t_sc = P.c["t"]
        P.wait("vector", "t", t_sc)
        P.op("vector", lambda e: e.tensor_copy(SGP[:, 0:3], bank(5)[:, 0:3]))
        v_sgp = P.op("vector", lambda e: e.tensor_copy(SGR[:, 0:3], bank(5)[:, 3:6]))

        def kk_bank(hp):
            b = 6 + hp // 4
            off = (hp % 4) * 128
            return PS[:, b, off:off + 128]

        def stat_a(pbuf, a, b):
            return STAT[:, pbuf * 8 + a:pbuf * 8 + b]

        sq_hist = {}
        v_kv_prev = None
        t_kk_hist = {}
        v_xn_hist = {}

        v_prep_end = P.c["v"]
        s_prep_end = P.c["s"]
        t_prep_end = P.c["t"]

        sq_hist = {}
        v_mod_hist = {}
        ev_v = {}
        ev_s = {}
        t_kk_hist = {}
        v_xn_hist = {}

        def a_prep(ft):
            pbuf = ft % 2
            xb3 = XT3[ft % 3]
            own = ft < NTO
            if ft >= 2:
                P.wait("scalar", "v", v_xn_hist.get(ft - 2, 0))
            P.dwait_tok("scalar", ax_tok[ft])
            P.op("scalar", lambda e, xb3=xb3, pbuf=pbuf: e.activation(
                SCR[:], xb3[:], ACT.Square, accum_out=stat_a(pbuf, 0, 1)))
            s_st = P.op("scalar", lambda e, xb3=xb3, pbuf=pbuf: e.activation(
                SCR[:], xb3[:], ACT.Identity, accum_out=stat_a(pbuf, 1, 2)))
            P.wait("vector", "s", s_st)
            P.op("vector", lambda e, pbuf=pbuf: e.tensor_single_scalar(
                stat_a(pbuf, 2, 3), stat_a(pbuf, 1, 2), 1.0 / D, ALU.mult))
            P.op("vector", lambda e, pbuf=pbuf: e.tensor_single_scalar(
                stat_a(pbuf, 3, 4), stat_a(pbuf, 0, 1), 1.0 / D, ALU.mult))
            P.op("vector", lambda e, pbuf=pbuf: e.tensor_mul(
                stat_a(pbuf, 4, 5), stat_a(pbuf, 2, 3), stat_a(pbuf, 2, 3)))
            v_var = P.op("vector", lambda e, pbuf=pbuf: e.tensor_sub(
                stat_a(pbuf, 5, 6), stat_a(pbuf, 3, 4), stat_a(pbuf, 4, 5)))
            P.wait("scalar", "v", v_var)
            s_sd = P.op("scalar", lambda e, pbuf=pbuf: e.activation(
                stat_a(pbuf, 6, 7), stat_a(pbuf, 5, 6), ACT.Sqrt, bias=EPSA))
            P.wait("vector", "s", s_sd)
            P.op("vector", lambda e, pbuf=pbuf: e.reciprocal(
                stat_a(pbuf, 7, 8), stat_a(pbuf, 6, 7)))
            if ft >= 2:
                P.dwait("vector", sq_hist.get(ft - 2, 0))
            v_xn = P.op("vector", lambda e, xb3=xb3, pbuf=pbuf: e.tensor_scalar(
                XNB[:, pbuf, :], xb3[:], stat_a(pbuf, 2, 3), stat_a(pbuf, 7, 8),
                ALU.subtract, ALU.mult))
            v_xn_hist[ft] = v_xn
            P.wait("sync", "v", v_xn)
            if own:
                dst3 = HF.rearrange("p j (t c) -> p j t c", c=128)[:, :, ft, :]
            else:
                if ft >= NTO + 2:
                    P.wait("sync", "t", t_kv_hist[ft - 2])
                dst3 = HTMP[:, pbuf, :, :]
            sq_t = P.dmat(dst3, XNB[:, pbuf, :])
            sq_hist[ft] = sq_t
            P.dwait("scalar", sq_t)
            for j in range(J):
                P.op("scalar", lambda e, dst3=dst3, j=j: e.activation(
                    dst3[:, j, :], dst3[:, j, :], ACT.Identity,
                    bias=SAF[:, j:j + 1], scale=CA1[:, j:j + 1]),
                    inc=(j == J - 1))
            v_mod_hist[ft] = P.c["s"]
            return dst3

        WQNA = [Uf[:, 5120 + k * 1024:5120 + (k + 1) * 1024] for k in range(4)]
        WFQA = [U[:, 18432:19456], U[:, 19456:20480]]
        WGSTG = [U[:, 20480:21504], U[:, 21504:22528]]
        wqa_tok = {}
        sqa_tok = {}
        sta_tok = {}
        vfin_hist = {}

        def wqa_src(m):
            if m < 64:
                return ((wg_h, wu_h)[m % 2])[(m // 2) * 128:(m // 2 + 1) * 128, :]
            mm = m - 64
            r, qq = mm // 4, mm % 4
            return wd_h[r * 128:(r + 1) * 128, qq * 1024:(qq + 1) * 1024]

        def emit_wq_unit(m):
            buf = WQNA[m % 4]
            P.dwait_tok("vector", wqa_tok[m])
            if m < 64:
                sp = SGP[:, m % 2:m % 2 + 1]
                sr = SGR[:, m % 2:m % 2 + 1]
            else:
                sp = SGP[:, 2:3]
                sr = SGR[:, 2:3]
            P.op("vector", lambda e, buf=buf, sp=sp: e.tensor_scalar(
                buf[:], buf[:], sp, CPX[:, 0:1], ALU.mult, ALU.add))
            P.op("vector", lambda e, buf=buf: e.tensor_scalar(
                buf[:], buf[:], CPX[:, 0:1], 1.0, ALU.subtract, ALU.min))
            if m >= 64:
                r = (m - 64) // 4
                P.op("vector", lambda e, buf=buf, r=r: e.tensor_scalar(
                    buf[:], buf[:], -1.0, GBF[:, r:r + 1], ALU.max, ALU.mult))
            if m >= 2:
                P.dwait_tok("vector", sqa_tok[m - 2])
            if m >= 64:
                vfin = P.op("vector", lambda e, m=m, buf=buf, sr=sr: e.tensor_scalar(
                    WFQA[m % 2][:], buf[:], sr, None, ALU.mult))
            else:
                vfin = P.op("vector", lambda e, m=m, buf=buf, sr=sr: e.tensor_scalar(
                    WFQA[m % 2][:], buf[:], -1.0, sr, ALU.max, ALU.mult))
            vfin_hist[m] = vfin
            if m + 4 < 96:
                P.wait("gpsimd", "v", vfin)
                P.dwait_tok("gpsimd", wqa_tok[m])
                wqa_tok[m + 4] = P.dma("gpsimd", ["sg", "sw"][m % 2],
                                       WQNA[m % 4][:], wqa_src(m + 4))
            P.wait("sync", "v", vfin)
            if m >= 2:
                P.dwait_tok("sync", sta_tok[m - 2])
            sqa_tok[m] = P.dmat(
                WGSTG[m % 2].rearrange("p (a c) -> p a c", c=128), WFQA[m % 2][:])
            P.dwait_tok("sync", sqa_tok[m])
            if m < 64:
                sta_tok[m] = P.dma("sync", "sx", wgq_hbm[m % 2, m // 2],
                                   WGSTG[m % 2][:])
            else:
                r, qq = (m - 64) // 4, (m - 64) % 4
                sta_tok[m] = P.dma(
                    "sync", "sx",
                    wdt_hbm[8 * qq:8 * (qq + 1), :, r * 128:(r + 1) * 128].rearrange(
                        "i p c -> p i c"),
                    WGSTG[m % 2].rearrange("p (a c) -> p a c", c=128))

        P.wait("gpsimd", "v", v_sgp)
        for uu in range(88, 96):
            P.dwait_tok("gpsimd", p1_tok[uu])
        for m0 in range(4):
            wqa_tok[m0] = P.dma("gpsimd", ["sg", "sw"][m0 % 2],
                                WQNA[m0 % 4][:], wqa_src(m0))
        wu_i = 0

        t_kv_hist = {}
        dst3s = {}
        ax_tok = {}
        P.wait("scalar", "v", max(v_prep_end, v_sgp))
        P.wait("scalar", "t", P.c["t"])
        for f0 in range(3):
            ax_tok[f0] = P.dma("scalar", "sa", XT3[f0][:],
                               x_h[f0 * 128:(f0 + 1) * 128, :])
        dst3s[0] = a_prep(0)
        dst3s[1] = a_prep(1)
        for ft in range(NTF):
            if ft + 3 < NTF:
                P.wait("scalar", "v", v_xn_hist.get(ft, 0))
                ax_tok[ft + 3] = P.dma(
                    "scalar", "sa", XT3[(ft + 3) % 3][:],
                    x_h[(ft + 3) * 128:(ft + 4) * 128, :])
            dst3 = dst3s.pop(ft)
            P.wait("tensor", "s", v_mod_hist[ft])
            if ft >= 1:
                P.wait("tensor", "s", ev_s.get(ft - 1, 0))
            for half in range(2):
                kb = bank(0 if half == 0 else 2)
                vb = bank(1 if half == 0 else 3)
                for k in range(J):
                    P.op("tensor", lambda e, k=k, half=half, dst3=dst3, kb=kb: e.matmul(
                        kb, dst3[:, k, :], WAT[:, k, 1, half * 4:(half + 1) * 4, :],
                        start=(k == 0), stop=(k == J - 1)), inc=False)
                    P.op("tensor", lambda e, k=k, half=half, dst3=dst3, vb=vb: e.matmul(
                        vb, dst3[:, k, :], WAT[:, k, 2, half * 4:(half + 1) * 4, :],
                        start=(k == 0), stop=(k == J - 1)), inc=(k == J - 1))
            t_kv = P.c["t"]
            t_kv_hist[ft] = t_kv
            P.wait("scalar", "t", t_kv)
            P.op("scalar", lambda e: e.copy(KTM[:, 0:512], bank(0)))
            ev_s1 = P.op("scalar", lambda e: e.copy(VTM[:, 0:512], bank(1)))
            P.op("scalar", lambda e: e.copy(KTM[:, 512:1024], bank(2)))
            ev_s[ft] = P.op("scalar", lambda e: e.copy(VTM[:, 512:1024], bank(3)))
            ev_v[ft] = ev_s[ft]
            ev_v1 = ev_s1
            if ft + 2 < NTF:
                dst3s[ft + 2] = a_prep(ft + 2)
            st_flag = (ft == 0)
            sp_flag = (ft == NTF - 1)
            for hp in range(HP):
                if hp == 0:
                    P.wait("tensor", "s", ev_s1)
                elif hp == 4:
                    P.wait("tensor", "s", ev_s[ft])
                kkb = kk_bank(hp)
                first = st_flag and hp % 4 == 0
                last = sp_flag and hp % 4 == 3
                P.op("tensor", lambda e, hp=hp, kkb=kkb, first=first: e.matmul(
                    kkb[0:64, 0:64], KTM[:, hp * 128:hp * 128 + 64],
                    KTM[:, hp * 128:hp * 128 + 64],
                    start=first, stop=False, tile_position=(0, 0)), inc=False)
                P.op("tensor", lambda e, hp=hp, kkb=kkb, last=last: e.matmul(
                    kkb[0:64, 64:128], KTM[:, hp * 128:hp * 128 + 64],
                    VTM[:, hp * 128:hp * 128 + 64],
                    start=False, stop=last, tile_position=(0, 0)), inc=False)
                P.op("tensor", lambda e, hp=hp, kkb=kkb, first=first: e.matmul(
                    kkb[64:128, 0:64], KTM[:, hp * 128 + 64:hp * 128 + 128],
                    KTM[:, hp * 128 + 64:hp * 128 + 128],
                    start=first, stop=False, tile_position=(0, 64)), inc=False)
                P.op("tensor", lambda e, hp=hp, kkb=kkb, last=last: e.matmul(
                    kkb[64:128, 64:128], KTM[:, hp * 128 + 64:hp * 128 + 128],
                    VTM[:, hp * 128 + 64:hp * 128 + 128],
                    start=False, stop=last, tile_position=(0, 64)),
                    inc=(hp == HP - 1))
            t_kk_hist[ft] = P.c["t"]
            while wu_i < ((ft + 1) * 96) // 64:
                emit_wq_unit(wu_i)
                wu_i += 1

        t_kkdone = P.c["t"]

        def nss(i, a, b):
            return NSS[:, i * 8 + a:i * 8 + b]

        P.wait("vector", "t", t_kkdone)
        P.op("vector", lambda e: e.tensor_copy(EYFB[:], EYF[:]))
        for hp in range(HP):
            P.op("vector", lambda e, hp=hp: e.scalar_tensor_tensor(
                KKS[:, hp, 0:64], EYF[:], ALP[:, 0:1], kk_bank(hp)[:, 0:64],
                ALU.mult, ALU.add))
            P.op("vector", lambda e, hp=hp: e.tensor_copy(
                KKS[:, hp, 64:128], kk_bank(hp)[:, 64:128]))
        for hp in range(HP):
            P.op("vector", lambda e, hp=hp: e.tensor_copy(KTVB[:, hp, :], KKS[:, hp, 64:128]))
        for hp in range(HP):
            P.op("vector", lambda e, hp=hp: e.tensor_mul(
                SCR[:, hp * 64:(hp + 1) * 64], KKS[:, hp, 0:64], KKS[:, hp, 0:64]))
        P.vgap()
        for hp in range(HP):
            P.op("vector", lambda e, hp=hp: e.reduce_sum(
                nss(0, hp, hp + 1), SCR[:, hp * 64:(hp + 1) * 64], AXX.X))
        P.vgap()
        v_fro = P.op("vector", lambda e: e.tensor_copy(BST[:, 0:8], nss(0, 0, 8)))
        P.wait("tensor", "v", v_fro)
        P.dwait("tensor", ld_sel)
        P.op("tensor", lambda e: e.matmul(bank(5)[0:2, 0:8], HSEL[:], BST[:, 0:8],
                                          start=True, stop=True))
        t_fro = P.c["t"]
        P.wait("vector", "t", t_fro)
        P.op("vector", lambda e: e.reciprocal(ROWS[:, 0:8], bank(5)[0:2, 0:8]))
        P.vgap()
        v_fr2 = P.op("vector", lambda e: e.tensor_copy(ROWSB[:, 0:8], ROWS[:, 0:8]))
        P.wait("tensor", "v", v_fr2)
        P.op("tensor", lambda e: e.matmul(bank(5)[:, 0:8], G2SEL[:], ROWSB[:, 0:8],
                                          start=True, stop=True))
        t_a0 = P.c["t"]
        P.wait("vector", "t", t_a0)
        P.op("vector", lambda e: e.tensor_copy(nss(1, 0, 8), bank(5)[:, 0:8]))
        P.vgap()
        for hp in range(HP):
            P.op("vector", lambda e, hp=hp: e.tensor_scalar(
                XNS[:, hp, :], KKS[:, hp, 0:64], nss(1, hp, hp + 1), None, ALU.mult))
        for hp in range(HP):
            P.op("vector", lambda e, hp=hp: e.memset(ABLK[:, hp, :], 0.0))
        for hp in range(HP):
            P.op("vector", lambda e, hp=hp: e.tensor_copy(
                ABLK[0:64, hp, 0:64], KKS[0:64, hp, 0:64]))
            P.op("vector", lambda e, hp=hp: e.tensor_copy(
                ABLK[64:128, hp, 64:128], KKS[64:128, hp, 0:64]))

        for it in range(NS_ITERS):
            P.wait("tensor", "v", P.c["v"])
            for hp in range(HP):
                P.op("tensor", lambda e, hp=hp: e.matmul(
                    bank(4)[:, hp * 64:(hp + 1) * 64], ABLK[:, hp, :], XNS[:, hp, :],
                    start=(hp == 0), stop=(hp == HP - 1)), inc=(hp == HP - 1))
            t_y = P.c["t"]
            P.wait("vector", "t", t_y)
            for hp in range(HP):
                P.op("vector", lambda e, hp=hp: e.scalar_tensor_tensor(
                    ZNS[:, hp, :], EYFB[:], 2.0, bank(4)[:, hp * 64:(hp + 1) * 64],
                    ALU.mult, ALU.subtract))
            v_z = P.c["v"]
            P.wait("tensor", "v", v_z)
            for hp in range(HP):
                P.op("tensor", lambda e, hp=hp: e.matmul(
                    bank(5)[0:64, hp * 64:(hp + 1) * 64], XNS[0:64, hp, :],
                    ZNS[0:64, hp, :], start=(hp == 0), stop=(hp == HP - 1),
                    tile_position=(0, 0)), inc=False)
                P.op("tensor", lambda e, hp=hp: e.matmul(
                    bank(5)[64:128, hp * 64:(hp + 1) * 64], XNS[64:128, hp, :],
                    ZNS[64:128, hp, :], start=(hp == 0), stop=(hp == HP - 1),
                    tile_position=(64, 64)), inc=(hp == HP - 1))
            t_x2i = P.c["t"]
            P.wait("vector", "t", t_x2i)
            for hp in range(HP):
                P.op("vector", lambda e, hp=hp: e.tensor_copy(
                    XNS[:, hp, :], bank(5)[:, hp * 64:(hp + 1) * 64]))

        P.wait("tensor", "v", P.c["v"])
        for hp in range(HP):
            P.op("tensor", lambda e, hp=hp: e.matmul(
                bank(4)[0:64, hp * 64:(hp + 1) * 64], KTVB[0:64, hp, :],
                XNS[0:64, hp, :], start=(hp == 0), stop=(hp == HP - 1),
                tile_position=(0, 0)), inc=False)
            P.op("tensor", lambda e, hp=hp: e.matmul(
                bank(4)[64:128, hp * 64:(hp + 1) * 64], KTVB[64:128, hp, :],
                XNS[64:128, hp, :], start=(hp == 0), stop=(hp == HP - 1),
                tile_position=(64, 64)), inc=(hp == HP - 1))
        t_zt = P.c["t"]
        P.wait("vector", "t", t_zt)
        P.wait("vector", "v", v_itp)
        for hp in range(HP):
            P.op("vector", lambda e, hp=hp: e.tensor_scalar(
                SCR[:, hp * 64:(hp + 1) * 64], bank(4)[:, hp * 64:(hp + 1) * 64],
                ITP[:, 0:1], None, ALU.mult))
        P.vgap()
        for hp in range(HP):
            P.op("vector", lambda e, hp=hp: e.reduce_max(
                nss(1, hp, hp + 1), SCR[:, hp * 64:(hp + 1) * 64], AXX.X))
        P.vgap()
        v_mx = P.op("vector", lambda e: e.tensor_single_scalar(
            nss(0, 0, 8), nss(1, 0, 8), -1.0, ALU.mult))
        P.wait("scalar", "v", v_mx)
        for hp in range(HP):
            P.op("scalar", lambda e, hp=hp: e.activation(
                SCR[:, hp * 64:(hp + 1) * 64], SCR[:, hp * 64:(hp + 1) * 64],
                ACT.Exp, bias=nss(0, hp, hp + 1)))
        s_exp = P.c["s"]
        P.wait("vector", "s", s_exp)
        for hp in range(HP):
            P.op("vector", lambda e, hp=hp: e.reduce_sum(
                nss(2, hp, hp + 1), SCR[:, hp * 64:(hp + 1) * 64], AXX.X))
        P.vgap()
        P.op("vector", lambda e: e.reciprocal(nss(3, 0, 8), nss(2, 0, 8)))
        P.vgap()
        for hp in range(HP):
            P.op("vector", lambda e, hp=hp: e.tensor_scalar(
                SMB[:, hp, :], SCR[:, hp * 64:(hp + 1) * 64], nss(3, hp, hp + 1),
                None, ALU.mult))
        v_sm = P.c["v"]
        P.wait("tensor", "v", v_sm)
        v_mev = {}
        for hp in range(HP):
            bb = 4 + hp % 2
            if hp >= 2:
                P.wait("tensor", "v", v_mev[hp - 2])
            P.op("tensor", lambda e, hp=hp, bb=bb: e.transpose(
                bankb(bb)[0:64, 0:64], SMB[0:64, hp, :],
                EYE[0:64, 0:64]), inc=False)
            t_tp = P.op("tensor", lambda e, hp=hp, bb=bb: e.transpose(
                bankb(bb)[64:128, 64:128], SMB[64:128, hp, :],
                EYE[64:128, 64:128], tile_position=(64, 64)))
            P.wait("vector", "t", t_tp)
            P.op("vector", lambda e, hp=hp, bb=bb: e.tensor_copy(
                MAPB[0:64, hp, :], bankb(bb)[0:64, 0:64]), inc=False)
            v_mev[hp] = P.op("vector", lambda e, hp=hp, bb=bb: e.tensor_copy(
                MAPB[64:128, hp, :], bankb(bb)[64:128, 64:128]))
        v_map = P.c["v"]

        sa_h2q_st = {}
        sq_h2_hist = {}
        XTC = YTG.rearrange("p a b -> p (a b)").bitcast(F32).rearrange(
            "p (b c) -> p b c", b=4)
        v_x2_hist = {}
        v_ln_hist = {}
        xtc_tok = {}
        x2st_tok = {}
        P.g_h2_prev = 0
        P.wait("scalar", "v", P.c["v"])
        P.wait("sync", "v", P.c["v"])
        P.wait("gpsimd", "v", P.c["v"])
        P.wait("gpsimd", "t", P.c["t"])
        P.wait("gpsimd", "s", P.c["s"])
        P.wait("tensor", "v", P.c["v"])
        P.wait("tensor", "s", P.c["s"])
        P.wait("tensor", "g", P.c["g"])
        v_ao_prev = P.c["v"]
        s_zbb7_prev = P.c["s"]
        s_oev7_prev = P.c["s"]
        qb_of = [bank(0), bank(1), bank(6), bank(7)]
        X2T4 = [Uf[:, 7168:8192], Uf[:, 8192:9216],
                Uf[:, 1024:2048], Uf[:, 11264:12288]]
        MU2 = SYCOL[:, 0:2]
        VB2 = SYCOL[:, 2:4]
        SD2 = SYCOL[:, 4:6]
        RS2 = SYCOL[:, 6:8]
        sq_ao_hist = {}
        va_hist = {}
        sqrt_hist = {}
        P.qbarrier("scalar", "sq")
        P.qbarrier("sync", "sx")
        P.qbarrier("sync", "sq")

        def t4_partA(st, t4):
            ft = st * 4 + t4
            P.dwait_tok("vector", xtc_tok[(st, t4)])
            if ft >= 4:
                P.dwait_tok("vector", x2st_tok[ft - 4])
            P.dwait("vector", sq_ao_hist[st])
            aosl = AOTM[:, :, t4, :]
            v_x2 = P.op("vector", lambda e, t4=t4, aosl=aosl: e.tensor_add(
                X2T4[t4][:], XTC[:, t4, :], aosl))
            v_x2_hist[(st, t4)] = v_x2
            P.wait("sync", "v", v_x2)
            x2st_tok[ft] = P.dma("sync", "sx",
                                 x2_hbm[ft * 128:(ft + 1) * 128, :], X2T4[t4][:])
            P.op("vector", lambda e, t4=t4: e.tensor_mul(
                SCR[:], X2T4[t4][:], X2T4[t4][:]))
            P.op("vector", lambda e: e.reduce_sum(stat_a(0, 0, 1), SCR[:], AXX.X))
            P.op("vector", lambda e, t4=t4: e.reduce_sum(
                stat_a(0, 1, 2), X2T4[t4][:], AXX.X))
            P.op("vector", lambda e, t4=t4: e.tensor_single_scalar(
                MU2[:, t4 % 2:t4 % 2 + 1], stat_a(0, 1, 2), 1.0 / D, ALU.mult))
            P.op("vector", lambda e: e.tensor_single_scalar(
                stat_a(0, 3, 4), stat_a(0, 0, 1), 1.0 / D, ALU.mult))
            P.op("vector", lambda e, t4=t4: e.tensor_mul(
                stat_a(0, 4, 5), MU2[:, t4 % 2:t4 % 2 + 1], MU2[:, t4 % 2:t4 % 2 + 1]))
            va_hist[(st, t4)] = P.op("vector", lambda e, t4=t4: e.tensor_sub(
                VB2[:, t4 % 2:t4 % 2 + 1], stat_a(0, 3, 4), stat_a(0, 4, 5)))

        def t4_sqrt(st, pair):
            P.wait("scalar", "v", va_hist[(st, 2 * pair + 1)])
            sqrt_hist[(st, pair)] = P.op("scalar", lambda e: e.activation(
                SD2[:, 0:2], VB2[:, 0:2], ACT.Sqrt, bias=EPSA))

        def t4_partB(st, t4):
            ft = st * 4 + t4
            P.wait("vector", "s", sqrt_hist[(st, t4 // 2)])
            P.op("vector", lambda e, t4=t4: e.reciprocal(
                RS2[:, t4 % 2:t4 % 2 + 1], SD2[:, t4 % 2:t4 % 2 + 1]))
            v_ln = P.op("vector", lambda e, t4=t4: e.tensor_scalar(
                SCR[:], X2T4[t4][:], MU2[:, t4 % 2:t4 % 2 + 1],
                RS2[:, t4 % 2:t4 % 2 + 1], ALU.subtract, ALU.mult))
            P.op("vector", lambda e: e.tensor_mul(SCR[:], SCR[:], CB1BC[:]))
            P.vgap()
            P.op("vector", lambda e: e.tensor_add(X3T[:, 0, :], SCR[:], SBBC[:]))
            P.vgap()
            P.op("vector", lambda e: e.reduce_max(stat_a(1, 0, 1), X3T[:, 0, :], AXX.X,
                                                  apply_absolute_value=True))
            P.vgap()
            P.op("vector", lambda e: e.reciprocal(stat_a(1, 1, 2), stat_a(1, 0, 1)))
            P.vgap()
            P.op("vector", lambda e: e.tensor_single_scalar(
                stat_a(1, 2, 3), stat_a(1, 1, 2), 127.0, ALU.mult))
            P.op("vector", lambda e: e.tensor_single_scalar(
                stat_a(1, 3, 4), stat_a(1, 0, 1), 1.0 / 127.0, ALU.mult))
            P.op("vector", lambda e, t4=t4: e.tensor_scalar(
                SCR[:], X3T[:, 0, :], stat_a(1, 2, 3), CPX[:, 0:1],
                ALU.mult, ALU.add))
            P.vgap()
            v_h2q = P.op("vector", lambda e, t4=t4: e.tensor_scalar(
                H2T1[:], SCR[:], CPX[:, 0:1], stat_a(1, 3, 4),
                ALU.subtract, ALU.mult))
            P.wait("sync", "v", v_h2q)
            if t4 == 0 and st >= 1:
                P.dwait("sync", sa_h2q_st[st - 1])
            sq_h2 = P.dmat(H2FM[:, t4 * 8:(t4 + 1) * 8, :], H2T1[:])
            sq_h2_hist[st] = sq_h2
            P.dwait("vector", sq_h2)

        def t4_store(st):
            P.dwait("sync", sq_h2_hist[st])
            sa_h2q_st[st] = P.dma("sync", "sx", h2q_hbm[st], H2FM[:])

        for t4 in range(4):
            xtc_tok[(0, t4)] = P.dma(
                "scalar", "sa", XTC[:, t4, :], x_h[t4 * 128:(t4 + 1) * 128, :])
        for st in range(NST):
            cols = slice(st * 512, (st + 1) * 512)
            if st >= 1:
                t4_partA(st - 1, 0)
                t4_partA(st - 1, 1)
            s_qev = {}
            P.wait("tensor", "s", s_zbb7_prev)
            for j in range(J):
                qb = qb_of[j % 4]
                if j >= 4:
                    P.wait("tensor", "s", s_qev[j - 4])
                for k in range(J):
                    P.op("tensor", lambda e, j=j, k=k, qb=qb, cols=cols: e.matmul(
                        qb, WAT[:, k, 0, j, :], HF[:, k, cols],
                        start=(k == 0), stop=(k == J - 1)), inc=(k == J - 1))
                t_q = P.c["t"]
                P.wait("scalar", "t", t_q)
                s_qev[j] = P.op("scalar", lambda e, j=j, qb=qb: e.activation(
                    QF[:, j, :], qb, ACT.Copy))
            s_oev = {}
            P.wait("tensor", "s", s_oev7_prev)
            for j in range(J):
                ob = bank(2 + j % 2)
                if j >= 2:
                    P.wait("tensor", "s", s_oev[j - 2])
                P.wait("tensor", "s", s_qev[j])
                P.op("tensor", lambda e, j=j, ob=ob: e.matmul(
                    ob[0:64, :], MAPB[0:64, j, :], QF[0:64, j, :],
                    start=True, stop=True, tile_position=(0, 0)), inc=False)
                P.op("tensor", lambda e, j=j, ob=ob: e.matmul(
                    ob[64:128, :], MAPB[64:128, j, :], QF[64:128, j, :],
                    start=True, stop=True, tile_position=(64, 64)), inc=True)
                t_o = P.c["t"]
                P.wait("scalar", "t", t_o)
                s_oev[j] = P.op("scalar", lambda e, j=j, ob=ob: e.activation(
                    OF[:, j, :], ob, ACT.Copy))
            s_oev7_prev = s_oev[J - 1]
            if st >= 1:
                t4_sqrt(st - 1, 0)
                t4_partB(st - 1, 0)
                t4_partB(st - 1, 1)
            s_zbb = {}
            s_sq = {}
            t_stats = {}
            P.wait("tensor", "v", v_ao_prev)
            P.wait("tensor", "s", s_oev[J - 1])
            for j in range(J):
                wb = qb_of[j % 4]
                if j < 4:
                    P.wait("tensor", "s", s_qev[j + 4])
                else:
                    P.wait("tensor", "s", s_zbb[j - 4])
                for k in range(J):
                    P.op("tensor", lambda e, j=j, k=k, wb=wb: e.matmul(
                        wb, WAT[:, k, 3, j, :], OF[:, k, :],
                        start=(k == 0), stop=False), inc=False)
                P.op("tensor", lambda e, j=j, wb=wb, cols=cols: e.matmul(
                    wb, EYE[:], HF[:, j, cols], start=False, stop=True), inc=True)
                t_w = P.c["t"]
                if j >= 2:
                    P.wait("scalar", "t", t_stats[j - 2])
                P.wait("scalar", "t", t_w)
                s_zbb[j] = P.op("scalar", lambda e, j=j, wb=wb: e.activation(
                    ZBB[:, j, :], wb, ACT.Identity, bias=BOF[:, j:j + 1]))
                s_sq[j] = P.op("scalar", lambda e, j=j: e.activation(
                    SQB[:, j % 2, :], ZBB[:, j, :], ACT.Square))
                P.wait("tensor", "s", s_sq[j])
                P.op("tensor", lambda e, j=j: e.matmul(
                    bank(4)[0:1, :], ONEC[:], ZBB[:, j, :],
                    start=(j == 0), stop=(j == J - 1)), inc=False)
                P.op("tensor", lambda e, j=j: e.matmul(
                    bank(5)[0:1, :], ONEC[:], SQB[:, j % 2, :],
                    start=(j == 0), stop=(j == J - 1)), inc=(j == J - 1))
                t_stats[j] = P.c["t"]
            s_zbb7_prev = s_sq[J - 1]
            t_st = P.c["t"]
            P.wait("vector", "t", t_st)
            P.op("vector", lambda e: e.tensor_single_scalar(st1(0), bank(4)[0:1, :], 1.0 / D, ALU.mult))
            P.op("vector", lambda e: e.tensor_single_scalar(st1(1), bank(5)[0:1, :], 1.0 / D, ALU.mult))
            P.vgap()
            P.op("vector", lambda e: e.tensor_mul(st1(2), st1(0), st1(0)))
            P.vgap()
            v_vr = P.op("vector", lambda e: e.tensor_sub(st1(1), st1(1), st1(2)))
            P.wait("scalar", "v", v_vr)
            s_sd2 = P.op("scalar", lambda e: e.activation(st1(2), st1(1), ACT.Sqrt, bias=EPSA[0:1, 0:1]))
            P.wait("vector", "s", s_sd2)
            P.op("vector", lambda e: e.reciprocal(st1(1), st1(2)))
            P.vgap()
            P.op("vector", lambda e: e.tensor_mul(st1(2), st1(0), st1(1)))
            P.vgap()
            P.op("vector", lambda e: e.tensor_copy(st1b(0), st1(1)))
            v_r2 = P.op("vector", lambda e: e.tensor_copy(st1b(1), st1(2)))
            P.wait("tensor", "v", v_r2)
            P.op("tensor", lambda e: e.matmul(
                bank(4), ONE1[0:1, 0:128], st1b(1), start=True, stop=True), inc=False)
            P.op("tensor", lambda e: e.matmul(
                bank(5), ONE1[0:1, 0:128], st1b(0), start=True, stop=True), inc=True)
            t_bc = P.c["t"]
            if st >= 1:
                t4_partA(st - 1, 2)
                t4_partA(st - 1, 3)
                t4_sqrt(st - 1, 1)
            P.wait("vector", "t", t_bc)
            if st >= 1:
                P.dwait("vector", sq_ao_hist[st - 1])
            for j in range(J):
                P.op("vector", lambda e, j=j: e.tensor_mul(
                    SCR[:, 0:512], ZBB[:, j, :], bank(5)))
                P.op("vector", lambda e, j=j: e.tensor_sub(
                    SCR[:, 0:512], SCR[:, 0:512], bank(4)))
                vj = P.op("vector", lambda e, j=j: e.tensor_scalar(
                    AOC[:, j * 512:(j + 1) * 512], SCR[:, 0:512],
                    G1F[:, j:j + 1], G0F[:, j:j + 1], ALU.mult, ALU.add))
                P.wait("sync", "v", vj)
                sq_ao = P.dmat(AOTM[:, j, :, :], AOC[:, j * 512:(j + 1) * 512])
            v_ao = P.c["v"]
            v_ao_prev = v_ao
            sq_ao_hist[st] = sq_ao
            if st >= 1:
                t4_partB(st - 1, 2)
                t4_partB(st - 1, 3)
                t4_store(st - 1)
            for t4 in range(4):
                if st >= 1:
                    P.wait("scalar", "v", v_x2_hist[(st - 1, t4)])
                    xtc_tok[(st, t4)] = P.dma(
                        "scalar", "sa", XTC[:, t4, :],
                        x_h[(st * 4 + t4) * 128:(st * 4 + t4 + 1) * 128, :])

        t4_partA(NST - 1, 0)
        t4_partA(NST - 1, 1)
        t4_sqrt(NST - 1, 0)
        t4_partB(NST - 1, 0)
        t4_partB(NST - 1, 1)
        t4_partA(NST - 1, 2)
        t4_partA(NST - 1, 3)
        t4_sqrt(NST - 1, 1)
        t4_partB(NST - 1, 2)
        t4_partB(NST - 1, 3)
        t4_store(NST - 1)

        WGT = HF
        WUT = WAT.rearrange("p a b c d -> p (a b c d)").rearrange(
            "p (k m) -> p k m", k=J)
        wgt_r = WGT.rearrange("p k (i c) -> p k i c", c=128)
        wut_r = WUT.rearrange("p k (i c) -> p k i c", c=128)
        h2q_r = h2q_hbm.rearrange("s p (t k) c -> s p t k c", k=J)
        s_gs = None
        v_up = None
        g_y_prev = 0
        s_up_hist = {}
        g_y_hist = {}
        s_up_save = {}
        g_y_save = {}
        sq_wq_last = None
        v_wq_by = {0: 0, 1: 0}
        sg_yq = {}
        t_gu_hist = {}
        P.wait("gpsimd", "v", P.c["v"])
        P.wait("gpsimd", "t", P.c["t"])
        P.wait("gpsimd", "s", P.c["s"])
        P.qbarrier("gpsimd", "sq")
        P.qbarrier("gpsimd", "sa")
        P.qbarrier("gpsimd", "sx")
        P.qbarrier("scalar", "sx")
        P.wait("scalar", "g", P.c["g"])
        P.wait("scalar", "v", P.c["v"])
        P.wait("scalar", "t", P.c["t"])
        P.wait("sync", "t", P.c["t"])
        P.wait("sync", "s", P.c["s"])
        WQN8 = [Uf[:, 2048:3072], Uf[:, 3072:4096], Uf[:, 5640:6664],
                Uf[:, 6664:7688], Uf[:, 7688:8712], Uf[:, 8712:9736],
                Uf[:, 9736:10760]]
        SCRQ2 = Uf[:, 10760:11784]
        WFQ2 = [WFQ, U[:, 2048:3072]]

        def wsrc_of(m):
            return ((wg_h, wu_h)[m % 2])[(m // 2) * 128:(m // 2 + 1) * 128, :]

        wq_tok = {}
        sq_wq_hist = {}
        g_wq_hist = {}
        g_wq_last = 0
        sa_h2t_tok = {}
        sa_h2t_tok[0] = P.dma("scalar", "sa", H2QT[:, :, :, :],
                              h2q_r[0, :, 0:2, :, :])
        for tg in range(1):
            if tg >= 1:
                P.dwait("gpsimd", sg_yq[tg - 1])
            sa_h2t = sa_h2t_tok[tg]
            h2b = H2QT if tg % 2 == 0 else H2QT2
            if tg == 0:
                wst_tok = {}
                for ii in range(4):
                    for which in range(2):
                        wst_tok[2 * ii + which] = P.dma(
                            "scalar", "sa", (wgt_r, wut_r)[which][:, :, ii, :],
                            wgq_hbm[which, ii])
            for i in range(NI):
                if tg == 0:
                    P.dwait_tok("tensor", wst_tok[2 * i])
                    P.dwait_tok("tensor", wst_tok[2 * i + 1])
                P.dwait_tok("tensor", sa_h2t)
                gb = bank(0 + 2 * (i % 2), TGW)
                ub = bank(1 + 2 * (i % 2), TGW)
                GSi = GS if i % 2 == 0 else GSB
                UPSi = UPS if i % 2 == 0 else UPSB
                if i >= 2:
                    P.wait("tensor", "s", s_up_save[i - 2])
                elif tg >= 1:
                    P.wait("tensor", "s", s_up_hist[30 + i])
                for k in range(J):
                    rhs = h2b[:, :, k, :]
                    P.op("tensor", lambda e, i=i, k=k, rhs=rhs, gb=gb: e.matmul(
                        gb, wgt_r[:, k, i, :], rhs,
                        start=(k == 0), stop=(k == J - 1)), inc=False)
                    P.op("tensor", lambda e, i=i, k=k, rhs=rhs, ub=ub: e.matmul(
                        ub, wut_r[:, k, i, :], rhs,
                        start=(k == 0), stop=(k == J - 1)), inc=(k == J - 1))
                t_gu = P.c["t"]
                if tg == 0 and i + 4 < NI:
                    P.wait("scalar", "t", t_gu)
                    P.dwait_tok("scalar", wst_tok[2 * i])
                    P.dwait_tok("scalar", wst_tok[2 * i + 1])
                    for which in range(2):
                        wst_tok[2 * (i + 4) + which] = P.dma(
                            "scalar", "sa", (wgt_r, wut_r)[which][:, :, i + 4, :],
                            wgq_hbm[which, i + 4])
                P.wait("scalar", "t", t_gu)
                if i >= 2:
                    P.wait("scalar", "g", g_y_save[i - 2])
                elif tg >= 1:
                    P.wait("scalar", "g", g_y_hist[30 + i])
                s_gs = P.op("scalar", lambda e, GSi=GSi, gb=gb: e.activation(
                    GSi[:], gb, ACT.Silu))
                s_up = P.op("scalar", lambda e, UPSi=UPSi, ub=ub: e.activation(
                    UPSi[:], ub, ACT.Copy))
                s_up_save[i] = s_up
                P.wait("gpsimd", "s", s_up)
                g_y = P.op("gpsimd", lambda e, i=i, GSi=GSi, UPSi=UPSi: e.tensor_mul(
                    YTG[:, i, :], GSi[:], UPSi[:]))
                g_y_save[i] = g_y
                g_y_prev = g_y
                P.wait("vector", "g", g_y)
                if i == 0:
                    P.op("vector", lambda e: e.tensor_copy(MXT[:], YTG[:, 0, :]))
                    P.op("vector", lambda e: e.tensor_copy(MNT[:], YTG[:, 0, :]))
                else:
                    P.op("vector", lambda e, i=i: e.tensor_max(
                        MXT[:], MXT[:], YTG[:, i, :]))
                    P.op("vector", lambda e, i=i: e.tensor_tensor(
                        MNT[:], MNT[:], YTG[:, i, :], ALU.min))
            s_up_hist = s_up_save
            g_y_hist = g_y_save
            s_up_save = {}
            g_y_save = {}
            t_gu_hist[tg] = P.c["t"]
            if tg + 1 < NTG:
                if tg >= 1:
                    P.wait("scalar", "t", t_gu_hist[tg - 1])
                h2bn = H2QT if (tg + 1) % 2 == 0 else H2QT2
                sa_h2t_tok[tg + 1] = P.dma(
                    "scalar", "sa", h2bn[:, :, :, :],
                    h2q_r[(tg + 1) // 2, :, 2 * ((tg + 1) % 2):2 * ((tg + 1) % 2) + 2, :, :])
            P.op("vector", lambda e: e.tensor_single_scalar(
                MNT[:], MNT[:], -1.0, ALU.mult))
            P.op("vector", lambda e: e.tensor_max(MXT[:], MXT[:], MNT[:]))
            g_ymax = P.c["v"]
            P.wait("tensor", "v", g_ymax)
            for t4 in range(2):
                bb = 4 + t4
                P.op("tensor", lambda e, t4=t4, bb=bb: e.transpose(
                    bankb(bb)[:, 0:128], MXT[:, t4 * 128:(t4 + 1) * 128], EYE[:]),
                    inc=True)
            t_mx = P.c["t"]
            P.wait("vector", "t", t_mx)
            for t4 in range(2):
                bb = 4 + t4
                P.op("vector", lambda e, t4=t4, bb=bb: e.reduce_max(
                    SYCOL[:, t4:t4 + 1], bankb(bb)[:, 0:128], AXX.X))
            for t4 in range(2):
                P.op("vector", lambda e, t4=t4: e.reciprocal(
                    SYCOL[:, t4 + 4:t4 + 5], SYCOL[:, t4:t4 + 1]))
            P.vgap()
            for t4 in range(2):
                P.op("vector", lambda e, t4=t4, tg=tg: e.tensor_single_scalar(
                    SYR[:, tg * 2 + t4:tg * 2 + t4 + 1], SYCOL[:, t4:t4 + 1],
                    1.0 / 127.0, ALU.mult))
                P.op("vector", lambda e, t4=t4: e.tensor_single_scalar(
                    SYCOL[:, t4:t4 + 1], SYCOL[:, t4 + 4:t4 + 5], 127.0, ALU.mult))
            P.vgap()
            v_syb = P.op("vector", lambda e: e.tensor_copy(SYKT[:, 0:2], SYCOL[:, 0:2]))
            P.wait("tensor", "v", v_syb)
            for t4 in range(2):
                P.op("tensor", lambda e, t4=t4: e.matmul(
                    bank(5)[0:1, t4 * 128:(t4 + 1) * 128], SYKT[:, t4:t4 + 1], EYE[:],
                    start=(t4 == 0), stop=(t4 == 1)), inc=(t4 == 1))
            t_syr = P.c["t"]
            P.wait("vector", "t", t_syr)
            v_row = P.op("vector", lambda e: e.tensor_copy(SYROW, bank(5)[0:1, 0:TGW]))
            P.wait("tensor", "v", v_row)
            P.op("tensor", lambda e: e.matmul(bank(5, TGW), ONE1[0:1, 0:128],
                                              SYROW, start=True, stop=True))
            t_syb = P.c["t"]
            P.wait("vector", "t", t_syb)
            v_sybs = P.op("vector", lambda e: e.tensor_copy(SYBS[:], bank(5, TGW)))
            P.wait("gpsimd", "v", v_sybs)
            v_rnd = {}
            for i in range(NI):
                if i >= 2:
                    P.wait("gpsimd", "v", v_rnd[i - 2])
                g_z = P.op("gpsimd", lambda e, i=i: e.tensor_mul(
                    ZQ[:, i % 2, :], YTG[:, i, :], SYBS[:]))
                P.wait("vector", "g", g_z)
                v_rnd[i] = P.op("vector", lambda e, i=i: e.tensor_scalar(
                    YTG[:, i, :], ZQ[:, i % 2, :], CPX[:, 0:1], CPX[:, 0:1],
                    ALU.add, ALU.subtract))
            v_yq = P.c["v"]
            P.wait("gpsimd", "v", v_yq)
            sg_yq[tg] = P.dma("gpsimd", "sg", yq_hbm[tg], YTG[:, :, :])

        v_yq_prev = v_yq
        g_zq_prev = P.c["g"]
        YTGflat2 = YTG.rearrange("p a b -> p (a b)")
        GS2 = [U[:, 9216:11264].rearrange("p (a c) -> p a c", c=512),
               U[:, 11272:13320].rearrange("p (a c) -> p a c", c=512)]
        YSTG = [U[:, 18432:19456].rearrange("p (a c) -> p a c", c=128),
                U[:, 19456:20480].rearrange("p (a c) -> p a c", c=128)]
        su2 = {}
        gy2 = {}
        yst_tok_prev = {}
        for tg in range(1, NTG):
            P.dwait("gpsimd", sg_yq.get(tg - 1, 0))
            P.qbarrier("gpsimd", "sx")
            h2b = H2QT if tg % 2 == 0 else H2QT2
            P.dwait_tok("tensor", sa_h2t_tok[tg])
            for t in range(2):
                for bb in range(4):
                    u = t * 4 + bb
                    gu = tg * 8 + u
                    bs = 4 * (u % 2)
                    bks = [bank(bs), bank(bs + 1), bank(bs + 2), bank(bs + 3)]
                    if gu >= 10:
                        P.wait("tensor", "s", su2.get(gu - 2, 0))
                    elif u == 0:
                        P.wait("tensor", "s", s_up_hist[31])
                    else:
                        P.wait("tensor", "v", v_yq_prev)
                    for k in range(J):
                        lhs = h2b[:, t, k, :]
                        for bi in range(2):
                            b = 2 * bb + bi
                            P.op("tensor", lambda e, lhs=lhs, b=b, bi=bi, k=k, bks=bks: e.matmul(
                                bks[bi], lhs, WGT[:, k, b * 512:(b + 1) * 512],
                                start=(k == 0), stop=(k == J - 1)), inc=False)
                            P.op("tensor", lambda e, lhs=lhs, b=b, bi=bi, k=k, bks=bks: e.matmul(
                                bks[2 + bi], lhs, WUT[:, k, b * 512:(b + 1) * 512],
                                start=(k == 0), stop=(k == J - 1)),
                                inc=(k == J - 1 and bi == 1))
                    t_gu = P.c["t"]
                    P.wait("scalar", "t", t_gu)
                    if gu >= 10:
                        P.wait("scalar", "g", gy2.get(gu - 2, 0))
                    elif tg == 1:
                        P.wait("scalar", "g", g_zq_prev)
                        P.wait("scalar", "v", v_yq_prev)
                    gsb = GS2[u % 2]
                    for bi in range(2):
                        P.op("scalar", lambda e, gsb=gsb, bi=bi, bks=bks: e.activation(
                            gsb[:, bi, :], bks[bi], ACT.Silu), inc=False)
                        P.op("scalar", lambda e, gsb=gsb, bi=bi, bks=bks: e.activation(
                            gsb[:, 2 + bi, :], bks[2 + bi], ACT.Copy),
                            inc=(bi == 1))
                    s_up2 = P.c["s"]
                    su2[gu] = s_up2
                    P.wait("gpsimd", "s", s_up2)
                    for bi in range(2):
                        b = 2 * bb + bi
                        P.op("gpsimd", lambda e, t=t, b=b, bi=bi, gsb=gsb: e.tensor_mul(
                            YTGflat2[:, t * 4096 + b * 512:t * 4096 + (b + 1) * 512],
                            gsb[:, bi, :], gsb[:, 2 + bi, :]), inc=(bi == 1))
                    g_y2 = P.c["g"]
                    gy2[gu] = g_y2
                    P.wait("vector", "g", g_y2)
                    for bi in range(2):
                        b = 2 * bb + bi
                        if u == 0 and bi == 0:
                            P.op("vector", lambda e, t=t, b=b: e.reduce_max(
                                SYCOL[:, 4 + t:5 + t],
                                YTGflat2[:, t * 4096 + b * 512:t * 4096 + (b + 1) * 512],
                                AXX.X, apply_absolute_value=True))
                        else:
                            P.op("vector", lambda e, t=t, b=b: e.reduce_max(
                                SYCOL[:, 6:7],
                                YTGflat2[:, t * 4096 + b * 512:t * 4096 + (b + 1) * 512],
                                AXX.X, apply_absolute_value=True))
                            P.vgap()
                            P.op("vector", lambda e, t=t: e.tensor_max(
                                SYCOL[:, 4 + t:5 + t], SYCOL[:, 4 + t:5 + t],
                                SYCOL[:, 6:7]))
                yt = YTGflat2[:, t * 4096:(t + 1) * 4096]
                P.op("vector", lambda e, t=t, tg=tg: e.tensor_single_scalar(
                    SYR[:, 2 * tg + t:2 * tg + t + 1], SYCOL[:, 4 + t:5 + t],
                    1.0 / 127.0, ALU.mult))
                P.op("vector", lambda e, t=t: e.reciprocal(
                    SYCOL[:, 7:8], SYCOL[:, 4 + t:5 + t]))
                P.vgap()
                P.op("vector", lambda e: e.tensor_single_scalar(
                    SYCOL[:, 7:8], SYCOL[:, 7:8], 127.0, ALU.mult))
                P.vgap()
                P.op("vector", lambda e, yt=yt: e.tensor_scalar(
                    yt, yt, SYCOL[:, 7:8], None, ALU.mult))
                P.vgap()
                v_rq = P.op("vector", lambda e, yt=yt: e.tensor_scalar(
                    yt, yt, CPX[:, 0:1], CPX[:, 0:1], ALU.add, ALU.subtract))
                P.wait("sync", "v", v_rq)
                yst_tok = {}
                for cc in range(4):
                    if cc >= 2:
                        P.dwait_tok("sync", yst_tok[cc - 2])
                    elif t == 1 or tg >= 2:
                        P.dwait_tok("sync", yst_tok_prev.get(cc + 2, 0))
                    sqy = P.dmat(YSTG[cc % 2][:, :, :],
                                 yt[:, cc * 1024:(cc + 1) * 1024])
                    P.dwait_tok("sync", sqy)
                    yst_tok[cc] = P.dma(
                        "sync", "sx",
                        yq_hbm[tg][:, 8 * cc:8 * cc + 8, t * 128:(t + 1) * 128],
                        YSTG[cc % 2][:, :, :])
                yst_tok_prev = yst_tok
                if t == 1:
                    v_yq_prev = v_rq
                    g_zq_prev = gy2[tg * 8 + 7]
            t_gu_hist[tg] = P.c["t"]
            if tg + 1 < NTG:
                P.wait("scalar", "t", t_gu_hist[tg - 1])
                h2bn = H2QT if (tg + 1) % 2 == 0 else H2QT2
                sa_h2t_tok[tg + 1] = P.dma(
                    "scalar", "sa", h2bn[:, :, :, :],
                    h2q_r[(tg + 1) // 2, :, 2 * ((tg + 1) % 2):2 * ((tg + 1) % 2) + 2, :, :])
            sg_yq[tg] = 0

        sa_out_hist = {}
        P.wait("gpsimd", "v", P.c["v"])
        P.wait("gpsimd", "s", P.c["s"])
        P.qbarrier("gpsimd", "sg")
        P.qbarrier("gpsimd", "sx")
        P.wait("scalar", "v", P.c["v"])
        P.wait("scalar", "g", P.c["g"])
        P.qbarrier("scalar", "sg")
        P.qbarrier("scalar", "sq")
        P.qbarrier("sync", "sa")
        P.qbarrier("sync", "sx")
        WDTS4 = U[:, 4096:8192].rearrange("p (b c) -> p b c", b=4)
        YTG2B = [YTG.rearrange("p a b -> p (a b)"), U[:, 8192:16384]]
        X2TB = Uf[:, 9216:11264].rearrange("p (b c) -> p b c", b=2)
        t_dn_prev = 0
        v_x3_prev = 0
        y_tok = {}
        for j in range(2):
            y_tok[j] = P.dma("gpsimd", ("sg", "sw")[j], YTG2B[j][:], yq_hbm[j])
        for pp in range(8):
            P.wait("tensor", "v", v_x3_prev)
            P.dwait_tok("tensor", y_tok[0])
            P.dwait_tok("tensor", y_tok[1])
            wds_tok = {}
            for w0 in range(4):
                eng0 = ("scalar", "sa") if w0 % 2 == 0 else ("gpsimd", "sg")
                P.wait(eng0[0], "t", t_dn_prev)
                wds_tok[w0] = P.dma(eng0[0], eng0[1], WDTS4[:, w0, :], wdt_hbm[w0])
            yv = [YTG2B[0].rearrange("p (i c) -> p i c", c=256),
                  YTG2B[1].rearrange("p (i c) -> p i c", c=256)]
            t_dn_i = {}
            for i in range(NI):
                P.dwait_tok("tensor", wds_tok[i])
                for j in range(2):
                    for t4 in range(2):
                        dpb = PS[:, 4 * j + 2 * t4:4 * j + 2 * t4 + 2, :].rearrange(
                            "p a b -> p (a b)")
                        for half in range(2):
                            P.op("tensor", lambda e, i=i, j=j, t4=t4, half=half, dpb=dpb, yv=yv: e.matmul(
                                dpb[:, half * 512:(half + 1) * 512],
                                yv[j][:, i, t4 * 128:(t4 + 1) * 128],
                                WDTS4[:, i % 4, half * 512:(half + 1) * 512],
                                start=(i == 0), stop=(i == NI - 1)),
                                inc=(half == 1 and t4 == 1 and j == 1))
                t_dn_i[i] = P.c["t"]
                if i + 4 < NI:
                    eng0 = ("scalar", "sa") if (i + 4) % 2 == 0 else ("gpsimd", "sg")
                    P.wait(eng0[0], "t", t_dn_i[i])
                    if i >= 1:
                        P.dwait_tok(eng0[0], wds_tok[i - 1])
                    wds_tok[i + 4] = P.dma(eng0[0], eng0[1],
                                           WDTS4[:, (i + 4) % 4, :], wdt_hbm[i + 4])
            t_dn = P.c["t"]
            t_dn_prev = t_dn
            if pp + 1 < 8:
                for j in range(2):
                    P.wait("gpsimd", "t", t_dn)
                    P.dwait_tok("gpsimd", y_tok[j])
                    y_tok[j] = P.dma("gpsimd", ("sg", "sw")[j], YTG2B[j][:],
                                     yq_hbm[2 * (pp + 1) + j])
            P.wait("vector", "t", t_dn)
            for j in range(2):
                tg = 2 * pp + j
                for t4 in range(2):
                    ft = tg * 2 + t4
                    P.wait("sync", "v", P.c["v"])
                    sa_x2r = P.dma("sync", "sq", X2TB[:, t4, :],
                                   x2_hbm[ft * 128:(ft + 1) * 128, :])
                    P.dwait("vector", sa_x2r)
                    if ft >= 2:
                        P.dwait("vector", sa_out_hist.get(ft - 2, 0))
                    dpb = PS[:, 4 * j + 2 * t4:4 * j + 2 * t4 + 2, :].rearrange(
                        "p a b -> p (a b)")
                    v_x3 = P.op("vector", lambda e, t4=t4, tg=tg, dpb=dpb: e.scalar_tensor_tensor(
                        X3T[:, t4 % 2, :], dpb, SYR[:, tg * 2 + t4:tg * 2 + t4 + 1],
                        X2TB[:, t4, :], ALU.mult, ALU.add))
                    P.wait("sync", "v", v_x3)
                    v_x3_prev = v_x3
                    sa_out_hist[ft] = P.dma("sync", "sq",
                                            out_h[ft * 128:(ft + 1) * 128, :],
                                            X3T[:, t4 % 2, :])

        P.qbarrier("sync", "sa")
        P.qbarrier("sync", "sg")
        P.qbarrier("sync", "sq")
        P.qbarrier("sync", "sw")
        P.qbarrier("sync", "sx")
        P.wait("sync", "v", P.c["v"])
        P.wait("sync", "t", P.c["t"])
        P.wait("sync", "s", P.c["s"])
        P.wait("sync", "g", P.c["g"])

        @block.sync
        def _(e):
            P.run("sync", e)

        @block.vector
        def _(e):
            P.run("vector", e)

        @block.scalar
        def _(e):
            P.run("scalar", e)

        @block.tensor
        def _(e):
            P.run("tensor", e)

        @block.gpsimd
        def _(e):
            P.run("gpsimd", e)

    return nc


_NC_CACHE = None


def kernel(L0_attn_x, L0_attn_Wq, L0_attn_Wk, L0_attn_Wv, L0_attn_Wo,
           L0_attn_bo, L0_attn_alpha, L0_attn_temperature, L0_attn_gamma,
           L0_attn_beta, L0_ffn_w_gate, L0_ffn_w_up, L0_ffn_w_down,
           L0_c, L0_wmod):
    global _NC_CACHE
    x = np.ascontiguousarray(np.asarray(L0_attn_x, dtype=np.float32))
    B, N, Dm = x.shape
    if _NC_CACHE is None:
        _NC_CACHE = build_nc()
    nc = _NC_CACHE

    eye = np.eye(128, dtype=np.float32).astype(BF)
    eyf = np.concatenate([np.eye(64, dtype=np.float32)] * 2, axis=0)
    hsel = np.zeros((128, 2), np.float32)
    hsel[0:64, 0] = 1.0
    hsel[64:128, 1] = 1.0
    hsel = hsel.astype(BF)
    g2sel = np.zeros((2, 128), np.float32)
    g2sel[0, 0:64] = 1.0
    g2sel[1, 64:128] = 1.0
    g2sel = g2sel.astype(BF)

    shared = {
        "wq": np.ascontiguousarray(L0_attn_Wq, np.float32),
        "wk": np.ascontiguousarray(L0_attn_Wk, np.float32),
        "wv": np.ascontiguousarray(L0_attn_Wv, np.float32),
        "wo": np.ascontiguousarray(L0_attn_Wo, np.float32),
        "bo": np.ascontiguousarray(np.asarray(L0_attn_bo, np.float32).reshape(1, Dm)),
        "alpha": np.ascontiguousarray(np.asarray(L0_attn_alpha, np.float32).reshape(1, 1)),
        "temp": np.ascontiguousarray(np.asarray(L0_attn_temperature, np.float32).reshape(1, 1)),
        "gamma": np.ascontiguousarray(np.asarray(L0_attn_gamma, np.float32).reshape(1, Dm)),
        "beta": np.ascontiguousarray(np.asarray(L0_attn_beta, np.float32).reshape(1, Dm)),
        "wg": np.ascontiguousarray(L0_ffn_w_gate, np.float32),
        "wu": np.ascontiguousarray(L0_ffn_w_up, np.float32),
        "wd": np.ascontiguousarray(L0_ffn_w_down, np.float32),
        "wmod": np.ascontiguousarray(L0_wmod, np.float32),
        "eye": eye,
        "eyf": eyf,
        "hsel": hsel,
        "g2sel": g2sel,
    }
    carr = np.ascontiguousarray(np.asarray(L0_c, np.float32))

    in_maps = []
    for core in range(8):
        b = core // 2
        half = core % 2
        xb = x[b]
        xr = np.concatenate([xb[half * TOWN:(half + 1) * TOWN],
                             xb[(1 - half) * TOWN:(2 - half) * TOWN]], axis=0)
        m = dict(shared)
        m["x"] = np.ascontiguousarray(xr)
        m["cvec"] = np.ascontiguousarray(carr[b].reshape(1, Dm))
        in_maps.append(m)

    res = run_bass_kernel_spmd(nc, in_maps, core_ids=list(range(8)))
    out = np.empty((B, N, Dm), dtype=np.float32)
    for core in range(8):
        b = core // 2
        half = core % 2
        out[b, half * TOWN:(half + 1) * TOWN] = res.results[core]["out"]
    return out


# version 20
# speedup vs baseline: 1.0171x; 1.0059x over previous
from contextlib import ExitStack

import numpy as np
import ml_dtypes
import concourse.bass as bass
import concourse.mybir as mybir
from concourse.bass_utils import run_bass_kernel_spmd

F32 = mybir.dt.float32
BF16 = mybir.dt.bfloat16
ALU = mybir.AluOpType
ACT = mybir.ActivationFunctionType
AXX = mybir.AxisListType
BF = ml_dtypes.bfloat16

D = 1024
J = 8
TFULL = 8192
TOWN = 4096
NTF = 64
NTO = 32
NST = 8
HP = 8
INNER = 4096
NI = 32
NTG = 16
TGW = 256
NS_ITERS = 18

C23 = float(np.float32(1.5 * 2 ** 23))
C52 = float(np.float32(1.5 * 2 ** 52))
EPS_LN = 1e-5


class Prog:
    ENGS = ("sync", "vector", "scalar", "tensor", "gpsimd")
    POOL = {"sa": 8, "sg": 8, "sq": 4, "sw": 8, "sx": 4}

    def __init__(self):
        self.q = {e: [] for e in self.ENGS}
        self.c = {"v": 0, "s": 0, "t": 0, "g": 0}
        for qn, n in self.POOL.items():
            for i in range(n):
                self.c[f"{qn}{i}"] = 0
        self.rr = {qn: 0 for qn in self.POOL}
        self.sem = {}
        self.esem = {"vector": "v", "scalar": "s", "tensor": "t", "gpsimd": "g"}
        self.lastw = {}

    def emit(self, eng, fn):
        self.q[eng].append(fn)

    def wait(self, eng, sem, val):
        if val is None or val <= 0:
            return
        key = (eng, sem)
        if self.lastw.get(key, -1) >= val:
            return
        self.lastw[key] = val
        s = self.sem[sem]
        self.emit(eng, lambda e: e.wait_ge(s, val))

    def dwait(self, eng, tok):
        if tok is None or tok == 0:
            return
        self.qbarrier(eng, tok[0][:2])

    def dwait_tok(self, eng, tok):
        if tok is None or tok == 0:
            return
        self.wait(eng, tok[0], tok[1])

    def dma(self, eng, queue, out, in_, **kw):
        i = self.rr[queue]
        self.rr[queue] = (i + 1) % self.POOL[queue]
        sn = f"{queue}{i}"
        self.wait(eng, sn, self.c[sn])
        s = self.sem[sn]
        self.emit(eng, lambda e: e.dma_start(out=out, in_=in_, **kw).then_inc(s, 16))
        self.c[sn] += 16
        return (sn, self.c[sn])

    def dmat(self, out, in_):
        i = self.rr["sq"]
        self.rr["sq"] = (i + 1) % self.POOL["sq"]
        sn = f"sq{i}"
        self.wait("sync", sn, self.c[sn])
        s = self.sem[sn]
        self.emit("sync", lambda e: e.dma_start_transpose(out, in_).then_inc(s, 16))
        self.c[sn] += 16
        return (sn, self.c[sn])

    def qbarrier(self, eng, queue):
        for i in range(self.POOL[queue]):
            self.wait(eng, f"{queue}{i}", self.c[f"{queue}{i}"])

    def op(self, eng, fn, inc=True, selfwait=True):
        sk = self.esem[eng]
        if selfwait and eng != "tensor":
            self.wait(eng, sk, self.c[sk])
        if inc:
            s = self.sem[sk]
            self.emit(eng, lambda e: fn(e).then_inc(s, 1))
            self.c[sk] += 1
            return self.c[sk]
        self.emit(eng, lambda e: fn(e))
        return None

    def vgap(self):
        pass

    def run(self, eng, e):
        for fn in self.q[eng]:
            fn(e)


def build_nc():
    nc = bass.Bass("TRN2", target_bir_lowering=False, debug=False)

    x_h = nc.dram_tensor("x", [TFULL, D], F32, kind="ExternalInput").ap()
    wq_h = nc.dram_tensor("wq", [D, D], F32, kind="ExternalInput").ap()
    wk_h = nc.dram_tensor("wk", [D, D], F32, kind="ExternalInput").ap()
    wv_h = nc.dram_tensor("wv", [D, D], F32, kind="ExternalInput").ap()
    wo_h = nc.dram_tensor("wo", [D, D], F32, kind="ExternalInput").ap()
    bo_h = nc.dram_tensor("bo", [1, D], F32, kind="ExternalInput").ap()
    alpha_h = nc.dram_tensor("alpha", [1, 1], F32, kind="ExternalInput").ap()
    temp_h = nc.dram_tensor("temp", [1, 1], F32, kind="ExternalInput").ap()
    gamma_h = nc.dram_tensor("gamma", [1, D], F32, kind="ExternalInput").ap()
    beta_h = nc.dram_tensor("beta", [1, D], F32, kind="ExternalInput").ap()
    wg_h = nc.dram_tensor("wg", [INNER, D], F32, kind="ExternalInput").ap()
    wu_h = nc.dram_tensor("wu", [INNER, D], F32, kind="ExternalInput").ap()
    wd_h = nc.dram_tensor("wd", [D, INNER], F32, kind="ExternalInput").ap()
    cvec_h = nc.dram_tensor("cvec", [1, D], F32, kind="ExternalInput").ap()
    wmod_h = nc.dram_tensor("wmod", [D, 6 * D], F32, kind="ExternalInput").ap()
    eye_h = nc.dram_tensor("eye", [128, 128], BF16, kind="ExternalInput").ap()
    eyf_h = nc.dram_tensor("eyf", [128, 64], F32, kind="ExternalInput").ap()
    hsel_h = nc.dram_tensor("hsel", [128, 2], BF16, kind="ExternalInput").ap()
    g2sel_h = nc.dram_tensor("g2sel", [2, 128], BF16, kind="ExternalInput").ap()

    out_h = nc.dram_tensor("out", [TOWN, D], F32, kind="ExternalOutput").ap()

    x2_hbm = nc.dram_tensor("x2s", [TOWN, D], F32).ap()
    h2q_hbm = nc.dram_tensor("h2qs", [NST, 128, NI, 128], BF16).ap()
    yq_hbm = nc.dram_tensor("yqs", [NTG, 128, NI, TGW], BF16).ap()
    wdt_hbm = nc.dram_tensor("wdts", [NI, 128, D], BF16).ap()
    wgq_hbm = nc.dram_tensor("wgq", [2, NI, 128, D], BF16).ap()

    P = Prog()

    with ExitStack() as stack:
        def sb(name, shape, dt):
            return stack.enter_context(nc.sbuf_tensor(name, shape, dt))

        HF = sb("HF", [128, J, TOWN], BF16)
        WAT = sb("WAT", [128, J, 4, J, 128], BF16)
        YTG = sb("YTG", [128, NI, TGW], BF16)
        U = sb("U", [128, 24576], BF16)
        SCR = sb("SCR", [128, D], F32)
        STAT = sb("STAT", [128, 16], F32)
        EYE = sb("EYE", [128, 128], BF16)
        EYF = sb("EYF", [128, 64], F32)
        ONE1 = sb("ONE1", [1, 128], BF16)
        N1R = sb("N1R", [1, 512], BF16)
        ONEC = sb("ONEC", [128, 1], BF16)
        CPX = sb("CPX", [128, 1], F32)
        DDT = sb("DDT", [128, 4], F32)
        CA1 = sb("CA1", [128, J], F32)
        SAF = sb("SAF", [128, J], F32)
        BOF = sb("BOF", [128, J], F32)
        GBF = sb("GBF", [128, J], F32)
        CB1BC = sb("CB1BC", [128, D], F32)
        SBBC = sb("SBBC", [128, D], F32)
        ALP = sb("ALP", [128, 1], F32)
        ITP = sb("ITP", [128, 1], F32)
        CSF = sb("CSF", [128, J], BF16)
        MAPB = sb("MAPB", [128, HP, 64], BF16)
        BST = sb("BST", [128, 8], BF16)
        NSS = sb("NSS", [128, 32], F32)
        HSEL = sb("HSEL", [128, 2], BF16)
        G2SEL = sb("G2SEL", [2, 128], BF16)
        SYR = sb("SYR", [128, NTG * 2], F32)
        SYCOL = sb("SYCOL", [128, 8], F32)
        SGP = sb("SGP", [128, 4], F32)
        SGR = sb("SGR", [128, 4], F32)
        G1F0 = sb("G1F0", [128, 16], F32)
        PS = stack.enter_context(nc.psum_tensor("PS", [128, 8, 512], F32))
        for k in list(P.c.keys()):
            P.sem[k] = stack.enter_context(nc.semaphore(k))
        block = stack.enter_context(nc.Block())

        Uf = U.bitcast(F32)
        XNB = U[:, 0:2048].rearrange("p (b c) -> p b c", b=2)
        HTMP = U[:, 2048:4096].rearrange("p (b j c) -> p b j c", b=2, j=J)
        KTM = U[:, 4096:5120]
        VTM = U[:, 5120:6144]
        XT = Uf[:, 3072:5120].rearrange("p (b c) -> p b c", b=2)
        XT3 = [Uf[:, 3072:4096], Uf[:, 4096:5120], Uf[:, 11264:12288]]
        KKS = Uf[:, 0:1024].rearrange("p (h c) -> p h c", h=HP)
        ABLK = U[:, 2048:3072].rearrange("p (h c) -> p h c", h=HP)
        XNS = U[:, 3072:3584].rearrange("p (h c) -> p h c", h=HP)
        ZNS = U[:, 3584:4096].rearrange("p (h c) -> p h c", h=HP)
        KTVB = U[:, 4096:4608].rearrange("p (h c) -> p h c", h=HP)
        SMB = U[:, 4608:5120].rearrange("p (h c) -> p h c", h=HP)
        X3T = Uf[:, 0:2048].rearrange("p (b c) -> p b c", b=2)
        ST1 = Uf[0:1, 5120:6656]
        H2T1 = U[:, 13312:14336]
        X2T = Uf[:, 7168:9216].rearrange("p (b c) -> p b c", b=2)
        H2FM = U[:, 18432:22528].rearrange("p (i c) -> p i c", i=NI)
        SQB = U[:, 4096:5120].rearrange("p (b c) -> p b c", b=2)
        H2QT = U[:, 0:2048].rearrange("p (t k c) -> p t k c", t=2, k=J)
        H2QT2 = U[:, 2048:4096].rearrange("p (t k c) -> p t k c", t=2, k=J)
        WFNAT = Uf[:, 2048:4096].rearrange("p (b c) -> p b c", b=2)
        WFQ = U[:, 8192:9216]
        GS = U[:, 9216:9216 + TGW]
        UPS = U[:, 9472:9472 + TGW]
        MXT = U[:, 9728:9728 + TGW]
        MNT = U[:, 10752:10752 + TGW]
        SYBS = U[:, 9984:9984 + TGW]
        ZQ = U[:, 10240:10752].rearrange("p (b c) -> p b c", b=2)
        GSB = U[:, 23568:23824]
        UPSB = U[:, 23824:24080]
        YQB = U[:, 10496:11008].rearrange("p (b c) -> p b c", b=2)
        SYKT = U[:, 11008:11016]
        WDQ = U[:, 11280:15376]
        WDNAT = Uf[:, 7688:9736]
        WDDEST = U[:, 19472:23568].rearrange("p (i c) -> p i c", i=NI)
        WDTS = U[:, 4096:6144].rearrange("p (b c) -> p b c", b=2)
        WDTS8 = U[:, 4096:12288].rearrange("p (b c) -> p b c", b=8)
        YQI = U[:, 6144:6656].rearrange("p (b c) -> p b c", b=2)
        ABSS = Uf[:, 9216:9312]
        PREPROW = U[:, 8192:9216][0:1, 0:1024]
        EYFB = U[:, 5120:5184]
        ROWS = STAT[0:2, 0:8]
        ROWSB = BST[0:2, 0:8]
        ST1B = U[:, 5120:6144][0:1, 0:1024]
        SYROW = U[:, 11016:11272][0:1, 0:256]
        G1R = U[:, 22528:23552][0:1, 0:1024]
        G0R = U[:, 23552:24576][0:1, 0:1024]

        QF = WAT[:, :, 1, 0:4, :].rearrange("p k r c -> p k (r c)")
        OF = WAT[:, :, 1, 4:8, :].rearrange("p k r c -> p k (r c)")
        ZBB = WAT[:, :, 2, 0:4, :].rearrange("p k r c -> p k (r c)")
        AOTM = WAT[:, :, 2, 4:8, :]
        AOC = U[:, 6144:10240]
        EPSA = DDT[:, 0:1]
        HFf = HF.rearrange("p j c -> p (j c)").bitcast(F32)
        MODR = HFf[0:1, 0:11 * D]
        CROW = HFf[0:1, 11 * D:12 * D]
        YTGflat = YTG.rearrange("p a b -> p (a b)")
        WNATv = YTGflat.rearrange("p (r f) -> p r f", r=J)
        WMODSv = YTGflat[:, 0:4096].rearrange("p (k c) -> p k c", k=J)

        def bank(i, w=512):
            return PS[:, i, 0:w]

        def bankb(i):
            return PS[:, i, :].bitcast(BF16)

        def st1(k):
            return ST1[0:1, (k % 3) * 512:(k % 3) * 512 + 512]

        def st1b(k):
            return ST1B[0:1, k * 512:(k + 1) * 512]

        ld_eye = P.dma("gpsimd", "sg", EYE[:], eye_h)
        ld_eyf = P.dma("gpsimd", "sg", EYF[:], eyf_h)
        ld_c = P.dma("gpsimd", "sg", CROW, cvec_h)
        P.dma("gpsimd", "sg", MODR[0:1, 6 * D:7 * D], bo_h)
        P.dma("gpsimd", "sg", MODR[0:1, 7 * D:7 * D + 1], alpha_h)
        P.dma("gpsimd", "sg", MODR[0:1, 7 * D + 4:7 * D + 5], temp_h)
        P.dma("gpsimd", "sg", MODR[0:1, 8 * D:9 * D], gamma_h)
        ld_small = P.dma("gpsimd", "sg", MODR[0:1, 9 * D:10 * D], beta_h)
        P.dma("gpsimd", "sg", HSEL[:], hsel_h)
        ld_sel = P.dma("gpsimd", "sg", G2SEL[:], g2sel_h)

        P.op("vector", lambda e: e.memset(DDT[:, 0:1], 2.0 ** -40))
        P.vgap()
        P.op("vector", lambda e: e.tensor_single_scalar(DDT[:, 1:2], DDT[:, 0:1], 1.0, ALU.add))
        P.vgap()
        P.op("vector", lambda e: e.tensor_single_scalar(DDT[:, 2:3], DDT[:, 1:2], 1.0, ALU.subtract))
        P.vgap()
        P.op("vector", lambda e: e.tensor_single_scalar(DDT[:, 3:4], DDT[:, 2:3], 2.0 ** 40, ALU.mult))
        P.vgap()
        P.op("vector", lambda e: e.tensor_scalar(
            CPX[:], DDT[:, 3:4], C52 - C23, C23, ALU.mult, ALU.add))
        P.vgap()
        P.op("vector", lambda e: e.memset(DDT[:, 0:1], EPS_LN))
        P.op("vector", lambda e: e.memset(ONE1[:], 1.0))
        P.op("vector", lambda e: e.memset(N1R[:], -1.0))
        P.op("vector", lambda e: e.memset(ONEC[:], 1.0))

        sq_wat = 0
        for wi, wsrc in enumerate((wq_h, wk_h, wv_h, wo_h)):
            if wi > 0:
                P.dwait("gpsimd", sq_wat)
            g_w = P.dma("gpsimd", "sg", WNATv[:, :, :],
                        wsrc.rearrange("(r p) f -> p r f", p=128))
            P.dwait("sync", g_w)
            for r in range(J):
                sq_wat = P.dmat(WAT[:, :, wi, r, :], WNATv[:, r, :])

        P.dwait("scalar", ld_c)
        s_csil = P.op("scalar", lambda e: e.activation(CROW, CROW, ACT.Silu))
        P.wait("tensor", "s", s_csil)
        P.dwait("tensor", ld_eyf)
        for j in range(J):
            P.op("tensor", lambda e, j=j: e.matmul(
                bank(7)[:, j:j + 1], CROW[0:1, j * 128:(j + 1) * 128],
                EYF[0:1, 0:1], start=(j == 0), stop=(j == J - 1)), inc=(j == J - 1))
        t_ctr = P.c["t"]
        P.wait("vector", "t", t_ctr)
        v_csf = P.op("vector", lambda e: e.tensor_copy(CSF[:], bank(7)[:, 0:J]))

        v_modev = v_csf
        P.dwait("gpsimd", sq_wat)
        for n in range(12):
            g_wm = P.dma("gpsimd", "sg", WMODSv,
                         wmod_h.rearrange("(k p) m -> p k m", p=128)[:, :, n * 512:(n + 1) * 512])
            P.dwait("tensor", g_wm)
            P.wait("tensor", "v", v_modev)
            for k in range(J):
                P.op("tensor", lambda e, k=k: e.matmul(
                    bank(6)[0:1, :], CSF[:, k:k + 1], WMODSv[:, k, :],
                    start=(k == 0), stop=(k == J - 1)), inc=(k == J - 1))
            t_mm = P.c["t"]
            P.wait("vector", "t", t_mm)
            v_modev = P.op("vector", lambda e, n=n: e.tensor_copy(
                MODR[0:1, n * 512:(n + 1) * 512], bank(6)[0:1, :]))
            P.wait("gpsimd", "t", t_mm)

        P.wait("scalar", "v", v_modev)
        for base in (2, 5):
            row = MODR[0:1, base * D:(base + 1) * D]
            sgn = MODR[0:1, 10 * D:11 * D]
            P.wait("scalar", "v", P.c["v"])
            P.op("scalar", lambda e, row=row, sgn=sgn: e.activation(sgn, row, ACT.Sign))
            s_sg = P.op("scalar", lambda e, row=row: e.activation(PREPROW, row, ACT.Abs))
            P.wait("vector", "s", s_sg)
            P.op("vector", lambda e: e.tensor_single_scalar(PREPROW, PREPROW, 0.25, ALU.max))
            P.op("vector", lambda e, row=row, sgn=sgn: e.tensor_mul(row, PREPROW, sgn))
        P.op("vector", lambda e: e.tensor_mul(G1R[:], MODR[0:1, 2 * D:3 * D], MODR[0:1, 8 * D:9 * D]))
        v_rows = P.op("vector", lambda e: e.tensor_mul(G0R[:], MODR[0:1, 2 * D:3 * D], MODR[0:1, 9 * D:10 * D]))

        last_fme = v_rows
        for (ri, dstt, addv) in ((1, CA1, 1.0), (0, SAF, 0.0), (6, BOF, 0.0), (5, GBF, 0.0)):
            P.wait("tensor", "v", last_fme)
            for j in range(J):
                P.op("tensor", lambda e, ri=ri, j=j: e.matmul(
                    bank(7)[:, j:j + 1],
                    MODR[0:1, ri * D + j * 128: ri * D + (j + 1) * 128],
                    EYF[0:1, 0:1], start=(j == 0), stop=(j == J - 1)), inc=(j == J - 1))
            t_tr = P.c["t"]
            P.wait("vector", "t", t_tr)
            last_fme = P.op("vector", lambda e, dstt=dstt, addv=addv: e.tensor_scalar(
                dstt[:], bank(7)[:, 0:J], addv, None, ALU.add))

        P.dwait("tensor", ld_eye)
        P.wait("tensor", "v", last_fme)
        for j in range(J):
            P.op("tensor", lambda e, j=j: e.matmul(
                bank(7)[:, j:j + 1], G1R[0:1, j * 128:(j + 1) * 128], EYE[0:1, 0:1],
                start=(j == 0), stop=False), inc=False)
            P.op("tensor", lambda e, j=j: e.matmul(
                bank(7)[:, 8 + j:9 + j], G0R[0:1, j * 128:(j + 1) * 128], EYE[0:1, 0:1],
                start=False, stop=(j == J - 1)), inc=(j == J - 1))
        t_g1f = P.c["t"]
        P.wait("vector", "t", t_g1f)
        P.op("vector", lambda e: e.tensor_copy(G1F0[:], bank(7)[:, 0:16]))
        G1F = G1F0[:, 0:8]
        G0F = G1F0[:, 8:16]
        v_cbx = last_fme
        for dst, ri, addv in ((CB1BC, 4, 1.0), (SBBC, 3, 0.0)):
            for half in range(2):
                P.op("vector", lambda e, ri=ri, addv=addv, half=half: e.tensor_scalar(
                    PREPROW[0:1, 0:512], MODR[0:1, ri * D + half * 512:ri * D + (half + 1) * 512],
                    addv, None, ALU.add))
                v_cbx = P.c["v"]
                P.wait("tensor", "v", v_cbx)
                P.op("tensor", lambda e: e.matmul(
                    bank(5), ONE1[0:1, 0:128], PREPROW[0:1, 0:512], start=True, stop=True))
                t_x = P.c["t"]
                P.wait("vector", "t", t_x)
                v_cbx = P.op("vector", lambda e, dst=dst, half=half: e.tensor_copy(
                    dst[:, half * 512:(half + 1) * 512], bank(5)))

        P.dwait("vector", ld_small)
        P.op("vector", lambda e: e.reciprocal(MODR[0:1, 7 * D + 8:7 * D + 9],
                                              MODR[0:1, 7 * D + 4:7 * D + 5]))
        P.vgap()
        P.op("vector", lambda e: e.tensor_copy(PREPROW[0:1, 0:1], MODR[0:1, 7 * D:7 * D + 1]))
        P.op("vector", lambda e: e.tensor_copy(PREPROW[0:1, 1:2], MODR[0:1, 7 * D + 8:7 * D + 9]))
        v_al = P.c["v"]
        P.wait("tensor", "v", v_al)
        P.op("tensor", lambda e: e.matmul(bank(5)[:, 0:2], ONE1[0:1, 0:128],
                                          PREPROW[0:1, 0:2], start=True, stop=True))
        t_al = P.c["t"]
        P.wait("vector", "t", t_al)
        P.op("vector", lambda e: e.tensor_copy(ALP[:], bank(5)[:, 0:1]))
        v_itp = P.op("vector", lambda e: e.tensor_copy(ITP[:], bank(5)[:, 1:2]))

        P1B = [Uf[:, k * 1024:(k + 1) * 1024] for k in range(8)]

        def p1_src(u):
            if u < 64:
                return ((wg_h, wu_h)[u % 2])[(u // 2) * 128:(u // 2 + 1) * 128, :]
            mm = u - 64
            r, qq = mm // 4, mm % 4
            return wd_h[r * 128:(r + 1) * 128, qq * 1024:(qq + 1) * 1024]

        def p1_col(u):
            if u < 64:
                return (u % 2) * 32 + u // 2
            return 64 + (u - 64)

        P1RING = ["sg", "sw"]
        p1_tok = {}
        s_abs_hist = {}
        P.wait("gpsimd", "t", P.c["t"])
        P.wait("gpsimd", "v", P.c["v"])
        for u0 in range(8):
            p1_tok[u0] = P.dma("gpsimd", P1RING[u0 % 2], P1B[u0][:], p1_src(u0))
        for u in range(96):
            P.dwait_tok("vector", p1_tok[u])
            na = p1_col(u)
            s_abs = P.op("vector", lambda e, u=u, na=na: e.reduce_sum(
                ABSS[:, na:na + 1], P1B[u % 8][:], AXX.X, apply_absolute_value=True))
            s_abs_hist[u] = s_abs
            if u + 8 < 96:
                P.wait("gpsimd", "v", s_abs)
                P.dwait_tok("gpsimd", p1_tok[u])
                p1_tok[u + 8] = P.dma("gpsimd", P1RING[(u + 8) % 2], P1B[u % 8][:],
                                      p1_src(u + 8))
        s_abs = s_abs_hist[95]
        for mi, (a, b) in enumerate(((0, 32), (32, 64), (64, 96))):
            P.op("vector", lambda e, a=a, b=b, mi=mi: e.reduce_sum(
                STAT[:, mi:mi + 1], ABSS[:, a:b], AXX.X))
        P.vgap()
        v_ab = P.op("vector", lambda e: e.tensor_copy(BST[:, 0:3], STAT[:, 0:3]))
        P.wait("tensor", "v", v_ab)
        P.op("tensor", lambda e: e.matmul(bank(5)[0:1, 0:3], ONEC[:], BST[:, 0:3],
                                          start=True, stop=True))
        t_ab = P.c["t"]
        P.wait("vector", "t", t_ab)
        P.op("vector", lambda e: e.tensor_scalar(
            st1(0)[0:1, 0:3], bank(5)[0:1, 0:3], 1.0 / (INNER * D), 1e-5,
            ALU.mult, ALU.max))
        P.vgap()
        P.op("vector", lambda e: e.reciprocal(st1(1)[0:1, 0:3], st1(0)[0:1, 0:3]))
        P.vgap()
        P.op("vector", lambda e: e.tensor_copy(ROWSB[0:1, 0:3], st1(1)[0:1, 0:3]))
        v_sc = P.op("vector", lambda e: e.tensor_copy(ROWSB[0:1, 3:6], st1(0)[0:1, 0:3]))
        P.wait("tensor", "v", v_sc)
        P.op("tensor", lambda e: e.matmul(bank(5)[:, 0:6], ONE1[0:1, 0:128],
                                          ROWSB[0:1, 0:6], start=True, stop=True))
        t_sc = P.c["t"]
        P.wait("vector", "t", t_sc)
        P.op("vector", lambda e: e.tensor_copy(SGP[:, 0:3], bank(5)[:, 0:3]))
        v_sgp = P.op("vector", lambda e: e.tensor_copy(SGR[:, 0:3], bank(5)[:, 3:6]))

        def kk_bank(hp):
            b = 6 + hp // 4
            off = (hp % 4) * 128
            return PS[:, b, off:off + 128]

        def stat_a(pbuf, a, b):
            return STAT[:, pbuf * 8 + a:pbuf * 8 + b]

        sq_hist = {}
        v_kv_prev = None
        t_kk_hist = {}
        v_xn_hist = {}

        v_prep_end = P.c["v"]
        s_prep_end = P.c["s"]
        t_prep_end = P.c["t"]

        sq_hist = {}
        v_mod_hist = {}
        ev_v = {}
        ev_s = {}
        t_kk_hist = {}
        v_xn_hist = {}

        def a_prep(ft):
            pbuf = ft % 2
            xb3 = XT3[ft % 3]
            own = ft < NTO
            if ft >= 2:
                P.wait("scalar", "v", v_xn_hist.get(ft - 2, 0))
            P.dwait_tok("scalar", ax_tok[ft])
            P.op("scalar", lambda e, xb3=xb3, pbuf=pbuf: e.activation(
                SCR[:], xb3[:], ACT.Square, accum_out=stat_a(pbuf, 0, 1)))
            s_st = P.op("scalar", lambda e, xb3=xb3, pbuf=pbuf: e.activation(
                SCR[:], xb3[:], ACT.Identity, accum_out=stat_a(pbuf, 1, 2)))
            P.wait("vector", "s", s_st)
            P.op("vector", lambda e, pbuf=pbuf: e.tensor_single_scalar(
                stat_a(pbuf, 2, 3), stat_a(pbuf, 1, 2), 1.0 / D, ALU.mult))
            P.op("vector", lambda e, pbuf=pbuf: e.tensor_single_scalar(
                stat_a(pbuf, 3, 4), stat_a(pbuf, 0, 1), 1.0 / D, ALU.mult))
            P.op("vector", lambda e, pbuf=pbuf: e.tensor_mul(
                stat_a(pbuf, 4, 5), stat_a(pbuf, 2, 3), stat_a(pbuf, 2, 3)))
            v_var = P.op("vector", lambda e, pbuf=pbuf: e.tensor_sub(
                stat_a(pbuf, 5, 6), stat_a(pbuf, 3, 4), stat_a(pbuf, 4, 5)))
            P.wait("scalar", "v", v_var)
            s_sd = P.op("scalar", lambda e, pbuf=pbuf: e.activation(
                stat_a(pbuf, 6, 7), stat_a(pbuf, 5, 6), ACT.Sqrt, bias=EPSA))
            P.wait("vector", "s", s_sd)
            P.op("vector", lambda e, pbuf=pbuf: e.reciprocal(
                stat_a(pbuf, 7, 8), stat_a(pbuf, 6, 7)))
            if ft >= 2:
                P.dwait("vector", sq_hist.get(ft - 2, 0))
            v_xn = P.op("vector", lambda e, xb3=xb3, pbuf=pbuf: e.tensor_scalar(
                XNB[:, pbuf, :], xb3[:], stat_a(pbuf, 2, 3), stat_a(pbuf, 7, 8),
                ALU.subtract, ALU.mult))
            v_xn_hist[ft] = v_xn
            P.wait("sync", "v", v_xn)
            if own:
                dst3 = HF.rearrange("p j (t c) -> p j t c", c=128)[:, :, ft, :]
            else:
                if ft >= NTO + 2:
                    P.wait("sync", "t", t_kv_hist[ft - 2])
                dst3 = HTMP[:, pbuf, :, :]
            sq_t = P.dmat(dst3, XNB[:, pbuf, :])
            sq_hist[ft] = sq_t
            P.dwait("scalar", sq_t)
            for j in range(J):
                P.op("scalar", lambda e, dst3=dst3, j=j: e.activation(
                    dst3[:, j, :], dst3[:, j, :], ACT.Identity,
                    bias=SAF[:, j:j + 1], scale=CA1[:, j:j + 1]),
                    inc=(j == J - 1))
            v_mod_hist[ft] = P.c["s"]
            return dst3

        WQNA = [Uf[:, 5120 + k * 1024:5120 + (k + 1) * 1024] for k in range(4)]
        WFQA = [U[:, 18432:19456], U[:, 19456:20480]]
        WGSTG = [U[:, 20480:21504], U[:, 21504:22528]]
        wqa_tok = {}
        sqa_tok = {}
        sta_tok = {}
        vfin_hist = {}

        def wqa_src(m):
            if m < 64:
                return ((wg_h, wu_h)[m % 2])[(m // 2) * 128:(m // 2 + 1) * 128, :]
            mm = m - 64
            r, qq = mm // 4, mm % 4
            return wd_h[r * 128:(r + 1) * 128, qq * 1024:(qq + 1) * 1024]

        def emit_wq_unit(m):
            buf = WQNA[m % 4]
            P.dwait_tok("vector", wqa_tok[m])
            if m < 64:
                sp = SGP[:, m % 2:m % 2 + 1]
                sr = SGR[:, m % 2:m % 2 + 1]
            else:
                sp = SGP[:, 2:3]
                sr = SGR[:, 2:3]
            P.op("vector", lambda e, buf=buf, sp=sp: e.tensor_scalar(
                buf[:], buf[:], sp, CPX[:, 0:1], ALU.mult, ALU.add))
            P.op("vector", lambda e, buf=buf: e.tensor_scalar(
                buf[:], buf[:], CPX[:, 0:1], 1.0, ALU.subtract, ALU.min))
            if m >= 64:
                r = (m - 64) // 4
                P.op("vector", lambda e, buf=buf, r=r: e.tensor_scalar(
                    buf[:], buf[:], -1.0, GBF[:, r:r + 1], ALU.max, ALU.mult))
            if m >= 2:
                P.dwait_tok("vector", sqa_tok[m - 2])
            if m >= 64:
                vfin = P.op("vector", lambda e, m=m, buf=buf, sr=sr: e.tensor_scalar(
                    WFQA[m % 2][:], buf[:], sr, None, ALU.mult))
            else:
                vfin = P.op("vector", lambda e, m=m, buf=buf, sr=sr: e.tensor_scalar(
                    WFQA[m % 2][:], buf[:], -1.0, sr, ALU.max, ALU.mult))
            vfin_hist[m] = vfin
            if m + 4 < 96:
                P.wait("gpsimd", "v", vfin)
                P.dwait_tok("gpsimd", wqa_tok[m])
                wqa_tok[m + 4] = P.dma("gpsimd", ["sg", "sw"][m % 2],
                                       WQNA[m % 4][:], wqa_src(m + 4))
            P.wait("sync", "v", vfin)
            if m >= 2:
                P.dwait_tok("sync", sta_tok[m - 2])
            sqa_tok[m] = P.dmat(
                WGSTG[m % 2].rearrange("p (a c) -> p a c", c=128), WFQA[m % 2][:])
            P.dwait_tok("sync", sqa_tok[m])
            if m < 64:
                sta_tok[m] = P.dma("sync", "sx", wgq_hbm[m % 2, m // 2],
                                   WGSTG[m % 2][:])
            else:
                r, qq = (m - 64) // 4, (m - 64) % 4
                sta_tok[m] = P.dma(
                    "sync", "sx",
                    wdt_hbm[8 * qq:8 * (qq + 1), :, r * 128:(r + 1) * 128].rearrange(
                        "i p c -> p i c"),
                    WGSTG[m % 2].rearrange("p (a c) -> p a c", c=128))

        P.wait("gpsimd", "v", v_sgp)
        for uu in range(88, 96):
            P.dwait_tok("gpsimd", p1_tok[uu])
        for m0 in range(4):
            wqa_tok[m0] = P.dma("gpsimd", ["sg", "sw"][m0 % 2],
                                WQNA[m0 % 4][:], wqa_src(m0))
        wu_i = 0

        t_kv_hist = {}
        dst3s = {}
        ax_tok = {}
        P.wait("scalar", "v", max(v_prep_end, v_sgp))
        P.wait("scalar", "t", P.c["t"])
        for f0 in range(3):
            ax_tok[f0] = P.dma("scalar", "sa", XT3[f0][:],
                               x_h[f0 * 128:(f0 + 1) * 128, :])
        dst3s[0] = a_prep(0)
        dst3s[1] = a_prep(1)
        for ft in range(NTF):
            if ft + 3 < NTF:
                P.wait("scalar", "v", v_xn_hist.get(ft, 0))
                ax_tok[ft + 3] = P.dma(
                    "scalar", "sa", XT3[(ft + 3) % 3][:],
                    x_h[(ft + 3) * 128:(ft + 4) * 128, :])
            dst3 = dst3s.pop(ft)
            P.wait("tensor", "s", v_mod_hist[ft])
            if ft >= 1:
                P.wait("tensor", "v", ev_v.get(ft - 1, 0))
                P.wait("tensor", "s", ev_s.get(ft - 1, 0))
            t_kv_h0 = 0
            for half in range(2):
                kb = bank(0 if half == 0 else 2)
                vb = bank(1 if half == 0 else 3)
                for k in range(J):
                    P.op("tensor", lambda e, k=k, half=half, dst3=dst3, kb=kb: e.matmul(
                        kb, dst3[:, k, :], WAT[:, k, 1, half * 4:(half + 1) * 4, :],
                        start=(k == 0), stop=(k == J - 1)), inc=False)
                    P.op("tensor", lambda e, k=k, half=half, dst3=dst3, vb=vb: e.matmul(
                        vb, dst3[:, k, :], WAT[:, k, 2, half * 4:(half + 1) * 4, :],
                        start=(k == 0), stop=(k == J - 1)), inc=(k == J - 1))
                if half == 0:
                    t_kv_h0 = P.c["t"]
            t_kv = P.c["t"]
            t_kv_hist[ft] = t_kv
            P.wait("vector", "t", t_kv_h0)
            ev_v1 = P.op("vector", lambda e: e.tensor_copy(KTM[:, 0:512], bank(0)))
            P.wait("scalar", "t", t_kv_h0)
            ev_s1 = P.op("scalar", lambda e: e.copy(VTM[:, 0:512], bank(1)))
            P.wait("vector", "t", t_kv)
            ev_v[ft] = P.op("vector", lambda e: e.tensor_copy(KTM[:, 512:1024], bank(2)))
            P.wait("scalar", "t", t_kv)
            ev_s[ft] = P.op("scalar", lambda e: e.copy(VTM[:, 512:1024], bank(3)))
            if ft + 2 < NTF:
                dst3s[ft + 2] = a_prep(ft + 2)
            st_flag = (ft == 0)
            sp_flag = (ft == NTF - 1)
            for hp in range(HP):
                if hp == 0:
                    P.wait("tensor", "v", ev_v1)
                    P.wait("tensor", "s", ev_s1)
                elif hp == 4:
                    P.wait("tensor", "v", ev_v[ft])
                    P.wait("tensor", "s", ev_s[ft])
                kkb = kk_bank(hp)
                first = st_flag and hp % 4 == 0
                last = sp_flag and hp % 4 == 3
                P.op("tensor", lambda e, hp=hp, kkb=kkb, first=first: e.matmul(
                    kkb[0:64, 0:64], KTM[:, hp * 128:hp * 128 + 64],
                    KTM[:, hp * 128:hp * 128 + 64],
                    start=first, stop=False, tile_position=(0, 0)), inc=False)
                P.op("tensor", lambda e, hp=hp, kkb=kkb, last=last: e.matmul(
                    kkb[0:64, 64:128], KTM[:, hp * 128:hp * 128 + 64],
                    VTM[:, hp * 128:hp * 128 + 64],
                    start=False, stop=last, tile_position=(0, 0)), inc=False)
                P.op("tensor", lambda e, hp=hp, kkb=kkb, first=first: e.matmul(
                    kkb[64:128, 0:64], KTM[:, hp * 128 + 64:hp * 128 + 128],
                    KTM[:, hp * 128 + 64:hp * 128 + 128],
                    start=first, stop=False, tile_position=(0, 64)), inc=False)
                P.op("tensor", lambda e, hp=hp, kkb=kkb, last=last: e.matmul(
                    kkb[64:128, 64:128], KTM[:, hp * 128 + 64:hp * 128 + 128],
                    VTM[:, hp * 128 + 64:hp * 128 + 128],
                    start=False, stop=last, tile_position=(0, 64)),
                    inc=(hp == HP - 1))
            t_kk_hist[ft] = P.c["t"]
            while wu_i < ((ft + 1) * 96) // 64:
                emit_wq_unit(wu_i)
                wu_i += 1

        t_kkdone = P.c["t"]

        def nss(i, a, b):
            return NSS[:, i * 8 + a:i * 8 + b]

        P.wait("vector", "t", t_kkdone)
        P.op("vector", lambda e: e.tensor_copy(EYFB[:], EYF[:]))
        for hp in range(HP):
            P.op("vector", lambda e, hp=hp: e.scalar_tensor_tensor(
                KKS[:, hp, 0:64], EYF[:], ALP[:, 0:1], kk_bank(hp)[:, 0:64],
                ALU.mult, ALU.add))
            P.op("vector", lambda e, hp=hp: e.tensor_copy(
                KKS[:, hp, 64:128], kk_bank(hp)[:, 64:128]))
        for hp in range(HP):
            P.op("vector", lambda e, hp=hp: e.tensor_copy(KTVB[:, hp, :], KKS[:, hp, 64:128]))
        for hp in range(HP):
            P.op("vector", lambda e, hp=hp: e.tensor_mul(
                SCR[:, hp * 64:(hp + 1) * 64], KKS[:, hp, 0:64], KKS[:, hp, 0:64]))
        P.vgap()
        for hp in range(HP):
            P.op("vector", lambda e, hp=hp: e.reduce_sum(
                nss(0, hp, hp + 1), SCR[:, hp * 64:(hp + 1) * 64], AXX.X))
        P.vgap()
        v_fro = P.op("vector", lambda e: e.tensor_copy(BST[:, 0:8], nss(0, 0, 8)))
        P.wait("tensor", "v", v_fro)
        P.dwait("tensor", ld_sel)
        P.op("tensor", lambda e: e.matmul(bank(5)[0:2, 0:8], HSEL[:], BST[:, 0:8],
                                          start=True, stop=True))
        t_fro = P.c["t"]
        P.wait("vector", "t", t_fro)
        P.op("vector", lambda e: e.reciprocal(ROWS[:, 0:8], bank(5)[0:2, 0:8]))
        P.vgap()
        v_fr2 = P.op("vector", lambda e: e.tensor_copy(ROWSB[:, 0:8], ROWS[:, 0:8]))
        P.wait("tensor", "v", v_fr2)
        P.op("tensor", lambda e: e.matmul(bank(5)[:, 0:8], G2SEL[:], ROWSB[:, 0:8],
                                          start=True, stop=True))
        t_a0 = P.c["t"]
        P.wait("vector", "t", t_a0)
        P.op("vector", lambda e: e.tensor_copy(nss(1, 0, 8), bank(5)[:, 0:8]))
        P.vgap()
        for hp in range(HP):
            P.op("vector", lambda e, hp=hp: e.tensor_scalar(
                XNS[:, hp, :], KKS[:, hp, 0:64], nss(1, hp, hp + 1), None, ALU.mult))
        for hp in range(HP):
            P.op("vector", lambda e, hp=hp: e.memset(ABLK[:, hp, :], 0.0))
        for hp in range(HP):
            P.op("vector", lambda e, hp=hp: e.tensor_copy(
                ABLK[0:64, hp, 0:64], KKS[0:64, hp, 0:64]))
            P.op("vector", lambda e, hp=hp: e.tensor_copy(
                ABLK[64:128, hp, 64:128], KKS[64:128, hp, 0:64]))

        for it in range(NS_ITERS):
            P.wait("tensor", "v", P.c["v"])
            for hp in range(HP):
                P.op("tensor", lambda e, hp=hp: e.matmul(
                    bank(4)[:, hp * 64:(hp + 1) * 64], ABLK[:, hp, :], XNS[:, hp, :],
                    start=(hp == 0), stop=(hp == HP - 1)), inc=(hp == HP - 1))
            t_y = P.c["t"]
            P.wait("vector", "t", t_y)
            for hp in range(HP):
                P.op("vector", lambda e, hp=hp: e.scalar_tensor_tensor(
                    ZNS[:, hp, :], EYFB[:], 2.0, bank(4)[:, hp * 64:(hp + 1) * 64],
                    ALU.mult, ALU.subtract))
            v_z = P.c["v"]
            P.wait("tensor", "v", v_z)
            for hp in range(HP):
                P.op("tensor", lambda e, hp=hp: e.matmul(
                    bank(5)[0:64, hp * 64:(hp + 1) * 64], XNS[0:64, hp, :],
                    ZNS[0:64, hp, :], start=(hp == 0), stop=(hp == HP - 1),
                    tile_position=(0, 0)), inc=False)
                P.op("tensor", lambda e, hp=hp: e.matmul(
                    bank(5)[64:128, hp * 64:(hp + 1) * 64], XNS[64:128, hp, :],
                    ZNS[64:128, hp, :], start=(hp == 0), stop=(hp == HP - 1),
                    tile_position=(64, 64)), inc=(hp == HP - 1))
            t_x2i = P.c["t"]
            P.wait("vector", "t", t_x2i)
            for hp in range(HP):
                P.op("vector", lambda e, hp=hp: e.tensor_copy(
                    XNS[:, hp, :], bank(5)[:, hp * 64:(hp + 1) * 64]))

        P.wait("tensor", "v", P.c["v"])
        for hp in range(HP):
            P.op("tensor", lambda e, hp=hp: e.matmul(
                bank(4)[0:64, hp * 64:(hp + 1) * 64], KTVB[0:64, hp, :],
                XNS[0:64, hp, :], start=(hp == 0), stop=(hp == HP - 1),
                tile_position=(0, 0)), inc=False)
            P.op("tensor", lambda e, hp=hp: e.matmul(
                bank(4)[64:128, hp * 64:(hp + 1) * 64], KTVB[64:128, hp, :],
                XNS[64:128, hp, :], start=(hp == 0), stop=(hp == HP - 1),
                tile_position=(64, 64)), inc=(hp == HP - 1))
        t_zt = P.c["t"]
        P.wait("vector", "t", t_zt)
        P.wait("vector", "v", v_itp)
        for hp in range(HP):
            P.op("vector", lambda e, hp=hp: e.tensor_scalar(
                SCR[:, hp * 64:(hp + 1) * 64], bank(4)[:, hp * 64:(hp + 1) * 64],
                ITP[:, 0:1], None, ALU.mult))
        P.vgap()
        for hp in range(HP):
            P.op("vector", lambda e, hp=hp: e.reduce_max(
                nss(1, hp, hp + 1), SCR[:, hp * 64:(hp + 1) * 64], AXX.X))
        P.vgap()
        v_mx = P.op("vector", lambda e: e.tensor_single_scalar(
            nss(0, 0, 8), nss(1, 0, 8), -1.0, ALU.mult))
        P.wait("scalar", "v", v_mx)
        for hp in range(HP):
            P.op("scalar", lambda e, hp=hp: e.activation(
                SCR[:, hp * 64:(hp + 1) * 64], SCR[:, hp * 64:(hp + 1) * 64],
                ACT.Exp, bias=nss(0, hp, hp + 1)))
        s_exp = P.c["s"]
        P.wait("vector", "s", s_exp)
        for hp in range(HP):
            P.op("vector", lambda e, hp=hp: e.reduce_sum(
                nss(2, hp, hp + 1), SCR[:, hp * 64:(hp + 1) * 64], AXX.X))
        P.vgap()
        P.op("vector", lambda e: e.reciprocal(nss(3, 0, 8), nss(2, 0, 8)))
        P.vgap()
        for hp in range(HP):
            P.op("vector", lambda e, hp=hp: e.tensor_scalar(
                SMB[:, hp, :], SCR[:, hp * 64:(hp + 1) * 64], nss(3, hp, hp + 1),
                None, ALU.mult))
        v_sm = P.c["v"]
        P.wait("tensor", "v", v_sm)
        v_mev = {}
        for hp in range(HP):
            bb = 4 + hp % 2
            if hp >= 2:
                P.wait("tensor", "v", v_mev[hp - 2])
            P.op("tensor", lambda e, hp=hp, bb=bb: e.transpose(
                bankb(bb)[0:64, 0:64], SMB[0:64, hp, :],
                EYE[0:64, 0:64]), inc=False)
            t_tp = P.op("tensor", lambda e, hp=hp, bb=bb: e.transpose(
                bankb(bb)[64:128, 64:128], SMB[64:128, hp, :],
                EYE[64:128, 64:128], tile_position=(64, 64)))
            P.wait("vector", "t", t_tp)
            P.op("vector", lambda e, hp=hp, bb=bb: e.tensor_copy(
                MAPB[0:64, hp, :], bankb(bb)[0:64, 0:64]), inc=False)
            v_mev[hp] = P.op("vector", lambda e, hp=hp, bb=bb: e.tensor_copy(
                MAPB[64:128, hp, :], bankb(bb)[64:128, 64:128]))
        v_map = P.c["v"]

        sa_h2q_st = {}
        sq_h2_hist = {}
        XTC = YTG.rearrange("p a b -> p (a b)").bitcast(F32).rearrange(
            "p (b c) -> p b c", b=4)
        v_x2_hist = {}
        v_ln_hist = {}
        xtc_tok = {}
        x2st_tok = {}
        P.g_h2_prev = 0
        P.wait("scalar", "v", P.c["v"])
        P.wait("sync", "v", P.c["v"])
        P.wait("gpsimd", "v", P.c["v"])
        P.wait("gpsimd", "t", P.c["t"])
        P.wait("gpsimd", "s", P.c["s"])
        P.wait("tensor", "v", P.c["v"])
        P.wait("tensor", "s", P.c["s"])
        P.wait("tensor", "g", P.c["g"])
        v_ao_prev = P.c["v"]
        s_zbb7_prev = P.c["s"]
        s_oev7_prev = P.c["s"]
        qb_of = [bank(0), bank(1), bank(6), bank(7)]
        X2T4 = [Uf[:, 7168:8192], Uf[:, 8192:9216],
                Uf[:, 1024:2048], Uf[:, 11264:12288]]
        MU2 = SYCOL[:, 0:2]
        VB2 = SYCOL[:, 2:4]
        SD2 = SYCOL[:, 4:6]
        RS2 = SYCOL[:, 6:8]
        sq_ao_hist = {}
        va_hist = {}
        sqrt_hist = {}
        P.qbarrier("scalar", "sq")
        P.qbarrier("sync", "sx")
        P.qbarrier("sync", "sq")

        def t4_partA(st, t4):
            ft = st * 4 + t4
            P.dwait_tok("vector", xtc_tok[(st, t4)])
            if ft >= 4:
                P.dwait_tok("vector", x2st_tok[ft - 4])
            P.dwait("vector", sq_ao_hist[st])
            aosl = AOTM[:, :, t4, :]
            v_x2 = P.op("vector", lambda e, t4=t4, aosl=aosl: e.tensor_add(
                X2T4[t4][:], XTC[:, t4, :], aosl))
            v_x2_hist[(st, t4)] = v_x2
            P.wait("sync", "v", v_x2)
            x2st_tok[ft] = P.dma("sync", "sx",
                                 x2_hbm[ft * 128:(ft + 1) * 128, :], X2T4[t4][:])
            P.op("vector", lambda e, t4=t4: e.tensor_mul(
                SCR[:], X2T4[t4][:], X2T4[t4][:]))
            P.op("vector", lambda e: e.reduce_sum(stat_a(0, 0, 1), SCR[:], AXX.X))
            P.op("vector", lambda e, t4=t4: e.reduce_sum(
                stat_a(0, 1, 2), X2T4[t4][:], AXX.X))
            P.op("vector", lambda e, t4=t4: e.tensor_single_scalar(
                MU2[:, t4 % 2:t4 % 2 + 1], stat_a(0, 1, 2), 1.0 / D, ALU.mult))
            P.op("vector", lambda e: e.tensor_single_scalar(
                stat_a(0, 3, 4), stat_a(0, 0, 1), 1.0 / D, ALU.mult))
            P.op("vector", lambda e, t4=t4: e.tensor_mul(
                stat_a(0, 4, 5), MU2[:, t4 % 2:t4 % 2 + 1], MU2[:, t4 % 2:t4 % 2 + 1]))
            va_hist[(st, t4)] = P.op("vector", lambda e, t4=t4: e.tensor_sub(
                VB2[:, t4 % 2:t4 % 2 + 1], stat_a(0, 3, 4), stat_a(0, 4, 5)))

        def t4_sqrt(st, pair):
            P.wait("scalar", "v", va_hist[(st, 2 * pair + 1)])
            sqrt_hist[(st, pair)] = P.op("scalar", lambda e: e.activation(
                SD2[:, 0:2], VB2[:, 0:2], ACT.Sqrt, bias=EPSA))

        def t4_partB(st, t4):
            ft = st * 4 + t4
            P.wait("vector", "s", sqrt_hist[(st, t4 // 2)])
            P.op("vector", lambda e, t4=t4: e.reciprocal(
                RS2[:, t4 % 2:t4 % 2 + 1], SD2[:, t4 % 2:t4 % 2 + 1]))
            v_ln = P.op("vector", lambda e, t4=t4: e.tensor_scalar(
                SCR[:], X2T4[t4][:], MU2[:, t4 % 2:t4 % 2 + 1],
                RS2[:, t4 % 2:t4 % 2 + 1], ALU.subtract, ALU.mult))
            P.op("vector", lambda e: e.tensor_mul(SCR[:], SCR[:], CB1BC[:]))
            P.vgap()
            P.op("vector", lambda e: e.tensor_add(X3T[:, 0, :], SCR[:], SBBC[:]))
            P.vgap()
            P.op("vector", lambda e: e.reduce_max(stat_a(1, 0, 1), X3T[:, 0, :], AXX.X,
                                                  apply_absolute_value=True))
            P.vgap()
            P.op("vector", lambda e: e.reciprocal(stat_a(1, 1, 2), stat_a(1, 0, 1)))
            P.vgap()
            P.op("vector", lambda e: e.tensor_single_scalar(
                stat_a(1, 2, 3), stat_a(1, 1, 2), 127.0, ALU.mult))
            P.op("vector", lambda e: e.tensor_single_scalar(
                stat_a(1, 3, 4), stat_a(1, 0, 1), 1.0 / 127.0, ALU.mult))
            P.op("vector", lambda e, t4=t4: e.tensor_scalar(
                SCR[:], X3T[:, 0, :], stat_a(1, 2, 3), CPX[:, 0:1],
                ALU.mult, ALU.add))
            P.vgap()
            v_h2q = P.op("vector", lambda e, t4=t4: e.tensor_scalar(
                H2T1[:], SCR[:], CPX[:, 0:1], stat_a(1, 3, 4),
                ALU.subtract, ALU.mult))
            P.wait("sync", "v", v_h2q)
            if t4 == 0 and st >= 1:
                P.dwait("sync", sa_h2q_st[st - 1])
            sq_h2 = P.dmat(H2FM[:, t4 * 8:(t4 + 1) * 8, :], H2T1[:])
            sq_h2_hist[st] = sq_h2
            P.dwait("vector", sq_h2)

        def t4_store(st):
            P.dwait("sync", sq_h2_hist[st])
            sa_h2q_st[st] = P.dma("sync", "sx", h2q_hbm[st], H2FM[:])

        for t4 in range(4):
            xtc_tok[(0, t4)] = P.dma(
                "scalar", "sa", XTC[:, t4, :], x_h[t4 * 128:(t4 + 1) * 128, :])
        for st in range(NST):
            cols = slice(st * 512, (st + 1) * 512)
            if st >= 1:
                t4_partA(st - 1, 0)
                t4_partA(st - 1, 1)
            s_qev = {}
            P.wait("tensor", "s", s_zbb7_prev)
            for j in range(J):
                qb = qb_of[j % 4]
                if j >= 4:
                    P.wait("tensor", "s", s_qev[j - 4])
                for k in range(J):
                    P.op("tensor", lambda e, j=j, k=k, qb=qb, cols=cols: e.matmul(
                        qb, WAT[:, k, 0, j, :], HF[:, k, cols],
                        start=(k == 0), stop=(k == J - 1)), inc=(k == J - 1))
                t_q = P.c["t"]
                P.wait("scalar", "t", t_q)
                s_qev[j] = P.op("scalar", lambda e, j=j, qb=qb: e.activation(
                    QF[:, j, :], qb, ACT.Copy))
            if st >= 1:
                t4_sqrt(st - 1, 0)
                t4_partB(st - 1, 0)
                t4_partB(st - 1, 1)
            s_oev = {}
            P.wait("tensor", "s", s_oev7_prev)
            for j in range(J):
                ob = bank(2 + j % 2)
                if j >= 2:
                    P.wait("tensor", "s", s_oev[j - 2])
                P.wait("tensor", "s", s_qev[j])
                P.op("tensor", lambda e, j=j, ob=ob: e.matmul(
                    ob[0:64, :], MAPB[0:64, j, :], QF[0:64, j, :],
                    start=True, stop=True, tile_position=(0, 0)), inc=False)
                P.op("tensor", lambda e, j=j, ob=ob: e.matmul(
                    ob[64:128, :], MAPB[64:128, j, :], QF[64:128, j, :],
                    start=True, stop=True, tile_position=(64, 64)), inc=True)
                t_o = P.c["t"]
                P.wait("scalar", "t", t_o)
                s_oev[j] = P.op("scalar", lambda e, j=j, ob=ob: e.activation(
                    OF[:, j, :], ob, ACT.Copy))
            s_oev7_prev = s_oev[J - 1]
            if st >= 1:
                t4_partA(st - 1, 2)
                t4_partA(st - 1, 3)
                t4_sqrt(st - 1, 1)
                t4_partB(st - 1, 2)
                t4_partB(st - 1, 3)
                t4_store(st - 1)
            for t4 in range(4):
                if st >= 1:
                    P.wait("scalar", "v", v_x2_hist[(st - 1, t4)])
                    xtc_tok[(st, t4)] = P.dma(
                        "scalar", "sa", XTC[:, t4, :],
                        x_h[(st * 4 + t4) * 128:(st * 4 + t4 + 1) * 128, :])
            s_zbb = {}
            s_sq = {}
            t_stats = {}
            P.wait("tensor", "v", v_ao_prev)
            P.wait("tensor", "s", s_oev[J - 1])
            for j in range(J):
                wb = qb_of[j % 4]
                if j < 4:
                    P.wait("tensor", "s", s_qev[j + 4])
                else:
                    P.wait("tensor", "s", s_zbb[j - 4])
                for k in range(J):
                    P.op("tensor", lambda e, j=j, k=k, wb=wb: e.matmul(
                        wb, WAT[:, k, 3, j, :], OF[:, k, :],
                        start=(k == 0), stop=False), inc=False)
                P.op("tensor", lambda e, j=j, wb=wb, cols=cols: e.matmul(
                    wb, EYE[:], HF[:, j, cols], start=False, stop=True), inc=True)
                t_w = P.c["t"]
                if j >= 2:
                    P.wait("scalar", "t", t_stats[j - 2])
                P.wait("scalar", "t", t_w)
                s_zbb[j] = P.op("scalar", lambda e, j=j, wb=wb: e.activation(
                    ZBB[:, j, :], wb, ACT.Identity, bias=BOF[:, j:j + 1]))
                s_sq[j] = P.op("scalar", lambda e, j=j: e.activation(
                    SQB[:, j % 2, :], ZBB[:, j, :], ACT.Square))
                P.wait("tensor", "s", s_sq[j])
                P.op("tensor", lambda e, j=j: e.matmul(
                    bank(4)[0:1, :], ONEC[:], ZBB[:, j, :],
                    start=(j == 0), stop=(j == J - 1)), inc=False)
                P.op("tensor", lambda e, j=j: e.matmul(
                    bank(5)[0:1, :], ONEC[:], SQB[:, j % 2, :],
                    start=(j == 0), stop=(j == J - 1)), inc=(j == J - 1))
                t_stats[j] = P.c["t"]
            s_zbb7_prev = s_sq[J - 1]
            t_st = P.c["t"]
            P.wait("vector", "t", t_st)
            P.op("vector", lambda e: e.tensor_single_scalar(st1(0), bank(4)[0:1, :], 1.0 / D, ALU.mult))
            P.op("vector", lambda e: e.tensor_single_scalar(st1(1), bank(5)[0:1, :], 1.0 / D, ALU.mult))
            P.vgap()
            P.op("vector", lambda e: e.tensor_mul(st1(2), st1(0), st1(0)))
            P.vgap()
            v_vr = P.op("vector", lambda e: e.tensor_sub(st1(1), st1(1), st1(2)))
            P.wait("scalar", "v", v_vr)
            s_sd2 = P.op("scalar", lambda e: e.activation(st1(2), st1(1), ACT.Sqrt, bias=EPSA[0:1, 0:1]))
            P.wait("vector", "s", s_sd2)
            P.op("vector", lambda e: e.reciprocal(st1(1), st1(2)))
            P.vgap()
            P.op("vector", lambda e: e.tensor_mul(st1(2), st1(0), st1(1)))
            P.vgap()
            P.op("vector", lambda e: e.tensor_copy(st1b(0), st1(1)))
            v_r2 = P.op("vector", lambda e: e.tensor_copy(st1b(1), st1(2)))
            P.wait("tensor", "v", v_r2)
            P.op("tensor", lambda e: e.matmul(
                bank(4), ONE1[0:1, 0:128], st1b(1), start=True, stop=True), inc=False)
            P.op("tensor", lambda e: e.matmul(
                bank(5), ONE1[0:1, 0:128], st1b(0), start=True, stop=True), inc=True)
            t_bc = P.c["t"]
            P.wait("vector", "t", t_bc)
            if st >= 1:
                P.dwait("vector", sq_ao_hist[st - 1])
            for j in range(J):
                P.op("vector", lambda e, j=j: e.tensor_mul(
                    SCR[:, 0:512], ZBB[:, j, :], bank(5)))
                P.op("vector", lambda e, j=j: e.tensor_sub(
                    SCR[:, 0:512], SCR[:, 0:512], bank(4)))
                vj = P.op("vector", lambda e, j=j: e.tensor_scalar(
                    AOC[:, j * 512:(j + 1) * 512], SCR[:, 0:512],
                    G1F[:, j:j + 1], G0F[:, j:j + 1], ALU.mult, ALU.add))
                P.wait("sync", "v", vj)
                sq_ao = P.dmat(AOTM[:, j, :, :], AOC[:, j * 512:(j + 1) * 512])
            v_ao = P.c["v"]
            v_ao_prev = v_ao
            sq_ao_hist[st] = sq_ao

        t4_partA(NST - 1, 0)
        t4_partA(NST - 1, 1)
        t4_sqrt(NST - 1, 0)
        t4_partB(NST - 1, 0)
        t4_partB(NST - 1, 1)
        t4_partA(NST - 1, 2)
        t4_partA(NST - 1, 3)
        t4_sqrt(NST - 1, 1)
        t4_partB(NST - 1, 2)
        t4_partB(NST - 1, 3)
        t4_store(NST - 1)

        WGT = HF
        WUT = WAT.rearrange("p a b c d -> p (a b c d)").rearrange(
            "p (k m) -> p k m", k=J)
        wgt_r = WGT.rearrange("p k (i c) -> p k i c", c=128)
        wut_r = WUT.rearrange("p k (i c) -> p k i c", c=128)
        h2q_r = h2q_hbm.rearrange("s p (t k) c -> s p t k c", k=J)
        s_gs = None
        v_up = None
        g_y_prev = 0
        s_up_hist = {}
        g_y_hist = {}
        s_up_save = {}
        g_y_save = {}
        sq_wq_last = None
        v_wq_by = {0: 0, 1: 0}
        sg_yq = {}
        t_gu_hist = {}
        P.wait("gpsimd", "v", P.c["v"])
        P.wait("gpsimd", "t", P.c["t"])
        P.wait("gpsimd", "s", P.c["s"])
        P.qbarrier("gpsimd", "sq")
        P.qbarrier("gpsimd", "sa")
        P.qbarrier("gpsimd", "sx")
        P.qbarrier("scalar", "sx")
        P.wait("scalar", "g", P.c["g"])
        P.wait("scalar", "v", P.c["v"])
        P.wait("scalar", "t", P.c["t"])
        P.wait("sync", "t", P.c["t"])
        P.wait("sync", "s", P.c["s"])
        WQN8 = [Uf[:, 2048:3072], Uf[:, 3072:4096], Uf[:, 5640:6664],
                Uf[:, 6664:7688], Uf[:, 7688:8712], Uf[:, 8712:9736],
                Uf[:, 9736:10760]]
        SCRQ2 = Uf[:, 10760:11784]
        WFQ2 = [WFQ, U[:, 2048:3072]]

        def wsrc_of(m):
            return ((wg_h, wu_h)[m % 2])[(m // 2) * 128:(m // 2 + 1) * 128, :]

        wq_tok = {}
        sq_wq_hist = {}
        g_wq_hist = {}
        g_wq_last = 0
        sa_h2t_tok = {}
        sa_h2t_tok[0] = P.dma("scalar", "sa", H2QT[:, :, :, :],
                              h2q_r[0, :, 0:2, :, :])
        for tg in range(1):
            if tg >= 1:
                P.dwait("gpsimd", sg_yq[tg - 1])
            sa_h2t = sa_h2t_tok[tg]
            h2b = H2QT if tg % 2 == 0 else H2QT2
            if tg == 0:
                wst_tok = {}
                for ii in range(4):
                    for which in range(2):
                        wst_tok[2 * ii + which] = P.dma(
                            "scalar", "sa", (wgt_r, wut_r)[which][:, :, ii, :],
                            wgq_hbm[which, ii])
            for i in range(NI):
                if tg == 0:
                    P.dwait_tok("tensor", wst_tok[2 * i])
                    P.dwait_tok("tensor", wst_tok[2 * i + 1])
                P.dwait_tok("tensor", sa_h2t)
                gb = bank(0 + 2 * (i % 2), TGW)
                ub = bank(1 + 2 * (i % 2), TGW)
                GSi = GS if i % 2 == 0 else GSB
                UPSi = UPS if i % 2 == 0 else UPSB
                if i >= 2:
                    P.wait("tensor", "s", s_up_save[i - 2])
                elif tg >= 1:
                    P.wait("tensor", "s", s_up_hist[30 + i])
                for k in range(J):
                    rhs = h2b[:, :, k, :]
                    P.op("tensor", lambda e, i=i, k=k, rhs=rhs, gb=gb: e.matmul(
                        gb, wgt_r[:, k, i, :], rhs,
                        start=(k == 0), stop=(k == J - 1)), inc=False)
                    P.op("tensor", lambda e, i=i, k=k, rhs=rhs, ub=ub: e.matmul(
                        ub, wut_r[:, k, i, :], rhs,
                        start=(k == 0), stop=(k == J - 1)), inc=(k == J - 1))
                t_gu = P.c["t"]
                if tg == 0 and i + 4 < NI:
                    P.wait("scalar", "t", t_gu)
                    P.dwait_tok("scalar", wst_tok[2 * i])
                    P.dwait_tok("scalar", wst_tok[2 * i + 1])
                    for which in range(2):
                        wst_tok[2 * (i + 4) + which] = P.dma(
                            "scalar", "sa", (wgt_r, wut_r)[which][:, :, i + 4, :],
                            wgq_hbm[which, i + 4])
                P.wait("scalar", "t", t_gu)
                if i >= 2:
                    P.wait("scalar", "g", g_y_save[i - 2])
                elif tg >= 1:
                    P.wait("scalar", "g", g_y_hist[30 + i])
                s_gs = P.op("scalar", lambda e, GSi=GSi, gb=gb: e.activation(
                    GSi[:], gb, ACT.Silu))
                s_up = P.op("scalar", lambda e, UPSi=UPSi, ub=ub: e.activation(
                    UPSi[:], ub, ACT.Copy))
                s_up_save[i] = s_up
                P.wait("gpsimd", "s", s_up)
                g_y = P.op("gpsimd", lambda e, i=i, GSi=GSi, UPSi=UPSi: e.tensor_mul(
                    YTG[:, i, :], GSi[:], UPSi[:]))
                g_y_save[i] = g_y
                g_y_prev = g_y
                P.wait("vector", "g", g_y)
                if i == 0:
                    P.op("vector", lambda e: e.tensor_copy(MXT[:], YTG[:, 0, :]))
                    P.op("vector", lambda e: e.tensor_copy(MNT[:], YTG[:, 0, :]))
                else:
                    P.op("vector", lambda e, i=i: e.tensor_max(
                        MXT[:], MXT[:], YTG[:, i, :]))
                    P.op("vector", lambda e, i=i: e.tensor_tensor(
                        MNT[:], MNT[:], YTG[:, i, :], ALU.min))
            s_up_hist = s_up_save
            g_y_hist = g_y_save
            s_up_save = {}
            g_y_save = {}
            t_gu_hist[tg] = P.c["t"]
            if tg + 1 < NTG:
                if tg >= 1:
                    P.wait("scalar", "t", t_gu_hist[tg - 1])
                h2bn = H2QT if (tg + 1) % 2 == 0 else H2QT2
                sa_h2t_tok[tg + 1] = P.dma(
                    "scalar", "sa", h2bn[:, :, :, :],
                    h2q_r[(tg + 1) // 2, :, 2 * ((tg + 1) % 2):2 * ((tg + 1) % 2) + 2, :, :])
            P.op("vector", lambda e: e.tensor_single_scalar(
                MNT[:], MNT[:], -1.0, ALU.mult))
            P.op("vector", lambda e: e.tensor_max(MXT[:], MXT[:], MNT[:]))
            g_ymax = P.c["v"]
            P.wait("tensor", "v", g_ymax)
            for t4 in range(2):
                bb = 4 + t4
                P.op("tensor", lambda e, t4=t4, bb=bb: e.transpose(
                    bankb(bb)[:, 0:128], MXT[:, t4 * 128:(t4 + 1) * 128], EYE[:]),
                    inc=True)
            t_mx = P.c["t"]
            P.wait("vector", "t", t_mx)
            for t4 in range(2):
                bb = 4 + t4
                P.op("vector", lambda e, t4=t4, bb=bb: e.reduce_max(
                    SYCOL[:, t4:t4 + 1], bankb(bb)[:, 0:128], AXX.X))
            for t4 in range(2):
                P.op("vector", lambda e, t4=t4: e.reciprocal(
                    SYCOL[:, t4 + 4:t4 + 5], SYCOL[:, t4:t4 + 1]))
            P.vgap()
            for t4 in range(2):
                P.op("vector", lambda e, t4=t4, tg=tg: e.tensor_single_scalar(
                    SYR[:, tg * 2 + t4:tg * 2 + t4 + 1], SYCOL[:, t4:t4 + 1],
                    1.0 / 127.0, ALU.mult))
                P.op("vector", lambda e, t4=t4: e.tensor_single_scalar(
                    SYCOL[:, t4:t4 + 1], SYCOL[:, t4 + 4:t4 + 5], 127.0, ALU.mult))
            P.vgap()
            v_syb = P.op("vector", lambda e: e.tensor_copy(SYKT[:, 0:2], SYCOL[:, 0:2]))
            P.wait("tensor", "v", v_syb)
            for t4 in range(2):
                P.op("tensor", lambda e, t4=t4: e.matmul(
                    bank(5)[0:1, t4 * 128:(t4 + 1) * 128], SYKT[:, t4:t4 + 1], EYE[:],
                    start=(t4 == 0), stop=(t4 == 1)), inc=(t4 == 1))
            t_syr = P.c["t"]
            P.wait("vector", "t", t_syr)
            v_row = P.op("vector", lambda e: e.tensor_copy(SYROW, bank(5)[0:1, 0:TGW]))
            P.wait("tensor", "v", v_row)
            P.op("tensor", lambda e: e.matmul(bank(5, TGW), ONE1[0:1, 0:128],
                                              SYROW, start=True, stop=True))
            t_syb = P.c["t"]
            P.wait("vector", "t", t_syb)
            v_sybs = P.op("vector", lambda e: e.tensor_copy(SYBS[:], bank(5, TGW)))
            P.wait("gpsimd", "v", v_sybs)
            v_rnd = {}
            for i in range(NI):
                if i >= 2:
                    P.wait("gpsimd", "v", v_rnd[i - 2])
                g_z = P.op("gpsimd", lambda e, i=i: e.tensor_mul(
                    ZQ[:, i % 2, :], YTG[:, i, :], SYBS[:]))
                P.wait("vector", "g", g_z)
                v_rnd[i] = P.op("vector", lambda e, i=i: e.tensor_scalar(
                    YTG[:, i, :], ZQ[:, i % 2, :], CPX[:, 0:1], CPX[:, 0:1],
                    ALU.add, ALU.subtract))
            v_yq = P.c["v"]
            P.wait("gpsimd", "v", v_yq)
            sg_yq[tg] = P.dma("gpsimd", "sg", yq_hbm[tg], YTG[:, :, :])

        v_yq_prev = v_yq
        g_zq_prev = P.c["g"]
        YTGflat2 = YTG.rearrange("p a b -> p (a b)")
        GS2 = [U[:, 9216:11264].rearrange("p (a c) -> p a c", c=512),
               U[:, 11272:13320].rearrange("p (a c) -> p a c", c=512)]
        YSTG = [U[:, 18432:19456].rearrange("p (a c) -> p a c", c=128),
                U[:, 19456:20480].rearrange("p (a c) -> p a c", c=128)]
        su2 = {}
        gy2 = {}
        yst_tok_prev = {}
        for tg in range(1, NTG):
            P.dwait("gpsimd", sg_yq.get(tg - 1, 0))
            P.qbarrier("gpsimd", "sx")
            h2b = H2QT if tg % 2 == 0 else H2QT2
            P.dwait_tok("tensor", sa_h2t_tok[tg])
            for t in range(2):
                for bb in range(4):
                    u = t * 4 + bb
                    gu = tg * 8 + u
                    bs = 4 * (u % 2)
                    bks = [bank(bs), bank(bs + 1), bank(bs + 2), bank(bs + 3)]
                    if gu >= 10:
                        P.wait("tensor", "s", su2.get(gu - 2, 0))
                    elif u == 0:
                        P.wait("tensor", "s", s_up_hist[31])
                    else:
                        P.wait("tensor", "v", v_yq_prev)
                    for k in range(J):
                        lhs = h2b[:, t, k, :]
                        for bi in range(2):
                            b = 2 * bb + bi
                            P.op("tensor", lambda e, lhs=lhs, b=b, bi=bi, k=k, bks=bks: e.matmul(
                                bks[bi], lhs, WGT[:, k, b * 512:(b + 1) * 512],
                                start=(k == 0), stop=(k == J - 1)), inc=False)
                            P.op("tensor", lambda e, lhs=lhs, b=b, bi=bi, k=k, bks=bks: e.matmul(
                                bks[2 + bi], lhs, WUT[:, k, b * 512:(b + 1) * 512],
                                start=(k == 0), stop=(k == J - 1)),
                                inc=(k == J - 1 and bi == 1))
                    t_gu = P.c["t"]
                    P.wait("scalar", "t", t_gu)
                    if gu >= 10:
                        P.wait("scalar", "g", gy2.get(gu - 2, 0))
                    elif tg == 1:
                        P.wait("scalar", "g", g_zq_prev)
                        P.wait("scalar", "v", v_yq_prev)
                    gsb = GS2[u % 2]
                    for bi in range(2):
                        P.op("scalar", lambda e, gsb=gsb, bi=bi, bks=bks: e.activation(
                            gsb[:, bi, :], bks[bi], ACT.Silu), inc=False)
                        P.op("scalar", lambda e, gsb=gsb, bi=bi, bks=bks: e.activation(
                            gsb[:, 2 + bi, :], bks[2 + bi], ACT.Copy),
                            inc=(bi == 1))
                    s_up2 = P.c["s"]
                    su2[gu] = s_up2
                    P.wait("gpsimd", "s", s_up2)
                    for bi in range(2):
                        b = 2 * bb + bi
                        P.op("gpsimd", lambda e, t=t, b=b, bi=bi, gsb=gsb: e.tensor_mul(
                            YTGflat2[:, t * 4096 + b * 512:t * 4096 + (b + 1) * 512],
                            gsb[:, bi, :], gsb[:, 2 + bi, :]), inc=(bi == 1))
                    g_y2 = P.c["g"]
                    gy2[gu] = g_y2
                    P.wait("vector", "g", g_y2)
                    for bi in range(2):
                        b = 2 * bb + bi
                        if u == 0 and bi == 0:
                            P.op("vector", lambda e, t=t, b=b: e.reduce_max(
                                SYCOL[:, 4 + t:5 + t],
                                YTGflat2[:, t * 4096 + b * 512:t * 4096 + (b + 1) * 512],
                                AXX.X, apply_absolute_value=True))
                        else:
                            P.op("vector", lambda e, t=t, b=b: e.reduce_max(
                                SYCOL[:, 6:7],
                                YTGflat2[:, t * 4096 + b * 512:t * 4096 + (b + 1) * 512],
                                AXX.X, apply_absolute_value=True))
                            P.vgap()
                            P.op("vector", lambda e, t=t: e.tensor_max(
                                SYCOL[:, 4 + t:5 + t], SYCOL[:, 4 + t:5 + t],
                                SYCOL[:, 6:7]))
                yt = YTGflat2[:, t * 4096:(t + 1) * 4096]
                P.op("vector", lambda e, t=t, tg=tg: e.tensor_single_scalar(
                    SYR[:, 2 * tg + t:2 * tg + t + 1], SYCOL[:, 4 + t:5 + t],
                    1.0 / 127.0, ALU.mult))
                P.op("vector", lambda e, t=t: e.reciprocal(
                    SYCOL[:, 7:8], SYCOL[:, 4 + t:5 + t]))
                P.vgap()
                P.op("vector", lambda e: e.tensor_single_scalar(
                    SYCOL[:, 7:8], SYCOL[:, 7:8], 127.0, ALU.mult))
                P.vgap()
                P.op("vector", lambda e, yt=yt: e.tensor_scalar(
                    yt, yt, SYCOL[:, 7:8], None, ALU.mult))
                P.vgap()
                v_rq = P.op("vector", lambda e, yt=yt: e.tensor_scalar(
                    yt, yt, CPX[:, 0:1], CPX[:, 0:1], ALU.add, ALU.subtract))
                P.wait("sync", "v", v_rq)
                yst_tok = {}
                for cc in range(4):
                    if cc >= 2:
                        P.dwait_tok("sync", yst_tok[cc - 2])
                    elif t == 1 or tg >= 2:
                        P.dwait_tok("sync", yst_tok_prev.get(cc + 2, 0))
                    sqy = P.dmat(YSTG[cc % 2][:, :, :],
                                 yt[:, cc * 1024:(cc + 1) * 1024])
                    P.dwait_tok("sync", sqy)
                    yst_tok[cc] = P.dma(
                        "sync", "sx",
                        yq_hbm[tg][:, 8 * cc:8 * cc + 8, t * 128:(t + 1) * 128],
                        YSTG[cc % 2][:, :, :])
                yst_tok_prev = yst_tok
                if t == 1:
                    v_yq_prev = v_rq
                    g_zq_prev = gy2[tg * 8 + 7]
            t_gu_hist[tg] = P.c["t"]
            if tg + 1 < NTG:
                P.wait("scalar", "t", t_gu_hist[tg - 1])
                h2bn = H2QT if (tg + 1) % 2 == 0 else H2QT2
                sa_h2t_tok[tg + 1] = P.dma(
                    "scalar", "sa", h2bn[:, :, :, :],
                    h2q_r[(tg + 1) // 2, :, 2 * ((tg + 1) % 2):2 * ((tg + 1) % 2) + 2, :, :])
            sg_yq[tg] = 0

        sa_out_hist = {}
        P.wait("gpsimd", "v", P.c["v"])
        P.wait("gpsimd", "s", P.c["s"])
        P.qbarrier("gpsimd", "sg")
        P.qbarrier("gpsimd", "sx")
        P.wait("scalar", "v", P.c["v"])
        P.wait("scalar", "g", P.c["g"])
        P.qbarrier("scalar", "sg")
        P.qbarrier("scalar", "sq")
        P.qbarrier("sync", "sa")
        P.qbarrier("sync", "sx")
        WDTS4 = U[:, 4096:8192].rearrange("p (b c) -> p b c", b=4)
        YTG2B = [YTG.rearrange("p a b -> p (a b)"), U[:, 8192:16384]]
        X2TB = Uf[:, 9216:11264].rearrange("p (b c) -> p b c", b=2)
        t_dn_prev = 0
        v_x3_prev = 0
        y_tok = {}
        for j in range(2):
            y_tok[j] = P.dma("gpsimd", ("sg", "sw")[j], YTG2B[j][:], yq_hbm[j])
        for pp in range(8):
            P.wait("tensor", "v", v_x3_prev)
            P.dwait_tok("tensor", y_tok[0])
            P.dwait_tok("tensor", y_tok[1])
            wds_tok = {}
            for w0 in range(4):
                eng0 = ("scalar", "sa") if w0 % 2 == 0 else ("gpsimd", "sg")
                P.wait(eng0[0], "t", t_dn_prev)
                wds_tok[w0] = P.dma(eng0[0], eng0[1], WDTS4[:, w0, :], wdt_hbm[w0])
            yv = [YTG2B[0].rearrange("p (i c) -> p i c", c=256),
                  YTG2B[1].rearrange("p (i c) -> p i c", c=256)]
            t_dn_i = {}
            for i in range(NI):
                P.dwait_tok("tensor", wds_tok[i])
                for j in range(2):
                    for t4 in range(2):
                        dpb = PS[:, 4 * j + 2 * t4:4 * j + 2 * t4 + 2, :].rearrange(
                            "p a b -> p (a b)")
                        for half in range(2):
                            P.op("tensor", lambda e, i=i, j=j, t4=t4, half=half, dpb=dpb, yv=yv: e.matmul(
                                dpb[:, half * 512:(half + 1) * 512],
                                yv[j][:, i, t4 * 128:(t4 + 1) * 128],
                                WDTS4[:, i % 4, half * 512:(half + 1) * 512],
                                start=(i == 0), stop=(i == NI - 1)),
                                inc=(half == 1 and t4 == 1 and j == 1))
                t_dn_i[i] = P.c["t"]
                if i + 4 < NI:
                    eng0 = ("scalar", "sa") if (i + 4) % 2 == 0 else ("gpsimd", "sg")
                    P.wait(eng0[0], "t", t_dn_i[i])
                    if i >= 1:
                        P.dwait_tok(eng0[0], wds_tok[i - 1])
                    wds_tok[i + 4] = P.dma(eng0[0], eng0[1],
                                           WDTS4[:, (i + 4) % 4, :], wdt_hbm[i + 4])
            t_dn = P.c["t"]
            t_dn_prev = t_dn
            if pp + 1 < 8:
                for j in range(2):
                    P.wait("gpsimd", "t", t_dn)
                    P.dwait_tok("gpsimd", y_tok[j])
                    y_tok[j] = P.dma("gpsimd", ("sg", "sw")[j], YTG2B[j][:],
                                     yq_hbm[2 * (pp + 1) + j])
            P.wait("vector", "t", t_dn)
            for j in range(2):
                tg = 2 * pp + j
                for t4 in range(2):
                    ft = tg * 2 + t4
                    P.wait("sync", "v", P.c["v"])
                    sa_x2r = P.dma("sync", "sq", X2TB[:, t4, :],
                                   x2_hbm[ft * 128:(ft + 1) * 128, :])
                    P.dwait("vector", sa_x2r)
                    if ft >= 2:
                        P.dwait("vector", sa_out_hist.get(ft - 2, 0))
                    dpb = PS[:, 4 * j + 2 * t4:4 * j + 2 * t4 + 2, :].rearrange(
                        "p a b -> p (a b)")
                    v_x3 = P.op("vector", lambda e, t4=t4, tg=tg, dpb=dpb: e.scalar_tensor_tensor(
                        X3T[:, t4 % 2, :], dpb, SYR[:, tg * 2 + t4:tg * 2 + t4 + 1],
                        X2TB[:, t4, :], ALU.mult, ALU.add))
                    P.wait("sync", "v", v_x3)
                    v_x3_prev = v_x3
                    sa_out_hist[ft] = P.dma("sync", "sq",
                                            out_h[ft * 128:(ft + 1) * 128, :],
                                            X3T[:, t4 % 2, :])

        P.qbarrier("sync", "sa")
        P.qbarrier("sync", "sg")
        P.qbarrier("sync", "sq")
        P.qbarrier("sync", "sw")
        P.qbarrier("sync", "sx")
        P.wait("sync", "v", P.c["v"])
        P.wait("sync", "t", P.c["t"])
        P.wait("sync", "s", P.c["s"])
        P.wait("sync", "g", P.c["g"])

        @block.sync
        def _(e):
            P.run("sync", e)

        @block.vector
        def _(e):
            P.run("vector", e)

        @block.scalar
        def _(e):
            P.run("scalar", e)

        @block.tensor
        def _(e):
            P.run("tensor", e)

        @block.gpsimd
        def _(e):
            P.run("gpsimd", e)

    return nc


_NC_CACHE = None


def kernel(L0_attn_x, L0_attn_Wq, L0_attn_Wk, L0_attn_Wv, L0_attn_Wo,
           L0_attn_bo, L0_attn_alpha, L0_attn_temperature, L0_attn_gamma,
           L0_attn_beta, L0_ffn_w_gate, L0_ffn_w_up, L0_ffn_w_down,
           L0_c, L0_wmod):
    global _NC_CACHE
    x = np.ascontiguousarray(np.asarray(L0_attn_x, dtype=np.float32))
    B, N, Dm = x.shape
    if _NC_CACHE is None:
        _NC_CACHE = build_nc()
    nc = _NC_CACHE

    eye = np.eye(128, dtype=np.float32).astype(BF)
    eyf = np.concatenate([np.eye(64, dtype=np.float32)] * 2, axis=0)
    hsel = np.zeros((128, 2), np.float32)
    hsel[0:64, 0] = 1.0
    hsel[64:128, 1] = 1.0
    hsel = hsel.astype(BF)
    g2sel = np.zeros((2, 128), np.float32)
    g2sel[0, 0:64] = 1.0
    g2sel[1, 64:128] = 1.0
    g2sel = g2sel.astype(BF)

    shared = {
        "wq": np.ascontiguousarray(L0_attn_Wq, np.float32),
        "wk": np.ascontiguousarray(L0_attn_Wk, np.float32),
        "wv": np.ascontiguousarray(L0_attn_Wv, np.float32),
        "wo": np.ascontiguousarray(L0_attn_Wo, np.float32),
        "bo": np.ascontiguousarray(np.asarray(L0_attn_bo, np.float32).reshape(1, Dm)),
        "alpha": np.ascontiguousarray(np.asarray(L0_attn_alpha, np.float32).reshape(1, 1)),
        "temp": np.ascontiguousarray(np.asarray(L0_attn_temperature, np.float32).reshape(1, 1)),
        "gamma": np.ascontiguousarray(np.asarray(L0_attn_gamma, np.float32).reshape(1, Dm)),
        "beta": np.ascontiguousarray(np.asarray(L0_attn_beta, np.float32).reshape(1, Dm)),
        "wg": np.ascontiguousarray(L0_ffn_w_gate, np.float32),
        "wu": np.ascontiguousarray(L0_ffn_w_up, np.float32),
        "wd": np.ascontiguousarray(L0_ffn_w_down, np.float32),
        "wmod": np.ascontiguousarray(L0_wmod, np.float32),
        "eye": eye,
        "eyf": eyf,
        "hsel": hsel,
        "g2sel": g2sel,
    }
    carr = np.ascontiguousarray(np.asarray(L0_c, np.float32))

    in_maps = []
    for core in range(8):
        b = core // 2
        half = core % 2
        xb = x[b]
        xr = np.concatenate([xb[half * TOWN:(half + 1) * TOWN],
                             xb[(1 - half) * TOWN:(2 - half) * TOWN]], axis=0)
        m = dict(shared)
        m["x"] = np.ascontiguousarray(xr)
        m["cvec"] = np.ascontiguousarray(carr[b].reshape(1, Dm))
        in_maps.append(m)

    res = run_bass_kernel_spmd(nc, in_maps, core_ids=list(range(8)))
    out = np.empty((B, N, Dm), dtype=np.float32)
    for core in range(8):
        b = core // 2
        half = core % 2
        out[b, half * TOWN:(half + 1) * TOWN] = res.results[core]["out"]
    return out
